# version 12
# baseline (speedup 1.0000x reference)
import contextlib
import numpy as np
import ml_dtypes
import concourse.bass as bass
import concourse.mybir as mybir
from concourse.ap import AP
from concourse.bass_utils import run_bass_kernel_spmd

F32 = mybir.dt.float32
BF16 = mybir.dt.bfloat16
AF = mybir.ActivationFunctionType
ALU = mybir.AluOpType
AX = mybir.AxisListType

NCORES = 8
B, N, D = 4, 4096, 1024
TOK = 2048
NT = TOK // 128
NKC = N // 128
DC = D // 128
MAGIC = float(np.float32(1.5 * 2 ** 23))
LN_EPS = 1e-5
G = 256
NG = TOK // G
HID1 = 2048
QK = 128
MOE_H = 2048
NE = 8
BIAS_SCALE = float(128.0 ** 0.5)
NPAIR = 8

ARENA_F32 = 41472

ENGS = ("tensor", "scalar", "vector", "gpsimd", "sync")


class Sem:
    __slots__ = ("h", "n")

    def __init__(self, h):
        self.h = h
        self.n = 0


class Rec:
    __slots__ = ("incs", "kb", "eng")

    def __init__(self, kb, eng):
        self.incs = []
        self.kb = kb
        self.eng = eng

    def inc(self, sem, k=1):
        self.incs.append((sem, k))
        sem.n += k
        self.kb.last_inc[self.eng] = True
        return self


class EngRec:
    def __init__(self, kb, name):
        self._kb = kb
        self._name = name

    def wait(self, sem, val=None):
        v = sem.n if val is None else val
        if v <= 0:
            return
        h = sem.h
        self._kb.streams[self._name].append(lambda E: E.wait_ge(h, v))

    def __getattr__(self, meth):
        kb, name = self._kb, self._name

        def call(*a, **kw):
            r = Rec(kb, name)

            def do(E):
                ins = getattr(E, meth)(*a, **kw)
                for sem, k in r.incs:
                    ins.then_inc(sem.h, k)
                return ins

            kb.streams[name].append(do)
            kb.last_inc[name] = False
            return r

        return call


class KB:
    def __init__(self, nc, es):
        self.nc = nc
        self.es = es
        self.streams = {e: [] for e in ENGS}
        self.last_inc = {e: True for e in ENGS}
        for e in ENGS:
            setattr(self, e, EngRec(self, e))
        self._sems = {}

    def sem(self, name):
        if name not in self._sems:
            self._sems[name] = Sem(self.es.enter_context(self.nc.semaphore(name)))
        return self._sems[name]

    def sbuf(self, stack, name, shape, dtype, side="right"):
        return stack.enter_context(
            self.nc.sbuf_tensor(name, shape, dtype, side=side))

    def psum(self, stack, name, shape, dtype=F32):
        return stack.enter_context(self.nc.psum_tensor(name, shape, dtype))

    def replay(self):
        with self.nc.Block() as block:
            for e in ENGS:
                fns = self.streams[e]
                getattr(block, e)(lambda E, fns=fns: [f(E) for f in fns])


def _t5_bucket_ext():
    ks = np.arange(-255, 256)
    n = np.maximum(ks, 0).astype(np.int64)
    val = (np.log(np.maximum(n, 1).astype(np.float32) / np.float32(16.0))
           / np.float32(np.log(8.0)) * np.float32(16.0)).astype(np.int32)
    val = np.minimum(16 + val, 31)
    return np.where(n < 16, n, val).astype(np.int64)


def _onehot_511():
    b = _t5_bucket_ext()
    oh = np.zeros((32, 511), np.float32)
    oh[b, np.arange(511)] = 1.0
    return oh


class Prog:
    def __init__(self, upto=None, no_cc=False):
        self.no_cc = no_cc
        self.upto = upto
        nc = bass.Bass("TRN2", target_bir_lowering=False, debug=False)
        self.nc = nc
        self.es = contextlib.ExitStack()
        self.kb = KB(nc, self.es)
        self._declare_io()
        self._consts()
        self.longs = contextlib.ExitStack()
        self.es.enter_context(self.longs)

    def _declare_io(self):
        nc = self.nc
        self.din = {}
        for nm, shp in [
            ("x_loc", [TOK, D]), ("x_rem", [TOK, D]),
            ("W0", [D, 3072]),
            ("w1", [4096, D]), ("b1", [4096]), ("w2", [D, 4096]), ("b2", [D]),
            ("w1shard", [512, D]), ("w2shard", [128, 4096]),
            ("ln_w", [D]), ("ln_b", [D]),
            ("Wh", [D, 2 * HID1]), ("bh", [2 * HID1]),
            ("Wqk", [D, QK]), ("bqk", [QK]),
            ("qk_w", [4, QK]), ("qk_b", [4, QK]), ("rel_table", [32]),
            ("Wout", [HID1, D]), ("bout", [D]),
            ("W1m", [NE, D, MOE_H]), ("b1m", [NE, MOE_H]),
            ("W3m", [NE, D, MOE_H]), ("b3m", [NE, MOE_H]),
            ("W2m", [NE, MOE_H, D]), ("b2m", [NE, D]),
            ("Wg", [D, NE]), ("bg", [NE]),
            ("Wf", [D, D]), ("bf", [D]),
            ("pmask", [1, 16]),
        ]:
            self.din[nm] = nc.dram_tensor(nm, shp, F32, kind="ExternalInput").ap()
        self.out = nc.dram_tensor("out", [TOK, D], F32, kind="ExternalOutput").ap()
        d = {}
        d["h_spill"] = nc.dram_tensor("h_spill", [4096, TOK], F32).ap()
        d["x1_spill"] = nc.dram_tensor("x1_spill", [TOK, D], F32).ap()
        d["attn0_spill"] = nc.dram_tensor("attn0_spill", [TOK, D], BF16).ap()
        d["attn1_spill"] = nc.dram_tensor("attn1_spill", [TOK, D], BF16).ap()
        d["v_spill"] = nc.dram_tensor("v_spill", [TOK, HID1], BF16).ap()
        d["red1_in"] = nc.dram_tensor("red1_in", [1, 16], F32).ap()
        d["red1_out"] = nc.dram_tensor("red1_out", [1, 16], F32).ap()
        d["red2_in"] = nc.dram_tensor("red2_in", [1, 16], F32).ap()
        d["red2_out"] = nc.dram_tensor("red2_out", [1, 16], F32).ap()
        d["pair_in"] = nc.dram_tensor("pair_in", [128, HID1], F32).ap()
        d["pair_out"] = nc.dram_tensor("pair_out", [256, HID1], F32).ap()
        d["wprime"] = nc.dram_tensor("wprime", [511], F32).ap()
        d["xn1fm_spill"] = nc.dram_tensor("xn1fm_spill", [128, 16384], BF16).ap()
        d["xn1_spill"] = nc.dram_tensor("xn1_spill", [TOK, D], BF16).ap()
        self.dscr = d

    def _consts(self):
        nc = self.nc
        self.ident_b = nc.inline_tensor(
            np.eye(128, dtype=ml_dtypes.bfloat16), "identb").ap()
        self.ident_f = nc.inline_tensor(np.eye(128, dtype=np.float32), "identf").ap()
        self.oh511 = nc.inline_tensor(_onehot_511(), "oh511").ap()
        m = np.zeros((256, 256), np.float32)
        jj, ii = np.meshgrid(np.arange(256), np.arange(256), indexing="ij")
        m[jj > ii] = -1.0e4
        m2 = np.zeros((128, 512), np.float32)
        for jc in range(2):
            m2[:, jc * 256:(jc + 1) * 256] = m[jc * 128:(jc + 1) * 128, :]
        self.maskT_add = nc.inline_tensor(m2, "masktadd").ap()

    def _arena(self):
        self.ar_f32 = self.kb.sbuf(self.longs, "arena", [128, ARENA_F32], F32,
                                   side="left")
        self.ar_b16 = self.ar_f32.bitcast(BF16)

    def tf(self, off, cols):
        assert off + cols <= ARENA_F32, (off, cols)
        return self.ar_f32[:, off:off + cols]

    def tb(self, off, cols):
        assert 2 * off + cols <= 2 * ARENA_F32, (off, cols)
        return self.ar_b16[:, 2 * off:2 * off + cols]

    def p0(self):
        kb = self.kb
        LS = self.longs
        self.P0 = contextlib.ExitStack()
        self.es.enter_context(self.P0)
        s_id = kb.sem("s_id")
        s_pe = kb.sem("s_pe0")
        s_ve = kb.sem("s_ve0")
        s_sc = kb.sem("s_sc0")

        self._arena()
        self.ones_col = kb.sbuf(LS, "ones_col", [1, 128], BF16, side="left")
        kb.vector.memset(self.ones_col[:], 1.0).inc(s_ve)
        self.ones_f = kb.sbuf(LS, "ones_f", [1, 128], F32, side="left")
        kb.vector.memset(self.ones_f[:], 1.0).inc(s_ve)
        self.IdB = kb.sbuf(LS, "IdB", [128, 128], BF16, side="left")
        self.IdF = kb.sbuf(LS, "IdF", [128, 128], F32, side="left")
        kb.sync.dma_start(out=self.IdB[:], in_=self.ident_b).inc(s_id, 16)
        kb.sync.dma_start(out=self.IdF[:], in_=self.ident_f).inc(s_id, 16)
        self.stats = kb.sbuf(LS, "stats", [128, 64], F32, side="left")
        kb.vector.memset(self.stats[:], 0.0).inc(s_ve)
        self.gmax1 = kb.sbuf(LS, "gmax1", [1, 16], F32, side="left")
        self.gmax2 = kb.sbuf(LS, "gmax2", [1, 16], F32, side="left")
        self.scl = kb.sbuf(LS, "scl", [128, 16], F32, side="left")
        self.sclr = kb.sbuf(LS, "sclr", [1, 16], F32, side="left")
        self.sclb = kb.sbuf(LS, "sclb", [1, 16], BF16, side="left")

        self.pS = [kb.psum(LS, f"pS{i}", [128, 1024], F32) for i in range(2)]
        self.pav = [kb.psum(LS, f"pav{i}", [128, 260], F32) for i in range(2)]
        self.ptb = kb.psum(LS, "ptb", [128, 1024], BF16)
        self.pu0 = kb.psum(LS, "pu0", [128, 512], F32)

        def bias_cols(name, src_flat_ap, L):
            c = L // 128
            s_b = kb.sem("s_bc_" + name)
            nat = kb.sbuf(self.P0, name + "_nat", [c, 128], F32)
            cols = kb.sbuf(LS, name + "_cols", [128, c], F32, side="left")
            kb.sync.dma_start(out=nat[:],
                              in_=src_flat_ap.rearrange("(c p) -> c p", p=128))\
                .inc(s_b, 16)
            kb.tensor.wait(s_b)
            kb.tensor.wait(s_id)
            kb.tensor.wait(s_ve)
            kb.tensor.transpose(self.pu0[:, 0:c], nat[:], self.IdF[0:c, 0:c])\
                .inc(s_pe)
            kb.vector.wait(s_pe)
            kb.vector.tensor_copy(out=cols[:], in_=self.pu0[:, 0:c]).inc(s_ve)
            return cols

        self.lnw_cols = bias_cols("lnw", self.din["ln_w"], D)
        self.lnb_cols = bias_cols("lnb", self.din["ln_b"], D)
        self.b1_cols = bias_cols("b1q", self.din["b1"], 4096)
        self.b2_cols = bias_cols("b2q", self.din["b2"], D)
        self.b1m_cols = bias_cols("b1m",
                                  self.din["b1m"].rearrange("e h -> (e h)"),
                                  NE * MOE_H)
        self.b3m_cols = bias_cols("b3m",
                                  self.din["b3m"].rearrange("e h -> (e h)"),
                                  NE * MOE_H)
        self.bh_cols = bias_cols("bhc", self.din["bh"], 2 * HID1)

        s_gp0 = kb.sem("s_gp0")
        self.s_gp0 = s_gp0

        def row_b(name, src_ap, L):
            tbf = kb.sbuf(LS, name + "_rb", [1, L], BF16, side="left")
            kb.gpsimd.dma_start(out=tbf[:],
                                in_=src_ap.rearrange("(o l) -> o l", o=1))\
                .inc(s_gp0, 16)
            return tbf

        self.bh_rb = row_b("bh", self.din["bh"], 2 * HID1)
        self.bqk_rb = row_b("bqk", self.din["bqk"], QK)
        self.bout_rb = row_b("bout", self.din["bout"], D)
        self.bg_rb = row_b("bg", self.din["bg"], NE)
        self.bf_rb = row_b("bf", self.din["bf"], D)
        self.b2_rb = row_b("b2r", self.din["b2"], D)
        self.qkw_rb = row_b("qkw", self.din["qk_w"].rearrange("a b -> (a b)"),
                            4 * QK)
        self.qkb_rb = row_b("qkb", self.din["qk_b"].rearrange("a b -> (a b)"),
                            4 * QK)
        self.b2m_b = kb.sbuf(LS, "b2m_b", [NE, D], BF16, side="left")
        kb.gpsimd.dma_start(out=self.b2m_b[:], in_=self.din["b2m"])\
            .inc(s_gp0, 16)
        self.pmask_sb = kb.sbuf(LS, "pmask_sb", [1, 16], F32, side="left")
        kb.sync.dma_start(out=self.pmask_sb[:], in_=self.din["pmask"])\
            .inc(kb.sem("s_pmld"), 16)

        self.qkwb_t = kb.sbuf(LS, "qkwb_t", [128, 8 * 128], F32, side="left")
        kb.tensor.wait(s_gp0)
        kb.tensor.wait(s_ve)
        for i in range(8):
            rowsrc = self.qkw_rb if i < 4 else self.qkb_rb
            rs = rowsrc[0:1, (i % 4) * 128:(i % 4) * 128 + 128]
            pb = self.pS[0] if i % 2 == 0 else self.pS[1]
            if i >= 2:
                kb.tensor.wait(s_ve)
            kb.tensor.matmul(pb[:, 0:128], self.ones_col[:], rs,
                             start=True, stop=True).inc(s_pe)
            kb.vector.wait(s_pe)
            kb.vector.tensor_copy(out=self.qkwb_t[:, i * 128:(i + 1) * 128],
                                  in_=pb[:, 0:128]).inc(s_ve)

        rel_sb = kb.sbuf(self.P0, "rel_sb", [32, 1], F32)
        oh_sb = kb.sbuf(self.P0, "oh_sb", [32, 511], F32)
        s_t5 = kb.sem("s_t5")
        kb.sync.dma_start(out=rel_sb[:],
                          in_=self.din["rel_table"].rearrange("(r o) -> r o", o=1))\
            .inc(s_t5, 16)
        kb.sync.dma_start(out=oh_sb[:], in_=self.oh511).inc(s_t5, 16)
        kb.tensor.wait(s_t5)
        kb.tensor.wait(s_ve)
        kb.tensor.matmul(self.pu0[0:1, 0:511], rel_sb[:], oh_sb[:],
                         start=True, stop=True).inc(s_pe)
        wrow = kb.sbuf(self.P0, "wrow", [1, 511], F32)
        kb.scalar.wait(s_pe)
        kb.scalar.activation(out=wrow[:], in_=self.pu0[0:1, 0:511],
                             func=AF.Identity, scale=BIAS_SCALE).inc(s_sc)
        kb.scalar.wait(s_sc)
        kb.scalar.dma_start(
            out=self.dscr["wprime"].rearrange("(o l) -> o l", o=1),
            in_=wrow[:]).inc(kb.sem("s_st"), 16)
        M1 = kb.sbuf(self.P0, "M1", [128, 2 * 256], F32)
        s_m1 = kb.sem("s_m1")
        kb.sync.wait(kb.sem("s_st"))
        for ci in range(2):
            kb.sync.dma_start(
                out=M1[:, ci * 256:(ci + 1) * 256],
                in_=AP(tensor=self.dscr["wprime"].tensor, offset=ci * 128,
                       ap=[[1, 128], [1, 256]])).inc(s_m1, 16)
        BiasIJ = kb.sbuf(self.P0, "BiasIJ", [128, 2 * 256], F32)
        kb.vector.wait(s_m1)
        for ci in range(2):
            rev = AP(tensor=M1, offset=ci * 256 + 255,
                     ap=[[2 * 256, 128], [-1, 256]])
            kb.vector.tensor_copy(out=BiasIJ[:, ci * 256:(ci + 1) * 256],
                                  in_=rev).inc(s_ve)
        self.BiasT = kb.sbuf(LS, "BiasT", [128, 2 * 256], F32, side="left")
        MaskT = kb.sbuf(self.P0, "MaskT", [128, 2 * 256], F32)
        s_mask = kb.sem("s_mask")
        kb.sync.dma_start(out=MaskT[:], in_=self.maskT_add).inc(s_mask, 16)
        for jc in range(2):
            kb.tensor.wait(s_ve)
            for icb in range(2):
                kb.tensor.transpose(
                    self.pu0[:, icb * 128:icb * 128 + 128],
                    BiasIJ[:, icb * 256 + jc * 128: icb * 256 + jc * 128 + 128],
                    self.IdF[:]).inc(s_pe)
            kb.vector.wait(s_pe)
            kb.vector.wait(s_mask)
            for icb in range(2):
                kb.vector.scalar_tensor_tensor(
                    out=self.BiasT[:, jc * 256 + icb * 128:
                                   jc * 256 + icb * 128 + 128],
                    in0=self.pu0[:, icb * 128:icb * 128 + 128], scalar=1.0,
                    in1=MaskT[:, jc * 256 + icb * 128: jc * 256 + icb * 128 + 128],
                    op0=ALU.mult, op1=ALU.add).inc(s_ve)
        self.p0_done = (s_pe, s_sc, s_ve)
        self.P0.close()

    def bcast_col(self, src11_f32, dst_col_f32, tag):
        kb = self.kb
        s_pe = kb.sem("s_bc_pe")
        s_ve = kb.sem("s_bc_ve")
        kb.vector.tensor_copy(out=self.sclb[0:1, 0:1], in_=src11_f32).inc(s_ve)
        kb.tensor.wait(s_ve)
        kb.tensor.matmul(self.pu0[:, 0:1], self.ones_col[:],
                         self.sclb[0:1, 0:1], start=True, stop=True).inc(s_pe)
        kb.vector.wait(s_pe)
        kb.vector.tensor_copy(out=dst_col_f32, in_=self.pu0[:, 0:1]).inc(s_ve)

    def p1(self):
        kb = self.kb
        XN = 0
        self.XN = XN
        xn_fm = self.tb(0, 2 * 32768)
        self.xn_fm = xn_fm

        P1 = contextlib.ExitStack()
        xld = [kb.sbuf(P1, f"xld{i}", [128, D], F32) for i in range(2)]
        xld += [self.tf(32768 + i * 1024, 1024) for i in range(4)]
        sqscr = kb.sbuf(P1, "sqscr", [128, D], F32)
        xnt = [kb.sbuf(P1, f"xnt{i}", [128, D], BF16) for i in range(2)]
        zb = kb.sbuf(self.longs, "zb", [128, 512], BF16, side="left")
        s_xld = [kb.sem("s_xld0"), kb.sem("s_xld1")]
        s_sc = kb.sem("s_p1sc")
        s_ve = kb.sem("s_p1ve")
        s_pe = kb.sem("s_p1pe")
        snap_ldA = {}
        snap_A = {}
        snap_B = {}
        snap_apply = {}
        snap_xnt_done = {}
        snap_evac = {}
        SLOT = (0, 8, 48, 56)

        p0pe, p0sc, p0ve = self.p0_done
        kb.vector.wait(p0sc)
        kb.scalar.wait(p0ve)
        kb.tensor.wait(p0ve)
        kb.sync.wait(p0pe)
        kb.sync.wait(p0ve)
        kb.sync.wait(p0sc)

        ve, sc, te = kb.vector, kb.scalar, kb.tensor
        ve.memset(zb[:], 0.0).inc(s_ve)
        self.zb = zb
        snap_zb = s_ve.n

        def stageA(t):
            buf = t % 6
            src = self.din["x_loc"] if t < NT else self.din["x_rem"]
            row0 = (t % NT) * 128
            cb = SLOT[t % 4]
            if t >= 2:
                kb.sync.wait(s_sc, snap_A[t - 2][0])
                kb.sync.wait(s_ve, snap_A[t - 2][1])
            if t >= 6:
                kb.sync.wait(s_sc, snap_apply[t - 6])
            kb.sync.dma_start(out=xld[buf][:], in_=src[row0:row0 + 128, :])\
                .inc(s_xld[buf % 2], 16)
            snap_ldA[t] = s_xld[buf % 2].n
            sc.wait(s_xld[buf % 2], snap_ldA[t])
            sc.wait(s_sc)
            if t >= 4:
                sc.wait(s_ve, snap_B[t - 4][1])
            sc.activation(out=sqscr[:], in_=xld[buf][:], func=AF.Square,
                          accum_out=self.stats[:, cb + 1:cb + 2]).inc(s_sc)
            ve.wait(s_xld[buf % 2], snap_ldA[t])
            ve.wait(s_ve)
            if t >= 4:
                ve.wait(s_sc, snap_B[t - 4][0])
            ve.reduce_sum(out=self.stats[:, cb + 0:cb + 1], in_=xld[buf][:],
                          axis=AX.X).inc(s_ve)
            snap_A[t] = (s_sc.n, s_ve.n)

        def stageB(t):
            cb = SLOT[t % 4]
            ve.wait(s_ve)
            ve.tensor_scalar(out=self.stats[:, cb + 2:cb + 3],
                             in0=self.stats[:, cb + 0:cb + 1],
                             scalar1=1.0 / D, scalar2=None, op0=ALU.mult)\
                .inc(s_ve)
            ve.wait(s_ve)
            ve.tensor_tensor(out=self.stats[:, cb + 3:cb + 4],
                             in0=self.stats[:, cb + 2:cb + 3],
                             in1=self.stats[:, cb + 2:cb + 3], op=ALU.mult)\
                .inc(s_ve)
            ve.wait(s_ve)
            ve.tensor_scalar(out=self.stats[:, cb + 4:cb + 5],
                             in0=self.stats[:, cb + 3:cb + 4],
                             scalar1=-1.0, scalar2=LN_EPS, op0=ALU.mult,
                             op1=ALU.add).inc(s_ve)
            sc.wait(s_sc)
            sc.wait(s_ve)
            sc.activation(out=self.stats[:, cb + 5:cb + 6],
                          in_=self.stats[:, cb + 1:cb + 2], func=AF.Ln,
                          scale=1.0 / D, bias=self.stats[:, cb + 4:cb + 5])\
                .inc(s_sc)
            sc.wait(s_sc)
            sc.activation(out=self.stats[:, cb + 6:cb + 7],
                          in_=self.stats[:, cb + 5:cb + 6], func=AF.Exp,
                          scale=-0.5).inc(s_sc)
            snap_ln = s_sc.n
            ve.wait(s_sc, snap_ln)
            ve.scalar_tensor_tensor(out=self.stats[:, cb + 7:cb + 8],
                                    in0=self.stats[:, cb + 2:cb + 3],
                                    scalar=-1.0,
                                    in1=self.stats[:, cb + 6:cb + 7],
                                    op0=ALU.mult, op1=ALU.mult).inc(s_ve)
            if t < NT:
                ve.wait(s_ve)
                ve.tensor_copy(out=self.stats[:, 16 + t:17 + t],
                               in_=self.stats[:, cb + 6:cb + 7]).inc(s_ve)
                ve.wait(s_ve)
                ve.tensor_copy(out=self.stats[:, 32 + t:33 + t],
                               in_=self.stats[:, cb + 7:cb + 8]).inc(s_ve)
            snap_B[t] = (s_sc.n, s_ve.n)

        def stageC(t):
            buf = t % 6
            cb = SLOT[t % 4]
            buf2 = t % 2
            if t >= 2:
                a, b = snap_xnt_done[t - 2]
                sc.wait(s_ve, a)
                sc.wait(s_pe, b)
            sc.wait(s_ve, snap_B[t][1])
            sc.wait(s_sc)
            sc.activation(out=xnt[buf2][:], in_=xld[buf][:],
                          func=AF.Identity,
                          scale=self.stats[:, cb + 6:cb + 7],
                          bias=self.stats[:, cb + 7:cb + 8]).inc(s_sc)
            snap_apply[t] = s_sc.n
            te.wait(s_sc, snap_apply[t])
            if t >= 1:
                te.wait(s_ve, snap_evac[t - 1])
            for d in range(DC):
                te.transpose(self.ptb[:, d * 128:(d + 1) * 128],
                             xnt[buf2][:, d * 128:(d + 1) * 128], self.IdB[:])\
                    .inc(s_pe)
            ve.wait(s_pe)
            if t == 0:
                ve.wait(s_ve, snap_zb)
            for d in range(DC):
                ve.tensor_tensor(
                    out=self.ar_b16[:, d * N + t * 128: d * N + t * 128 + 128],
                    in0=self.ptb[:, d * 128:(d + 1) * 128],
                    in1=zb[:, 0:128], op=ALU.add).inc(s_ve)
            snap_evac[t] = s_ve.n
            snap_xnt_done[t] = (s_ve.n, s_pe.n)

        for t in range(NKC + 4):
            if t < NKC:
                stageA(t)
            if 2 <= t and t - 2 < NKC:
                stageB(t - 2)
            if t >= 4:
                stageC(t - 4)

        P1.close()
        self.s_p1 = (s_sc, s_ve, s_pe)

    def xn(self, d, a, w):
        return self.ar_b16[:, d * N + a: d * N + a + w]

    def p2(self):
        kb = self.kb
        s_sc1, s_ve1, s_pe1 = self.s_p1
        ve, sc, te, gp, sy = kb.vector, kb.scalar, kb.tensor, kb.gpsimd, kb.sync

        P2w = contextlib.ExitStack()
        xld2 = [kb.sbuf(P2w, f"xld2_{i}", [128, D], F32) for i in range(2)]
        xn32 = kb.sbuf(P2w, "xn32", [128, D], F32)
        s_x2 = [kb.sem("s_x2_0"), kb.sem("s_x2_1")]
        s_15s = kb.sem("s_15s")
        s_15v = kb.sem("s_15v")
        snap_x2 = {}
        sy.wait(s_ve1)
        sy.wait(s_sc1)
        for t in range(NT):
            buf = t % 2
            if t >= 2:
                sy.wait(s_15s, snap_x2[t - 2])
            sy.dma_start(out=xld2[buf][:],
                         in_=self.din["x_loc"][t * 128:(t + 1) * 128, :])\
                .inc(s_x2[buf], 16)
            sc.wait(s_x2[buf])
            sc.wait(s_15v)
            sc.activation(out=xn32[:], in_=xld2[buf][:], func=AF.Identity,
                          scale=self.stats[:, 16 + t:17 + t],
                          bias=self.stats[:, 32 + t:33 + t]).inc(s_15s)
            snap_x2[t] = s_15s.n
            ve.wait(s_15s)
            ve.wait(s_15v)
            ve.reduce_max(out=self.stats[:, 49:50], in_=xn32[:], axis=AX.X,
                          apply_absolute_value=True).inc(s_15v)
            ve.wait(s_15v)
            if t == 0:
                ve.tensor_copy(out=self.stats[:, 48:49],
                               in_=self.stats[:, 49:50]).inc(s_15v)
            else:
                ve.tensor_tensor(out=self.stats[:, 48:49],
                                 in0=self.stats[:, 48:49],
                                 in1=self.stats[:, 49:50], op=ALU.max)\
                    .inc(s_15v)
        te.wait(s_15v)
        te.wait(s_ve1)
        te.transpose(self.pu0[0:1, 0:128], self.stats[:, 48:49], self.IdF[:])\
            .inc(kb.sem("s_15p"))
        ve.wait(kb.sem("s_15p"))
        ve.reduce_max(out=self.stats[0:1, 50:51], in_=self.pu0[0:1, 0:128],
                      axis=AX.X).inc(s_15v)

        wsh = kb.sbuf(P2w, "wsh", [128, 1024], F32)
        s_wld = kb.sem("s_wld")
        s_wve = kb.sem("s_wve")
        s_wpe = kb.sem("s_wpe")
        sy.wait(s_sc1)
        sy.wait(s_ve1)
        sy.wait(s_pe1)
        ve.wait(s_ve1)
        for j, (col, srcname) in enumerate(((51, "w1shard"), (52, "w2shard"))):
            for i in range(4):
                if j + i > 0:
                    sy.wait(s_wve)
                if srcname == "w1shard":
                    srcap = self.din[srcname][i * 128:(i + 1) * 128, :]
                else:
                    srcap = self.din[srcname][:, i * 1024:(i + 1) * 1024]
                sy.dma_start(out=wsh[:], in_=srcap).inc(s_wld, 16)
                ve.wait(s_wld)
                ve.wait(s_wve)
                ve.reduce_max(out=self.stats[:, 55:56], in_=wsh[:], axis=AX.X,
                              apply_absolute_value=True).inc(s_wve)
                ve.wait(s_wve)
                if i == 0:
                    ve.tensor_copy(out=self.stats[:, col:col + 1],
                                   in_=self.stats[:, 55:56]).inc(s_wve)
                else:
                    ve.tensor_tensor(out=self.stats[:, col:col + 1],
                                     in0=self.stats[:, col:col + 1],
                                     in1=self.stats[:, 55:56], op=ALU.max)\
                        .inc(s_wve)
        P2w.close()
        wred = kb.sbuf(self.longs, "wred", [2, 1], F32, side="left")
        te.wait(s_wve)
        te.wait(s_ve1)
        te.wait(kb.sem("s_15v"))
        te.transpose(self.pu0[0:2, 0:128], self.stats[:, 51:53], self.IdF[:])\
            .inc(s_wpe)
        ve.wait(s_wpe)
        ve.reduce_max(out=wred[:], in_=self.pu0[0:2, 0:128], axis=AX.X)\
            .inc(s_wve)
        te.wait(s_wve)
        te.transpose(self.pu0[0:1, 0:2], wred[:], self.IdF[0:2, 0:2]).inc(s_wpe)
        ve.wait(s_wpe)
        ve.wait(s_ve1)
        ve.wait(kb.sem("s_15v"))
        ve.memset(self.sclr[:], 0.0).inc(s_wve)
        ve.wait(s_wve)
        ve.tensor_copy(out=self.sclr[0:1, 0:1], in_=self.stats[0:1, 50:51])\
            .inc(s_wve)
        ve.wait(s_wve)
        ve.tensor_copy(out=self.sclr[0:1, 1:3], in_=self.pu0[0:1, 0:2])\
            .inc(s_wve)
        s_gp = kb.sem("s_gp")
        s_cc1 = kb.sem("s_cc1")
        gp.wait(s_wve)
        gp.dma_start(out=self.dscr["red1_in"], in_=self.sclr[:]).inc(s_gp, 16)
        gp.wait(s_gp)
        if self.no_cc:
            gp.dma_start(out=self.dscr["red1_out"], in_=self.dscr["red1_in"])\
                .inc(s_cc1, 16)
        else:
            gp.collective_compute(
                "AllReduce", ALU.max, replica_groups=[list(range(NCORES))],
                ins=[self.dscr["red1_in"].opt()],
                outs=[self.dscr["red1_out"].opt()],
            ).inc(s_cc1, 1)
        self.s_cc1 = s_cc1

        O_W, O_KF, O_QF, O_VT, O_EB, O_AG = (32768, 34304, 36352, 37376,
                                             39456, 40484)
        assert O_AG + 512 <= ARENA_F32
        Wq = [self.tb(O_W, 1024)] * 2
        Wk = [self.tb(O_W + 512, 1024)] * 2
        Wv = [self.tb(O_W + 1024, 1024)] * 2
        k_fm = self.tb(O_KF, 4096)
        q_fm = self.tb(O_QF, 2048)
        v_tm = self.tb(O_VT, 4160)
        Eb = [self.tb(O_EB + i * 512, 1024) for i in range(2)]
        att_g = [self.tb(O_AG + i * 256, 512) for i in range(2)]
        self.zt = kb.sbuf(self.longs, "zt", [128, 8], F32, side="left")
        self.zr = kb.sbuf(self.longs, "zr", [128, 8], F32, side="left")

        s_wp_ = kb.sem("s_wp")
        s_wp = [s_wp_, s_wp_]
        s_prj = kb.sem("s_prj")
        s_pse = kb.sem("s_pse")
        s_pve = kb.sem("s_pve")
        s_ast = [kb.sem("s_ast0"), kb.sem("s_ast1")]

        ve.wait(s_ve1)
        vt3 = v_tm.rearrange("p (k c) -> p k c", c=130)
        ve.memset(vt3[:, :, 64:65], 1.0).inc(s_pve)
        ve.memset(vt3[:, :, 129:130], 1.0).inc(s_pve)

        def load_pair(p):
            buf = p % 2
            if p >= 1:
                gp.wait(s_prj, self._wfree[p - 1])
            ha, hb = 2 * p, 2 * p + 1
            for dsttile, off in ((Wq[buf], 0), (Wk[buf], 64), (Wv[buf], 128)):
                for ri, h in enumerate((ha, hb)):
                    c0 = 192 * h + off
                    for d in range(DC):
                        gp.dma_start(
                            out=dsttile[:, d * 128 + ri * 64:
                                        d * 128 + ri * 64 + 64],
                            in_=self.din["W0"][d * 128:(d + 1) * 128,
                                               c0:c0 + 64]).inc(s_wp[buf], 16)

        self._wfree = {}
        snap_exp = {}
        snap_av = {}
        snap_st = {}
        snap_pavfree = 0
        gp.wait(s_sc1)
        gp.wait(s_ve1)
        load_pair(0)

        for p in range(NPAIR):
            buf = p % 2
            te.wait(s_wp[buf])
            te.wait(s_sc1)
            te.wait(s_ve1)
            if p >= 1:
                te.wait(s_pse)
                te.wait(s_pve)
            self._wfree[p] = None
            for blk in range(8):
                ps = self.pS[blk % 2]
                if blk >= 2:
                    te.wait(s_pse, snap_kq.get(blk - 2, 0))
                last = None
                for d in range(DC):
                    last = te.matmul(ps[:, 0:512],
                                     Wk[buf][:, d * 128:(d + 1) * 128],
                                     self.xn(d, blk * 512, 512),
                                     start=(d == 0), stop=(d == DC - 1))
                last.inc(s_prj)
                sc.wait(s_prj)
                sc.activation(out=k_fm[:, blk * 512:(blk + 1) * 512],
                              in_=ps[:, 0:512], func=AF.Identity).inc(s_pse)
                if blk == 0:
                    snap_kq = {}
                snap_kq[blk] = s_pse.n
            for blk in range(4):
                ps = self.pS[blk % 2]
                te.wait(s_pse, snap_kq[6 + (blk % 2)] if blk < 2 else
                        snap_q.get(blk - 2, 0))
                last = None
                for d in range(DC):
                    last = te.matmul(ps[:, 0:512],
                                     Wq[buf][:, d * 128:(d + 1) * 128],
                                     self.xn(d, blk * 512, 512),
                                     start=(d == 0), stop=(d == DC - 1))
                last.inc(s_prj)
                sc.wait(s_prj)
                sc.activation(out=q_fm[:, blk * 512:(blk + 1) * 512],
                              in_=ps[:, 0:512], func=AF.Identity).inc(s_pse)
                if blk == 0:
                    snap_q = {}
                snap_q[blk] = s_pse.n
            snap_vt = {}
            for kc in range(NKC):
                pv = self.pav[kc % 2]
                if kc >= 2:
                    te.wait(s_pve, snap_vt[kc - 2])
                last = None
                for d in range(DC):
                    last = te.matmul(
                        pv[:, 0:128],
                        self.xn(d, kc * 128, 128),
                        Wv[buf][:, d * 128:(d + 1) * 128],
                        start=(d == 0), stop=(d == DC - 1))
                last.inc(s_prj)
                if kc == 0:
                    self._wfree[p] = s_prj.n
                ve.wait(s_prj)
                ve.tensor_copy(out=vt3[:, kc, 0:64], in_=pv[:, 0:64]).inc(s_pve)
                ve.tensor_copy(out=vt3[:, kc, 65:129], in_=pv[:, 64:128])\
                    .inc(s_pve)
                snap_vt[kc] = s_pve.n
            self._wfree[p] = s_prj.n
            if p + 1 < NPAIR:
                load_pair(p + 1)
            te.wait(s_pse)
            te.wait(s_pve)
            flat = [(qb, kc) for qb in range(4) for kc in range(NKC)]

            def emit_S(j):
                qb_, kc_ = flat[j]
                ps_ = self.pS[j % 2]
                if j >= 2:
                    te.wait(s_pse, snap_exp[j - 2])
                te.matmul(ps_[:, 0:512],
                          k_fm[0:64, kc_ * 128:kc_ * 128 + 128],
                          q_fm[0:64, qb_ * 512:(qb_ + 1) * 512],
                          start=True, stop=True, tile_position=(0, 0))
                r_ = te.matmul(ps_[:, 512:1024],
                               k_fm[64:128, kc_ * 128:kc_ * 128 + 128],
                               q_fm[64:128, qb_ * 512:(qb_ + 1) * 512],
                               start=True, stop=True, tile_position=(64, 0))
                r_.inc(s_prj)
                snap_st[j] = s_prj.n

            emit_S(0)
            for idx, (qb, kc) in enumerate(flat):
                if idx + 1 < len(flat):
                    emit_S(idx + 1)
                ps = self.pS[idx % 2]
                sc.wait(s_prj, snap_st[idx])
                if idx >= 2:
                    sc.wait(s_prj, snap_av[idx - 2])
                sc.activation(out=Eb[idx % 2][:], in_=ps[:], func=AF.Exp,
                              scale=0.125).inc(s_pse)
                snap_exp[idx] = s_pse.n
                te.wait(s_pse, snap_exp[idx])
                if kc == 0:
                    te.wait(s_pve, snap_pavfree)
                r = None
                for h in range(2):
                    for tq in range(4):
                        r = te.matmul(
                            self.pav[h][:, tq * 65:(tq + 1) * 65],
                            Eb[idx % 2][:, h * 512 + tq * 128:
                                        h * 512 + tq * 128 + 128],
                            v_tm[:, kc * 130 + h * 65: kc * 130 + h * 65 + 65],
                            start=(kc == 0 and tq == 0),
                            stop=(kc == NKC - 1 and tq == 3))
                r.inc(s_prj)
                snap_av[idx] = s_prj.n
                if kc < NKC - 1:
                    continue
                abuf = att_g[qb % 2]
                ve.wait(s_prj, snap_av[qb * NKC + NKC - 1])
                if p * 4 + qb >= 2:
                    ve.wait(s_ast[qb % 2], self._astd.get((p * 4 + qb) - 2, 0))
                for h in range(2):
                    zap = AP(tensor=self.pav[h], offset=64,
                             ap=[[260, 128], [65, 4]])
                    ve.wait(s_pve)
                    ve.tensor_copy(out=self.zt[:, h * 4:(h + 1) * 4],
                                   in_=zap).inc(s_pve)
                ve.wait(s_pve)
                ve.reciprocal(out=self.zr[:], in_=self.zt[:]).inc(s_pve)
                ve.wait(s_pve)
                for h in range(2):
                    for tq in range(4):
                        ve.tensor_scalar(
                            out=abuf[:, tq * 128 + h * 64: tq * 128 + h * 64 + 64],
                            in0=self.pav[h][:, tq * 65: tq * 65 + 64],
                            scalar1=self.zr[:, h * 4 + tq: h * 4 + tq + 1],
                            scalar2=None, op0=ALU.mult).inc(s_pve)
                snap_pavfree = s_pve.n
                sy.wait(s_pve)
                for tq in range(4):
                    row0 = qb * 512 + tq * 128
                    sy.dma_start(
                        out=self.dscr["attn0_spill"][row0:row0 + 128,
                                                     2 * p * 64: 2 * p * 64 + 128],
                        in_=abuf[:, tq * 128: tq * 128 + 128])\
                        .inc(s_ast[qb % 2], 16)
                if not hasattr(self, "_astd"):
                    self._astd = {}
                self._astd[p * 4 + qb] = s_ast[qb % 2].n
        self.s_p2 = (s_prj, s_pse, s_pve, s_ast)
        self.s_ast_fin = (s_ast[0].n, s_ast[1].n)

    def p3(self):
        kb = self.kb
        ve, sc, te, gp, sy = (kb.vector, kb.scalar, kb.tensor, kb.gpsimd,
                              kb.sync)
        s_prj, s_pse, s_pve, s_ast = self.s_p2

        s_gm1 = kb.sem("s_gm1")
        gp.wait(self.s_cc1)
        gp.dma_start(out=self.gmax1[:], in_=self.dscr["red1_out"]).inc(s_gm1, 16)

        s3v = kb.sem("s_p3v")
        s3s = kb.sem("s_p3s")
        s3p = kb.sem("s_p3p")
        s3ld = kb.sem("s_p3ld")
        ve.wait(s_gm1)
        ve.wait(s_pve)
        te.wait(s_gm1)
        te.wait(s_prj)
        te.wait(kb.sem("s_wve"))
        te.wait(kb.sem("s_15v"))
        for i in range(3):
            r = te.matmul(self.pu0[:, i:i + 1], self.ones_f[:],
                          self.gmax1[0:1, i:i + 1], start=True, stop=True)
            r.inc(s3p)
            ve.wait(s3p)
            ve.tensor_copy(out=self.scl[:, i:i + 1], in_=self.pu0[:, i:i + 1])\
                .inc(s3v)
        for (srcc, dstc) in ((0, 3), (1, 4), (2, 5)):
            ve.wait(s3v)
            ve.reciprocal(out=self.scl[:, 8:9], in_=self.scl[:, srcc:srcc + 1])\
                .inc(s3v)
            ve.wait(s3v)
            ve.tensor_scalar(out=self.scl[:, dstc:dstc + 1],
                             in0=self.scl[:, 8:9], scalar1=127.0, scalar2=None,
                             op0=ALU.mult).inc(s3v)
        for (a, b, dstc) in ((0, 1, 6), (0, 2, 7)):
            ve.wait(s3v)
            ve.tensor_tensor(out=self.scl[:, 8:9], in0=self.scl[:, a:a + 1],
                             in1=self.scl[:, b:b + 1], op=ALU.mult).inc(s3v)
            ve.wait(s3v)
            ve.tensor_scalar(out=self.scl[:, dstc:dstc + 1],
                             in0=self.scl[:, 8:9], scalar1=1.0 / 16129.0,
                             scalar2=None, op0=ALU.mult).inc(s3v)
        ve.wait(s3v)
        ve.memset(self.scl[:, 15:16], -MAGIC).inc(s3v)
        self.b1s = kb.sbuf(self.longs, "b1s", [128, 32], F32, side="left")
        ve.wait(s3v)
        ve.tensor_scalar(out=self.b1s[:], in0=self.b1_cols[:],
                         scalar1=self.scl[:, 6:7], scalar2=None, op0=ALU.mult)\
            .inc(s3v)

        OXQ = 32768
        xq = self.tb(OXQ, 16384)
        self.OXQ = OXQ
        P3t = contextlib.ExitStack()
        x3 = [kb.sbuf(P3t, f"x3_{i}", [128, D], F32) for i in range(2)]
        xn3 = kb.sbuf(P3t, "xn3", [128, D], F32)
        xq3 = kb.sbuf(P3t, "xq3", [128, D], BF16)
        s_x3 = [kb.sem("s_x3_0"), kb.sem("s_x3_1")]
        ve.wait(s_pse)
        ve.wait(kb.sem("s_ast0"))
        ve.wait(kb.sem("s_ast1"))
        te.wait(kb.sem("s_ast0"))
        te.wait(kb.sem("s_ast1"))
        te.wait(s_pse)
        te.wait(s_pve)
        sc.wait(s_pse)
        snap_q3 = {}
        for t in range(NT):
            buf = t % 2
            if t >= 2:
                sy.wait(kb.sem("s_p3s"), snap_q3[t - 2][0])
            sy.dma_start(out=x3[buf][:],
                         in_=self.din["x_loc"][t * 128:(t + 1) * 128, :])\
                .inc(s_x3[buf], 16)
            sc.wait(s_x3[buf])
            sc.wait(s3v)
            sc.activation(out=xn3[:], in_=x3[buf][:], func=AF.Identity,
                          scale=self.stats[:, 16 + t:17 + t],
                          bias=self.stats[:, 32 + t:33 + t]).inc(s3s)
            snap_q3[t] = (s3s.n, s3v.n)
            ve.wait(s3s)
            ve.wait(s3v)
            ve.tensor_scalar(out=xn3[:], in0=xn3[:],
                             scalar1=self.scl[:, 3:4], scalar2=MAGIC,
                             op0=ALU.mult, op1=ALU.add).inc(s3v)
            snap_s1x = s3v.n
            sc.wait(s3v, snap_s1x)
            if t >= 1:
                sc.wait(s3p, snap_q3t)
            sc.wait(s3s)
            sc.activation(out=xq3[:], in_=xn3[:], func=AF.Identity,
                          bias=self.scl[:, 15:16]).inc(s3s)
            snap_s2x = s3s.n
            te.wait(s3s, snap_s2x)
            if t >= 1:
                te.wait(s3v, snap_q3e)
            for d in range(DC):
                te.transpose(self.ptb[:, d * 128:(d + 1) * 128],
                             xq3[:, d * 128:(d + 1) * 128], self.IdB[:])\
                    .inc(s3p)
            snap_q3t = s3p.n
            ve.wait(s3p)
            for d in range(DC):
                ve.tensor_copy(
                    out=xq[:, d * 2048 + t * 128: d * 2048 + t * 128 + 128],
                    in_=self.ptb[:, d * 128:(d + 1) * 128]).inc(s3v)
            snap_q3e = s3v.n
        P3t.close()
        if self.upto == "P3q":
            return

        w1b = [self.tf(i * 4096, 4096) for i in range(2)]
        tq = self.tf(8192, 1024)
        w1q = self.tb(9216, 4096)
        w1qT = self.tb(11264, 4096)
        hblk = self.tf(13312, 8192)
        s_w1ld = [kb.sem("s_w1ld0"), kb.sem("s_w1ld1")]
        s_hsp = kb.sem("s_hsp")
        snap_w1free = {}
        snap_hblk = {}

        sy.wait(s_pse)
        sy.wait(s_pve)
        sy.wait(s_prj)
        sy.wait(s3v)
        sy.wait(s3p)
        hmaxcol = 56

        s_w1p = [s_w1ld[0], s_w1ld[1], kb.sem("s_x2_0"), kb.sem("s_x2_1")]
        s_hsp_m = [s_hsp, kb.sem("s_wld"), kb.sem("s_x3_0"), kb.sem("s_x3_1")]
        self.s_hsp_m = s_hsp_m
        snap_w1p = {}
        snap_s1last = {}

        def load_w1(hb):
            buf = hb % 2
            if hb >= 1:
                sy.wait(s3v, snap_s1last[hb - 1])
            for i in range(4):
                sy.dma_start(
                    out=w1b[buf][:, i * 1024:(i + 1) * 1024],
                    in_=self.din["w1"][hb * 512 + i * 128:
                                       hb * 512 + i * 128 + 128, :])\
                    .inc(s_w1p[i], 16)
                snap_w1p[(hb, i)] = s_w1p[i].n

        tq2 = [self.tf(21504 + j * 1024, 1024) for j in range(2)]
        w1q2 = [w1q, self.tb(23552, 4096)]
        snap_step1 = {}
        snap_step2 = {}
        snap_tg = {}
        snap_evg_sc = {}
        snap_evg_ve = {}
        snap_mm_done = {}
        snap_ev = {}
        snap_absm = {}
        snap_spill = {}

        def quant_piece(hb, i):
            buf = hb % 2
            pc = hb * 4 + i
            ve.wait(s_w1p[i], snap_w1p[(hb, i)])
            if pc >= 2:
                ve.wait(s3s, snap_step2[pc - 2])
            ve.wait(s3v)
            ve.tensor_scalar(out=tq2[pc % 2][:],
                             in0=w1b[buf][:, i * 1024:(i + 1) * 1024],
                             scalar1=self.scl[:, 4:5], scalar2=MAGIC,
                             op0=ALU.mult, op1=ALU.add).inc(s3v)
            snap_step1[pc] = s3v.n
            if i == 3:
                snap_s1last[hb] = s3v.n
            sc.wait(s3v, snap_step1[pc])
            if hb >= 2 and i == 0:
                sc.wait(s3p, snap_tg[(hb - 2, 3)])
            sc.wait(s3s)
            sc.activation(out=w1q2[buf][:, i * 1024:(i + 1) * 1024],
                          in_=tq2[pc % 2][:], func=AF.Identity,
                          bias=self.scl[:, 15:16]).inc(s3s)
            snap_step2[pc] = s3s.n

        def tg(hb, i):
            buf = hb % 2
            te.wait(s3s, snap_step2[hb * 4 + i])
            prev = (hb, i - 1) if i > 0 else (hb - 1, 3)
            if prev in snap_evg_sc:
                te.wait(s3s, snap_evg_sc[prev])
            te.wait(s3p)
            for d in range(DC):
                te.transpose(self.ptb[:, d * 128:(d + 1) * 128],
                             w1q2[buf][:, i * 1024 + d * 128:
                                       i * 1024 + d * 128 + 128],
                             self.IdB[:]).inc(s3p)
            snap_tg[(hb, i)] = s3p.n
            sc.wait(s3p, snap_tg[(hb, i)])
            if hb >= 1 and i == 0:
                sc.wait(s3p, snap_mm_done[hb - 1])
            sc.wait(s3s)
            sc.activation(out=w1qT[:, i * 1024:(i + 1) * 1024],
                          in_=self.ptb[:], func=AF.Identity).inc(s3s)
            snap_evg_sc[(hb, i)] = s3s.n

        load_w1(0)
        for i in range(4):
            quant_piece(0, i)
        NHB = 0 if self.upto == "P3z" else (
            1 if self.upto in ("P3m", "P3t") else 8)
        for hb in range(NHB):
            if hb + 1 < NHB:
                load_w1(hb + 1)
            tg(hb, 0)
            tg(hb, 1)
            if self.upto == "P3t":
                tg(hb, 2)
                tg(hb, 3)
                return
            for m in range(4):
                te.wait(s3s, snap_evg_sc[(hb, m)])
                for tb2 in range(4):
                    idx = m * 4 + tb2
                    ps = self.pS[idx % 2]
                    if hb * 16 + idx >= 2:
                        te.wait(s3s, snap_ev.get(hb * 16 + idx - 2, 0))
                    lastm = None
                    for d in range(DC):
                        lastm = te.matmul(
                            ps[:, 0:512],
                            w1qT[:, m * 1024 + d * 128: m * 1024 + d * 128 + 128],
                            xq[:, d * 2048 + tb2 * 512:
                               d * 2048 + tb2 * 512 + 512],
                            start=(d == 0), stop=(d == DC - 1))
                    lastm.inc(s3p)
                    sc.wait(s3p)
                    sc.wait(s3s)
                    if hb >= 1 and tb2 == 0:
                        sc.wait(s_hsp_m[m], snap_spill[(hb - 1, m)])
                        sc.wait(s3v, snap_absm[(hb - 1, m)])
                    bcol = hb * 4 + m
                    sc.activation(out=hblk[:, m * 2048 + tb2 * 512:
                                           m * 2048 + tb2 * 512 + 512],
                                  in_=ps[:, 0:512], func=AF.Gelu,
                                  scale=self.scl[:, 6:7],
                                  bias=self.b1s[:, bcol:bcol + 1]).inc(s3s)
                    snap_ev[hb * 16 + idx] = s3s.n
                    if tb2 == 0 and m + 2 < 4:
                        tg(hb, m + 2)
                if hb + 1 < NHB:
                    quant_piece(hb + 1, m)
                ve.wait(s3s, snap_ev[hb * 16 + m * 4 + 3])
                ve.wait(s3v)
                ve.reduce_max(out=self.stats[:, 57:58],
                              in_=hblk[:, m * 2048:(m + 1) * 2048], axis=AX.X,
                              apply_absolute_value=True).inc(s3v)
                ve.wait(s3v)
                if hb == 0 and m == 0:
                    ve.tensor_copy(out=self.stats[:, hmaxcol:hmaxcol + 1],
                                   in_=self.stats[:, 57:58]).inc(s3v)
                else:
                    ve.tensor_tensor(out=self.stats[:, hmaxcol:hmaxcol + 1],
                                     in0=self.stats[:, hmaxcol:hmaxcol + 1],
                                     in1=self.stats[:, 57:58], op=ALU.max)\
                        .inc(s3v)
                snap_absm[(hb, m)] = s3v.n
                sy.wait(s3s, snap_ev[hb * 16 + m * 4 + 3])
                sy.dma_start(
                    out=self.dscr["h_spill"][hb * 512 + m * 128:
                                             hb * 512 + m * 128 + 128, :],
                    in_=hblk[:, m * 2048:(m + 1) * 2048]).inc(s_hsp_m[m], 16)
                snap_spill[(hb, m)] = s_hsp_m[m].n
            snap_mm_done[hb] = s3p.n

        if self.upto in ("P3m", "P3z"):
            return
        te.wait(s3v)
        te.wait(s3p)
        te.transpose(self.pu0[0:1, 0:128],
                     self.stats[:, hmaxcol:hmaxcol + 1], self.IdF[:]).inc(s3p)
        ve.wait(s3p)
        ve.reduce_max(out=self.stats[0:1, 58:59], in_=self.pu0[0:1, 0:128],
                      axis=AX.X).inc(s3v)
        ve.wait(s3v)
        ve.memset(self.sclr[:], 0.0).inc(s3v)
        ve.wait(s3v)
        ve.tensor_copy(out=self.sclr[0:1, 0:1], in_=self.stats[0:1, 58:59])\
            .inc(s3v)
        s_cc2 = kb.sem("s_cc2")
        s_gp2 = kb.sem("s_gp2")
        gp.wait(s3v)
        gp.dma_start(out=self.dscr["red2_in"], in_=self.sclr[:]).inc(s_gp2, 16)
        gp.wait(s_gp2)
        if self.no_cc:
            gp.dma_start(out=self.dscr["red2_out"], in_=self.dscr["red2_in"])\
                .inc(s_cc2, 16)
        else:
            gp.collective_compute(
                "AllReduce", ALU.max, replica_groups=[list(range(NCORES))],
                ins=[self.dscr["red2_in"].opt()],
                outs=[self.dscr["red2_out"].opt()],
            ).inc(s_cc2, 1)
        gp.wait(s_cc2)
        gp.dma_start(out=self.gmax2[:], in_=self.dscr["red2_out"]).inc(s_gp2, 16)
        self.s_p3 = (s3p, s3s, s3v, s_hsp, s_gp2)

    def p4(self):
        kb = self.kb
        ve, sc, te, gp, sy = (kb.vector, kb.scalar, kb.tensor, kb.gpsimd,
                              kb.sync)
        s3p, s3s, s3v, s_hsp, s_gp2 = self.s_p3
        s4v = kb.sem("s_p4v")
        s4s = kb.sem("s_p4s")
        s4p = kb.sem("s_p4p")

        ve.wait(s_gp2)
        ve.wait(s3v)
        te.wait(s_gp2)
        te.wait(s3p)
        te.wait(s3v)
        r = te.matmul(self.pu0[:, 3:4], self.ones_f[:],
                      self.gmax2[0:1, 0:1], start=True, stop=True).inc(s4p)
        ve.wait(s4p)
        ve.tensor_copy(out=self.scl[:, 9:10], in_=self.pu0[:, 3:4]).inc(s4v)
        ve.wait(s4v)
        ve.reciprocal(out=self.scl[:, 8:9], in_=self.scl[:, 9:10]).inc(s4v)
        ve.wait(s4v)
        ve.tensor_scalar(out=self.scl[:, 10:11], in0=self.scl[:, 8:9],
                         scalar1=127.0, scalar2=None, op0=ALU.mult).inc(s4v)
        ve.wait(s4v)
        ve.tensor_tensor(out=self.scl[:, 8:9], in0=self.scl[:, 9:10],
                         in1=self.scl[:, 2:3], op=ALU.mult).inc(s4v)
        ve.wait(s4v)
        ve.tensor_scalar(out=self.scl[:, 11:12], in0=self.scl[:, 8:9],
                         scalar1=1.0 / 16129.0, scalar2=None, op0=ALU.mult)\
            .inc(s4v)
        B2S = self.tf(35840, 1024)
        te.wait(s4v)
        for nb in range(2):
            r = te.matmul(self.pS[0][:, nb * 512:(nb + 1) * 512],
                          self.ones_col[:],
                          self.b2_rb[0:1, nb * 512:(nb + 1) * 512],
                          start=True, stop=True).inc(s4p)
        ve.wait(s4p)
        for nb in range(2):
            ve.tensor_scalar(out=B2S[:, nb * 512:(nb + 1) * 512],
                             in0=self.pS[0][:, nb * 512:(nb + 1) * 512],
                             scalar1=self.scl[:, 11:12], scalar2=None,
                             op0=ALU.mult).inc(s4v)

        x1 = self.tf(0, 16384)
        hq = self.tb(16384, 16384)
        w2qT = self.tb(24576, 8192)
        P4t = contextlib.ExitStack()
        w2cb = [self.tf(28672 + i * 1024, 1024) for i in range(2)]
        hcb = [self.tf(30720 + i * 2048, 2048) for i in range(2)]
        w2qn = [self.tb(34816 + i * 512, 1024) for i in range(2)]
        s_w2ld = [kb.sem("s_w2ld0"), kb.sem("s_w2ld1")]
        s_hld = [kb.sem("s_hld0"), kb.sem("s_hld1")]

        def load_hc(hcx):
            buf = hcx % 2
            if hcx >= 2:
                gp.wait(s4s, self._w2free[hcx - 2])
                sy.wait(s4s, self._hfree[hcx - 2])
            for i in range(DC):
                gp.dma_start(
                    out=w2cb[buf][:, i * 128:(i + 1) * 128],
                    in_=self.din["w2"][i * 128:(i + 1) * 128,
                                       hcx * 128: hcx * 128 + 128])\
                    .inc(s_w2ld[buf], 16)
            sy.dma_start(out=hcb[buf][:],
                         in_=self.dscr["h_spill"][hcx * 128:(hcx + 1) * 128, :])\
                .inc(s_hld[buf], 16)

        self._w2free = {}
        self._hfree = {}
        snap_x1ev = {}
        snap_mmg = {}
        ve.wait(s3s)
        ve.wait(s3p)
        sc.wait(s3p)
        sc.wait(s3v)
        te.wait(s3s)
        gp.wait(s3v)
        gp.wait(s3s)
        sy.wait(s3v)
        for s_ in self.s_hsp_m:
            ve.wait(s_)
            sc.wait(s_)
            sy.wait(s_)
        load_hc(0)
        snap_s1h = {}
        snap_s2h = {}
        snap_s1w = {}
        snap_s2w = {}
        snap_tgw = {}
        snap_evw_sc = {}
        snap_evw_ve = {}
        for hg in range(4):
            for c8 in range(8):
                hcx = hg * 8 + c8
                buf = hcx % 2
                if hcx + 1 < 32:
                    load_hc(hcx + 1)
                ve.wait(s_hld[buf])
                ve.wait(s4v)
                ve.tensor_scalar(out=hcb[buf][:], in0=hcb[buf][:],
                                 scalar1=self.scl[:, 10:11], scalar2=MAGIC,
                                 op0=ALU.mult, op1=ALU.add).inc(s4v)
                snap_s1h[hcx] = s4v.n
                sc.wait(s4v, snap_s1h[hcx])
                if hg >= 1 and c8 == 0:
                    sc.wait(s4p, snap_mmg[hg - 1])
                sc.wait(s4s)
                sc.activation(out=hq[:, c8 * 2048:(c8 + 1) * 2048],
                              in_=hcb[buf][:], func=AF.Identity,
                              bias=self.scl[:, 15:16]).inc(s4s)
                snap_s2h[hcx] = s4s.n
                self._hfree[hcx] = s4s.n
                ve.wait(s_w2ld[buf])
                ve.wait(s4v)
                ve.tensor_scalar(out=w2cb[buf][:], in0=w2cb[buf][:],
                                 scalar1=self.scl[:, 5:6], scalar2=MAGIC,
                                 op0=ALU.mult, op1=ALU.add).inc(s4v)
                snap_s1w[hcx] = s4v.n
                sc.wait(s4v, snap_s1w[hcx])
                if hcx >= 2:
                    sc.wait(s4p, snap_tgw[hcx - 2])
                sc.wait(s4s)
                sc.activation(out=w2qn[buf][:], in_=w2cb[buf][:],
                              func=AF.Identity, bias=self.scl[:, 15:16]).inc(s4s)
                snap_s2w[hcx] = s4s.n
                self._w2free[hcx] = s4s.n
                te.wait(s4s, snap_s2w[hcx])
                if hcx >= 1:
                    te.wait(s4s, snap_evw_sc[hcx - 1])
                te.wait(s4p)
                for i in range(DC):
                    te.transpose(self.ptb[:, i * 128:(i + 1) * 128],
                                 w2qn[buf][:, i * 128:(i + 1) * 128],
                                 self.IdB[:]).inc(s4p)
                snap_tgw[hcx] = s4p.n
                sc.wait(s4p, snap_tgw[hcx])
                sc.wait(s4s)
                sc.activation(out=w2qT[:, c8 * 1024:(c8 + 1) * 1024],
                              in_=self.ptb[:], func=AF.Identity).inc(s4s)
                snap_evw_sc[hcx] = s4s.n
            te.wait(s4s, snap_evw_sc[hg * 8 + 7])
            for tt in range(16):
                for nb in range(2):
                    idx = tt * 2 + nb
                    ps = self.pS[idx % 2]
                    gidx = hg * 32 + idx
                    if gidx >= 2:
                        te.wait(s4v, snap_x1ev.get(gidx - 2, 0))
                    lastm = None
                    for c8 in range(8):
                        lastm = te.matmul(
                            ps[:, 0:512],
                            hq[:, c8 * 2048 + tt * 128: c8 * 2048 + tt * 128 + 128],
                            w2qT[:, c8 * 1024 + nb * 512: c8 * 1024 + nb * 512 + 512],
                            start=(c8 == 0), stop=(c8 == 7))
                    lastm.inc(s4p)
                    ve.wait(s4p)
                    dst = x1[:, tt * 1024 + nb * 512: tt * 1024 + nb * 512 + 512]
                    if hg == 0:
                        ve.tensor_copy(out=dst, in_=ps[:, 0:512]).inc(s4v)
                    else:
                        ve.scalar_tensor_tensor(out=dst, in0=ps[:, 0:512],
                                                scalar=1.0, in1=dst,
                                                op0=ALU.mult, op1=ALU.add)\
                            .inc(s4v)
                    snap_x1ev[gidx] = s4v.n
            snap_mmg[hg] = s4p.n
        P4t.close()

        P4f = contextlib.ExitStack()
        xf = [kb.sbuf(P4f, f"xf{i}", [128, 1024], F32) for i in range(2)]
        af = [kb.sbuf(P4f, f"af{i}", [128, 1024], BF16) for i in range(2)]
        s_xf = [kb.sem("s_xf0"), kb.sem("s_xf1")]
        s_af = [kb.sem("s_af0"), kb.sem("s_af1")]
        snap_fin = {}
        for tt in range(NT):
            buf = tt % 2
            if tt >= 2:
                sy.wait(s4v, snap_fin[tt - 2])
            sy.dma_start(out=xf[buf][:],
                         in_=self.din["x_loc"][tt * 128:(tt + 1) * 128, :])\
                .inc(s_xf[buf], 16)
            sy.dma_start(out=af[buf][:],
                         in_=self.dscr["attn0_spill"][tt * 128:(tt + 1) * 128, :])\
                .inc(s_af[buf], 16)
            dst = x1[:, tt * 1024:(tt + 1) * 1024]
            ve.wait(s4v)
            ve.tensor_scalar(out=dst, in0=dst, scalar1=self.scl[:, 11:12],
                             scalar2=None, op0=ALU.mult).inc(s4v)
            ve.wait(s4v)
            ve.tensor_tensor(out=dst, in0=dst, in1=B2S[:], op=ALU.add).inc(s4v)
            ve.wait(s_af[buf])
            ve.wait(s4v)
            ve.tensor_tensor(out=dst, in0=dst, in1=af[buf][:], op=ALU.add)\
                .inc(s4v)
            ve.wait(s_xf[buf])
            ve.wait(s4v)
            ve.tensor_tensor(out=dst, in0=dst, in1=xf[buf][:], op=ALU.add)\
                .inc(s4v)
            snap_fin[tt] = s4v.n
        P4f.close()
        self.x1 = x1
        self.s_p4 = (s4p, s4s, s4v)
        self.s_p4_dma = (s_xf[0], s_xf[1], s_af[0], s_af[1])

    def p5(self):
        kb = self.kb
        ve, sc, te, gp, sy = (kb.vector, kb.scalar, kb.tensor, kb.gpsimd,
                              kb.sync)
        s4p, s4s, s4v = self.s_p4
        s5v = kb.sem("s_p5v")
        s5s = kb.sem("s_p5s")
        s5p = kb.sem("s_p5p")
        s_x1sp = kb.sem("s_x1sp")
        s_xn1sp = kb.sem("s_xn1sp")

        x1 = self.x1
        xn1_tm = self.tb(16384, 16384)
        xn1_fm = self.tb(24576, 16384)
        normed = self.tb(32768, 16384)
        self.xn1_fm_off = 24576
        self.normed_off = 32768

        P5t = contextlib.ExitStack()
        nt5 = kb.sbuf(P5t, "nt5", [128, D], BF16)
        sq5 = kb.sbuf(P5t, "sq5", [128, D], F32)

        sc.wait(s4v)
        sc.wait(s4p)
        ve.wait(s4v)
        ve.wait(s4p)
        te.wait(s4v)
        sy.wait(s4v)

        def ln_stats(src_ap_f32orbf, cb, is_f32):
            sc.wait(s5v)
            sc.wait(s5s)
            sc.activation(out=sq5[:], in_=src_ap_f32orbf, func=AF.Square,
                          accum_out=self.stats[:, cb + 1:cb + 2]).inc(s5s)
            ve.wait(s5v)
            ve.wait(s5s)
            ve.reduce_sum(out=self.stats[:, cb + 0:cb + 1], in_=src_ap_f32orbf,
                          axis=AX.X).inc(s5v)
            ve.wait(s5v)
            ve.tensor_scalar(out=self.stats[:, cb + 2:cb + 3],
                             in0=self.stats[:, cb + 0:cb + 1],
                             scalar1=1.0 / D, scalar2=None, op0=ALU.mult)\
                .inc(s5v)
            ve.wait(s5v)
            ve.tensor_tensor(out=self.stats[:, cb + 3:cb + 4],
                             in0=self.stats[:, cb + 2:cb + 3],
                             in1=self.stats[:, cb + 2:cb + 3], op=ALU.mult)\
                .inc(s5v)
            ve.wait(s5v)
            ve.tensor_scalar(out=self.stats[:, cb + 4:cb + 5],
                             in0=self.stats[:, cb + 3:cb + 4],
                             scalar1=-1.0, scalar2=LN_EPS, op0=ALU.mult,
                             op1=ALU.add).inc(s5v)
            sc.wait(s5s)
            sc.wait(s5v)
            sc.activation(out=self.stats[:, cb + 5:cb + 6],
                          in_=self.stats[:, cb + 1:cb + 2], func=AF.Ln,
                          scale=1.0 / D, bias=self.stats[:, cb + 4:cb + 5])\
                .inc(s5s)
            sc.wait(s5s)
            sc.activation(out=self.stats[:, cb + 6:cb + 7],
                          in_=self.stats[:, cb + 5:cb + 6], func=AF.Exp,
                          scale=-0.5).inc(s5s)
            ve.wait(s5s)
            ve.scalar_tensor_tensor(out=self.stats[:, cb + 7:cb + 8],
                                    in0=self.stats[:, cb + 2:cb + 3],
                                    scalar=-1.0,
                                    in1=self.stats[:, cb + 6:cb + 7],
                                    op0=ALU.mult, op1=ALU.mult).inc(s5v)

        snap_tr5 = 0
        for t in range(NT):
            cb = (t % 2) * 8
            cb2 = 16 + (t % 2) * 8
            x1t = x1[:, t * 1024:(t + 1) * 1024]
            xn1t = xn1_tm[:, t * 1024:(t + 1) * 1024]
            ln_stats(x1t, cb, True)
            sc.wait(s5v)
            sc.activation(out=xn1t, in_=x1t, func=AF.Identity,
                          scale=self.stats[:, cb + 6:cb + 7],
                          bias=self.stats[:, cb + 7:cb + 8]).inc(s5s)
            sy.wait(s5s)
            sy.dma_start(out=self.dscr["x1_spill"][t * 128:(t + 1) * 128, :],
                         in_=x1t).inc(s_x1sp, 16)
            sy.dma_start(out=self.dscr["xn1_spill"][t * 128:(t + 1) * 128, :],
                         in_=xn1t).inc(s_x1sp, 16)
            ln_stats(xn1t, cb2, False)
            sc.wait(s5v)
            sc.wait(s5s)
            if t >= 1:
                sc.wait(s5p, snap_nt5)
            sc.activation(out=nt5[:], in_=xn1t, func=AF.Identity,
                          scale=self.stats[:, cb2 + 6:cb2 + 7],
                          bias=self.stats[:, cb2 + 7:cb2 + 8]).inc(s5s)
            te.wait(s5s)
            if snap_tr5:
                te.wait(s5v, snap_tr5)
            for d in range(DC):
                te.transpose(self.ptb[:, d * 128:(d + 1) * 128],
                             xn1t[:, d * 128:(d + 1) * 128], self.IdB[:])\
                    .inc(s5p)
            ve.wait(s5p)
            for d in range(DC):
                ve.tensor_copy(
                    out=xn1_fm[:, d * TOK + t * 128: d * TOK + t * 128 + 128],
                    in_=self.ptb[:, d * 128:(d + 1) * 128]).inc(s5v)
            te.wait(s5v)
            for d in range(DC):
                te.transpose(self.ptb[:, d * 128:(d + 1) * 128],
                             nt5[:, d * 128:(d + 1) * 128], self.IdB[:])\
                    .inc(s5p)
            snap_nt5 = s5p.n
            ve.wait(s5p)
            for d in range(DC):
                ve.tensor_scalar(
                    out=normed[:, d * TOK + t * 128: d * TOK + t * 128 + 128],
                    in0=self.ptb[:, d * 128:(d + 1) * 128],
                    scalar1=self.lnw_cols[:, d:d + 1],
                    scalar2=self.lnb_cols[:, d:d + 1],
                    op0=ALU.mult, op1=ALU.add).inc(s5v)
            snap_tr5 = s5v.n

        sy.wait(s5v)
        sy.dma_start(out=self.dscr["xn1fm_spill"], in_=xn1_fm).inc(s_xn1sp, 16)

        wg_sb = kb.sbuf(self.longs, "wg_sb", [128, 64], F32, side="left")
        s_wg = kb.sem("s_wg")
        for d in range(DC):
            sy.dma_start(out=wg_sb[:, d * 8:(d + 1) * 8],
                         in_=self.din["Wg"][d * 128:(d + 1) * 128, :])\
                .inc(s_wg, 16)
        wgb_sb = kb.sbuf(self.longs, "wgb_sb", [128, 64], BF16, side="left")
        ve.wait(s_wg)
        ve.wait(s5v)
        ve.tensor_copy(out=wgb_sb[:], in_=wg_sb[:]).inc(s5v)
        glog = kb.sbuf(self.longs, "glog", [128, 128], F32, side="left")
        gext = kb.sbuf(self.longs, "gext", [128, 128], F32, side="left")
        zg = kb.sbuf(self.longs, "zg", [128, 48], F32, side="left")
        self.we_mat = kb.sbuf(self.longs, "we_mat", [128, 128], F32,
                              side="left")
        self.we_fm = kb.sbuf(self.longs, "we_fm", [8, 2048], BF16, side="left")
        te.wait(s5v)
        for t in range(NT):
            pvx = self.pav[t % 2]
            if t >= 2:
                te.wait(s5v, snap_gev.get(t - 2, 0))
            lastm = None
            for d in range(DC):
                lastm = te.matmul(pvx[:, 0:8],
                                  xn1_fm[:, d * TOK + t * 128:
                                         d * TOK + t * 128 + 128],
                                  wgb_sb[:, d * 8:(d + 1) * 8],
                                  start=(d == 0), stop=False)
            lastm = te.matmul(pvx[:, 0:8], self.ones_col[:],
                              self.bg_rb[:], start=False, stop=True)
            lastm.inc(s5p)
            ve.wait(s5p)
            ve.tensor_copy(out=glog[:, t * 8:(t + 1) * 8], in_=pvx[:, 0:8])\
                .inc(s5v)
            if t == 0:
                snap_gev = {}
            snap_gev[t] = s5v.n
        sc.wait(s5v)
        sc.activation(out=gext[:], in_=glog[:], func=AF.Exp).inc(s5s)
        ve.wait(s5s)
        g3 = gext.rearrange("p (t e) -> p t e", e=8)
        ve.reduce_sum(out=zg[:, 0:16], in_=g3, axis=AX.X).inc(s5v)
        ve.wait(s5v)
        ve.reduce_max(out=zg[:, 16:32], in_=g3, axis=AX.X).inc(s5v)
        ve.wait(s5v)
        ve.reciprocal(out=zg[:, 32:48], in_=zg[:, 0:16]).inc(s5v)
        gt2 = kb.sbuf(P5t, "gt2", [128, 16], F32)
        for t in range(NT):
            sl = slice(t * 8, (t + 1) * 8)
            ve.wait(s5v)
            ve.tensor_single_scalar(out=gt2[:, 0:8], in_=gext[:, sl],
                                    scalar=zg[:, 16 + t:17 + t],
                                    op=ALU.is_equal).inc(s5v)
            ve.wait(s5v)
            ve.scalar_tensor_tensor(out=gt2[:, 0:8], in0=gt2[:, 0:8],
                                    scalar=-1.0e30, in1=gext[:, sl],
                                    op0=ALU.mult, op1=ALU.add).inc(s5v)
            ve.wait(s5v)
            ve.reduce_max(out=gt2[:, 8:9], in_=gt2[:, 0:8], axis=AX.X)\
                .inc(s5v)
            ve.wait(s5v)
            ve.tensor_single_scalar(out=gt2[:, 0:8], in_=gext[:, sl],
                                    scalar=gt2[:, 8:9], op=ALU.is_ge)\
                .inc(s5v)
            ve.wait(s5v)
            ve.tensor_tensor(out=gt2[:, 0:8], in0=gt2[:, 0:8],
                             in1=gext[:, sl], op=ALU.mult).inc(s5v)
            ve.wait(s5v)
            ve.tensor_scalar(out=self.we_mat[:, sl], in0=gt2[:, 0:8],
                             scalar1=zg[:, 32 + t:33 + t], scalar2=None,
                             op0=ALU.mult).inc(s5v)
        web = kb.sbuf(P5t, "web", [128, 128], BF16)
        ve.wait(s5v)
        ve.tensor_copy(out=web[:], in_=self.we_mat[:]).inc(s5v)
        te.wait(s5v)
        for t in range(NT):
            pvx = self.pav[t % 2]
            if t >= 2:
                te.wait(s5v, snap_wev.get(t - 2, 0))
            te.transpose(pvx.bitcast(BF16)[0:8, 0:128],
                         web[:, t * 8:(t + 1) * 8], self.IdB[:]).inc(s5p)
            ve.wait(s5p)
            ve.tensor_copy(out=self.we_fm[:, t * 128:(t + 1) * 128],
                           in_=pvx.bitcast(BF16)[0:8, 0:128]).inc(s5v)
            if t == 0:
                snap_wev = {}
            snap_wev[t] = s5v.n
        P5t.close()
        self.s_p5 = (s5p, s5s, s5v, s_x1sp, s_xn1sp)

    def p6(self):
        kb = self.kb
        ve, sc, te, gp, sy = (kb.vector, kb.scalar, kb.tensor, kb.gpsimd,
                              kb.sync)
        s5p, s5s, s5v, s_x1sp, s_xn1sp = self.s_p5
        s6v = kb.sem("s_p6v")
        s6s = kb.sem("s_p6s")
        s6p = kb.sem("s_p6p")
        s_whld = kb.sem("s_whld")
        s_vsp = kb.sem("s_vsp")

        Whv = self.tb(0, 16384)
        Whg = self.tb(8192, 16384)
        v_g = self.tb(16384, 4096)
        qq_fm = self.tb(18432, 2048)
        lq_fm = self.tb(19456, 2048)
        qk_fm = self.tb(20480, 2048)
        link = self.tb(21504, 2048)
        linkvT = self.tf(22528, 2048)
        qk_tm = self.tb(40960, 256)
        qk4 = self.tb(41088, 256)

        self.L1 = dict(Whv=Whv, Whg=Whg, v_g=v_g, qq=qq_fm, lq=lq_fm,
                       qk=qk_fm, link=link, linkvT=linkvT)

        gp.wait(s_x1sp)
        gp.wait(s5s)
        gp.wait(s5v)
        for s_ in self.s_p4_dma:
            gp.wait(s_)
        wqk_b = kb.sbuf(self.longs, "wqk_b", [128, 1024], BF16, side="left")
        s_wqk = kb.sem("s_wqk")
        for d in range(DC):
            gp.dma_start(out=wqk_b[:, d * 128:(d + 1) * 128],
                         in_=self.din["Wqk"][d * 128:(d + 1) * 128, :])\
                .inc(s_wqk, 16)
        for d in range(DC):
            gp.dma_start(out=Whv[:, d * 2048:(d + 1) * 2048],
                         in_=self.din["Wh"][d * 128:(d + 1) * 128, 0:2048])\
                .inc(s_whld, 16)
            gp.dma_start(out=Whg[:, d * 2048:(d + 1) * 2048],
                         in_=self.din["Wh"][d * 128:(d + 1) * 128, 2048:4096])\
                .inc(s_whld, 16)

        def nslice(d, tok0, w):
            return self.tb(self.normed_off, 16384)[:, d * TOK + tok0:
                                                   d * TOK + tok0 + w]

        te.wait(s5v)
        te.wait(s5s)
        te.wait(s_wqk)
        if self.upto == "P6a":
            return
        snap_qkev = 0
        snap_vev = {}
        snap_lkv = {}
        ngroups = 1 if self.upto == "P6b" else NG
        for g in range(ngroups):
            t0 = g * G
            for half in range(2):
                pvx = self.pav[half]
                if g > 0 or half > 0:
                    te.wait(s6v, snap_qkev)
                lastm = None
                for d in range(DC):
                    lastm = te.matmul(pvx[:, 0:128],
                                      nslice(d, t0 + half * 128, 128),
                                      wqk_b[:, d * 128:(d + 1) * 128],
                                      start=(d == 0), stop=False)
                lastm = te.matmul(pvx[:, 0:128], self.ones_col[:],
                                  self.bqk_rb[:], start=False, stop=True)
                lastm.inc(s6p)
                sc.wait(s6p)
                if g > 0 or half > 0:
                    sc.wait(s6v, snap_qkev)
                sc.activation(out=qk_tm[:, half * 128:(half + 1) * 128],
                              in_=pvx[:, 0:128], func=AF.Silu).inc(s6s)
            for vx, dst, is_fm in ((0, qq_fm, True), (1, lq_fm, True),
                                   (2, qk_fm, True), (3, link, False)):
                for half in range(2):
                    ve.wait(s6s)
                    ve.wait(s6v)
                    ve.tensor_tensor(out=qk4[:, half * 128:(half + 1) * 128],
                                     in0=qk_tm[:, half * 128:(half + 1) * 128],
                                     in1=self.qkwb_t[:, vx * 128:(vx + 1) * 128],
                                     op=ALU.mult).inc(s6v)
                    ve.wait(s6v)
                    ve.tensor_tensor(out=qk4[:, half * 128:(half + 1) * 128],
                                     in0=qk4[:, half * 128:(half + 1) * 128],
                                     in1=self.qkwb_t[:, (4 + vx) * 128:
                                                     (5 + vx) * 128],
                                     op=ALU.add).inc(s6v)
                if not is_fm:
                    ve.wait(s6v)
                    ve.tensor_copy(out=link[:, g * 256:(g + 1) * 256],
                                   in_=qk4[:]).inc(s6v)
                else:
                    te.wait(s6v)
                    for half in range(2):
                        te.transpose(
                            self.ptb[:, half * 128:half * 128 + 128],
                            qk4[:, half * 128:(half + 1) * 128],
                            self.IdB[:]).inc(s6p)
                    ve.wait(s6p)
                    for half in range(2):
                        ve.tensor_copy(
                            out=dst[:, g * 256 + half * 128:
                                    g * 256 + half * 128 + 128],
                            in_=self.ptb[:, half * 128:half * 128 + 128])\
                            .inc(s6v)
            snap_qkev = s6v.n
            te.wait(s_whld)
            te.wait(s6v)
            for half in range(2):
                for nb in range(4):
                    vidx = half * 4 + nb
                    ps = self.pS[vidx % 2]
                    if vidx >= 2:
                        te.wait(s6s, snap_vev[vidx - 2])
                    lastm = None
                    for d in range(DC):
                        lastm = te.matmul(
                            ps[:, 0:512],
                            nslice(d, t0 + half * 128, 128),
                            Whv[:, d * 2048 + nb * 512: d * 2048 + nb * 512 + 512],
                            start=(d == 0), stop=False)
                    lastm = te.matmul(ps[:, 0:512], self.ones_col[:],
                                      self.bh_rb[0:1, nb * 512:(nb + 1) * 512],
                                      start=False, stop=True)
                    lastm.inc(s6p)
                    sc.wait(s6p)
                    sc.activation(out=v_g[:, half * 2048 + nb * 512:
                                          half * 2048 + nb * 512 + 512],
                                  in_=ps[:, 0:512], func=AF.Silu).inc(s6s)
                    snap_vev[vidx] = s6s.n
            sy.wait(s6s)
            for half in range(2):
                sy.dma_start(
                    out=self.dscr["v_spill"][t0 + half * 128:
                                             t0 + half * 128 + 128, :],
                    in_=v_g[:, half * 2048:(half + 1) * 2048]).inc(s_vsp, 16)
            snap_vspill = s_vsp.n
            te.wait(s6v)
            te.wait(s6s)
            for nb in range(4):
                lidx = g * 4 + nb
                ps = self.pS[nb % 2]
                if nb >= 2:
                    te.wait(s6v, snap_lkv[lidx - 2])
                lastm = None
                for half in range(2):
                    lastm = te.matmul(
                        ps[:, 0:512],
                        link[:, g * 256 + half * 128: g * 256 + half * 128 + 128],
                        v_g[:, half * 2048 + nb * 512:
                            half * 2048 + nb * 512 + 512],
                        start=(half == 0), stop=(half == 1))
                lastm.inc(s6p)
                ve.wait(s6p)
                dstv = linkvT[:, nb * 512:(nb + 1) * 512]
                if g == 0:
                    ve.tensor_scalar(out=dstv, in0=ps[:, 0:512],
                                     scalar1=1.0 / G, scalar2=None,
                                     op0=ALU.mult).inc(s6v)
                else:
                    ve.scalar_tensor_tensor(out=dstv, in0=ps[:, 0:512],
                                            scalar=1.0 / G, in1=dstv,
                                            op0=ALU.mult, op1=ALU.add)\
                        .inc(s6v)
                snap_lkv[lidx] = s6v.n
            sc.wait(s_vsp, snap_vspill)
            sc.wait(s6v)

        if self.upto in ("P6b",):
            return
        s_cc3 = kb.sem("s_cc3")
        s_gp3 = kb.sem("s_gp3")
        gp.wait(s6v)
        gp.dma_start(out=self.dscr["pair_in"], in_=linkvT).inc(s_gp3, 16)
        gp.wait(s_gp3)
        if self.no_cc:
            gp.dma_start(out=self.dscr["pair_out"][0:128, :],
                         in_=self.dscr["pair_in"]).inc(s_cc3, 16)
            gp.dma_start(out=self.dscr["pair_out"][128:256, :],
                         in_=self.dscr["pair_in"]).inc(s_cc3, 16)
        else:
            gp.collective_compute(
                "AllGather", ALU.bypass,
                replica_groups=[[2 * i, 2 * i + 1] for i in range(4)],
                ins=[self.dscr["pair_in"].opt()],
                outs=[self.dscr["pair_out"].opt()],
            ).inc(s_cc3, 1)
        gp.wait(s_cc3)
        gp.dma_start(out=linkvT, in_=self.dscr["pair_out"][0:128, :])\
            .inc(s_gp3, 16)
        te.wait(kb.sem("s_pmld"))
        te.wait(s6p)
        r = te.matmul(self.pu0[:, 12:13], self.ones_f[:],
                      self.pmask_sb[0:1, 0:1], start=True, stop=True).inc(s6p)
        ve.wait(s6p)
        ve.wait(s6v)
        ve.tensor_copy(out=self.scl[:, 12:13], in_=self.pu0[:, 12:13])\
            .inc(s6v)
        ve.wait(s_gp3)
        ve.wait(s6v)
        ve.tensor_scalar(out=linkvT, in0=linkvT,
                         scalar1=self.scl[:, 12:13], scalar2=None,
                         op0=ALU.mult).inc(s6v)
        self.s_p6 = (s6p, s6s, s6v, s_vsp)

    def p7(self):
        kb = self.kb
        ve, sc, te, gp, sy = (kb.vector, kb.scalar, kb.tensor, kb.gpsimd,
                              kb.sync)
        s6p, s6s, s6v, s_vsp = self.s_p6
        s7v = kb.sem("s_p7v")
        s7s = kb.sem("s_p7s")
        s7p = kb.sem("s_p7p")
        s_wo = kb.sem("s_wo")
        s_vrl = kb.sem("s_vrl")
        s_xnr = kb.sem("s_xnr")
        s_a1 = kb.sem("s_a1")

        L = self.L1
        v_g = L["v_g"]
        prefix = L["linkvT"]
        Wout_b = self.tb(0, 16384)
        gate = self.tb(24576 * 2 // 2, 0)
        gate = self.tb(24576, 4096)
        houtT = self.tb(26624, 4096)
        a1g = self.tb(28672, 2048)
        attnT = self.tb(29696, 512)
        prefixb = self.tb(30720, 2048)
        xn1g = self.tb(31744, 2048)
        sim_s = self.tf(40960, 512)

        sxsp = self.s_p5[4]
        for eng in (sc, ve, te, sy, gp):
            eng.wait(sxsp)
        gp.wait(s6p)
        gp.wait(s6s)
        for m in range(16):
            gp.dma_start(out=Wout_b[:, m * 1024:(m + 1) * 1024],
                         in_=self.din["Wout"][m * 128:(m + 1) * 128, :])\
                .inc(s_wo, 16)

        def nslice(d, tok0, w):
            return self.tb(self.normed_off, 16384)[:, d * TOK + tok0:
                                                   d * TOK + tok0 + w]

        snap = {}
        for g in range(NG):
            t0 = g * G
            sy.wait(s6s)
            if g == 0:
                sy.wait(s_vsp)
                sy.wait(s6p)
                sy.wait(self.s_p5[4])
            if g >= 1:
                sy.wait(s7p, snap[("vfree", g - 1)])
                sy.wait(s7v, snap[("xnfree", g - 1)])
            for half in range(2):
                sy.dma_start(out=v_g[:, half * 2048:(half + 1) * 2048],
                             in_=self.dscr["v_spill"][t0 + half * 128:
                                                      t0 + half * 128 + 128, :])\
                    .inc(s_vrl, 16)
                sy.dma_start(out=xn1g[:, half * 1024:(half + 1) * 1024],
                             in_=self.dscr["xn1_spill"][t0 + half * 128:
                                                        t0 + half * 128 + 128, :])\
                    .inc(s_xnr, 16)
            ve.wait(s6v)
            ve.wait(s7v)
            if g >= 1:
                ve.wait(s7p, snap[("pfb_free", g - 1)])
            ve.tensor_copy(out=prefixb[:], in_=prefix[:]).inc(s7v)
            te.wait(s_wo)
            te.wait(s7v)
            te.wait(s6v)
            for m in range(16):
                pvx = self.pS[m % 2]
                if g * 16 + m >= 2:
                    te.wait(s7s, snap.get(("gev", g * 16 + m - 2), 0))
                lastm = None
                for d in range(DC):
                    lastm = te.matmul(
                        pvx[:, 0:256],
                        L["Whg"][:, d * 2048 + m * 128: d * 2048 + m * 128 + 128],
                        nslice(d, t0, 256),
                        start=(d == 0), stop=(d == DC - 1))
                lastm.inc(s7p)
                sc.wait(s7p)
                sc.activation(out=gate[:, m * 256:(m + 1) * 256],
                              in_=pvx[:, 0:256], func=AF.Silu,
                              bias=self.bh_cols[:, 16 + m:17 + m]).inc(s7s)
                snap[("gev", g * 16 + m)] = s7s.n
            te.wait(s7s)
            if g >= 1:
                te.wait(s7v, snap[("qo_done", g - 1)])
            for half in range(2):
                pvx = self.pav[half]
                r = te.matmul(pvx[:, 0:256],
                              L["qk"][:, g * 256 + half * 128:
                                      g * 256 + half * 128 + 128],
                              L["qq"][:, g * 256:(g + 1) * 256],
                              start=True, stop=True).inc(s7p)
            ve.wait(s7p)
            for half in range(2):
                ve.scalar_tensor_tensor(
                    out=sim_s[:, half * 256:(half + 1) * 256],
                    in0=self.pav[half][:, 0:256], scalar=1.0 / G,
                    in1=self.BiasT[:, half * 256:(half + 1) * 256],
                    op0=ALU.mult, op1=ALU.add).inc(s7v)
            sc.wait(s7v)
            sc.activation(out=sim_s[:], in_=sim_s[:], func=AF.Relu).inc(s7s)
            ve.wait(s7s)
            ve.tensor_tensor(out=attnT[:], in0=sim_s[:], in1=sim_s[:],
                             op=ALU.mult).inc(s7v)
            te.wait(s7v)
            ve.wait(s_vrl)
            te.wait(s_vrl)
            for m in range(16):
                pvx = self.pav[m % 2]
                if m >= 2:
                    te.wait(s7v, snap[("qev", m - 2)])
                lastm = te.matmul(pvx[:, 0:256],
                                  v_g[:, 0 * 2048 + m * 128: 0 * 2048 + m * 128 + 128],
                                  attnT[:, 0:256], start=True, stop=False)
                lastm = te.matmul(pvx[:, 0:256],
                                  v_g[:, 1 * 2048 + m * 128: 1 * 2048 + m * 128 + 128],
                                  attnT[:, 256:512], start=False, stop=False)
                lastm = te.matmul(pvx[:, 0:256],
                                  prefixb[:, m * 128:(m + 1) * 128],
                                  L["lq"][:, g * 256:(g + 1) * 256],
                                  start=False, stop=True)
                lastm.inc(s7p)
                ve.wait(s7p)
                ve.wait(s7s)
                ve.tensor_tensor(out=houtT[:, m * 256:(m + 1) * 256],
                                 in0=pvx[:, 0:256],
                                 in1=gate[:, m * 256:(m + 1) * 256],
                                 op=ALU.mult).inc(s7v)
                snap[("qev", m)] = s7v.n
            snap[("qo_done", g)] = s7v.n
            te.wait(s7v)
            for half in range(2):
                for nb in range(2):
                    widx = half * 2 + nb
                    ps = self.pS[widx % 2]
                    if widx >= 2:
                        te.wait(s7v, snap[("wev", widx - 2)])
                    lastm = None
                    for m in range(16):
                        lastm = te.matmul(
                            ps[:, 0:512],
                            houtT[:, m * 256 + half * 128: m * 256 + half * 128 + 128],
                            Wout_b[:, m * 1024 + nb * 512: m * 1024 + nb * 512 + 512],
                            start=(m == 0), stop=False)
                    lastm = te.matmul(ps[:, 0:512], self.ones_col[:],
                                      self.bout_rb[0:1, nb * 512:(nb + 1) * 512],
                                      start=False, stop=True)
                    lastm.inc(s7p)
                    ve.wait(s7p)
                    ve.wait(s_xnr)
                    ve.tensor_tensor(
                        out=a1g[:, half * 1024 + nb * 512:
                                half * 1024 + nb * 512 + 512],
                        in0=ps[:, 0:512],
                        in1=xn1g[:, half * 1024 + nb * 512:
                                 half * 1024 + nb * 512 + 512],
                        op=ALU.add).inc(s7v)
                    snap[("wev", widx)] = s7v.n
            snap[("xnfree", g)] = s7v.n
            sy.wait(s7v)
            for half in range(2):
                sy.dma_start(
                    out=self.dscr["attn1_spill"][t0 + half * 128:
                                                 t0 + half * 128 + 128, :],
                    in_=a1g[:, half * 1024:(half + 1) * 1024]).inc(s_a1, 16)
            if True:
                snap[("a1", g)] = s_a1.n
            ve.wait(s_a1, snap[("a1", g)]) if False else None
            te.wait(s7v)
            for nb in range(4):
                ps = self.pS[nb % 2]
                if nb >= 2:
                    te.wait(s7v, snap[("lkv2", nb - 2)])
                lastm = None
                for half in range(2):
                    lastm = te.matmul(
                        ps[:, 0:512],
                        L["link"][:, g * 256 + half * 128:
                                  g * 256 + half * 128 + 128],
                        v_g[:, half * 2048 + nb * 512:
                            half * 2048 + nb * 512 + 512],
                        start=(half == 0), stop=(half == 1))
                lastm.inc(s7p)
                ve.wait(s7p)
                ve.scalar_tensor_tensor(
                    out=prefix[:, nb * 512:(nb + 1) * 512],
                    in0=ps[:, 0:512], scalar=1.0 / G,
                    in1=prefix[:, nb * 512:(nb + 1) * 512],
                    op0=ALU.mult, op1=ALU.add).inc(s7v)
                snap[("lkv2", nb)] = s7v.n
            snap[("vfree", g)] = s7p.n
            snap[("pfb_free", g)] = s7p.n
            ve.wait(s_a1, snap[("a1", g)])
        self.s_p7 = (s7p, s7s, s7v, s_a1)

    def p8(self):
        kb = self.kb
        ve, sc, te, gp, sy = (kb.vector, kb.scalar, kb.tensor, kb.gpsimd,
                              kb.sync)
        s7p, s7s, s7v, s_a1 = self.s_p7
        s8v = kb.sem("s_p8v")
        s8s = kb.sem("s_p8s")
        s8p = kb.sem("s_p8p")
        s_xnl = kb.sem("s_xnl")
        s_wm = [kb.sem("s_wm0"), kb.sem("s_wm1")]

        mixed = self.tf(0, 16384)
        xn1f = self.tb(16384, 16384)
        W1b = [self.tb(24576 + i * 2048, 4096) for i in range(2)]
        W3b = [self.tb(28672 + i * 2048, 4096) for i in range(2)]
        W2b = [self.tb(32768 + i * 2048, 4096) for i in range(2)]
        h1b = self.tb(36864, 4096)
        h3b = self.tb(38912, 4096)
        self.mixed = mixed

        for eng in (gp, sy, ve, sc, te):
            eng.wait(s7p)
            eng.wait(s7v)
            eng.wait(s7s)
        gp.wait(s_a1)
        gp.dma_start(out=xn1f[:], in_=self.dscr["xn1fm_spill"]).inc(s_xnl, 16)

        def load_ehb(e, hb, buf):
            for d in range(DC):
                gp.dma_start(
                    out=W1b[buf][:, d * 512:(d + 1) * 512],
                    in_=self.din["W1m"][e, d * 128:(d + 1) * 128,
                                        hb * 512:(hb + 1) * 512])\
                    .inc(s_wm[buf], 16)
                gp.dma_start(
                    out=W3b[buf][:, d * 512:(d + 1) * 512],
                    in_=self.din["W3m"][e, d * 128:(d + 1) * 128,
                                        hb * 512:(hb + 1) * 512])\
                    .inc(s_wm[buf], 16)
            for hc in range(4):
                gp.dma_start(
                    out=W2b[buf][:, hc * 1024:(hc + 1) * 1024],
                    in_=self.din["W2m"][e, hb * 512 + hc * 128:
                                        hb * 512 + hc * 128 + 128, :])\
                    .inc(s_wm[buf], 16)

        load_ehb(0, 0, 0)
        te.wait(s_xnl)
        snap = {}
        first = True
        step = 0
        for e in range(NE):
            for hb in range(4):
                buf = step % 2
                if step + 1 < 32:
                    ne, nhb = (e, hb + 1) if hb < 3 else (e + 1, 0)
                    if step >= 1:
                        gp.wait(s8p, snap[("wfree", step - 1)])
                    load_ehb(ne, nhb, (step + 1) % 2)
                te.wait(s_wm[buf])
                for th in range(2):
                    for which in ("h1", "h3"):
                        hsrc = W1b[buf] if which == "h1" else W3b[buf]
                        for m in range(4):
                            for nb2 in range(2):
                                hidx = m * 2 + nb2
                                ps = self.pS[hidx % 2]
                                if hidx >= 2:
                                    if which == "h1":
                                        te.wait(s8s, snap.get(("h1f", hidx - 2), 0))
                                    else:
                                        te.wait(s8v, snap.get(("h3f", hidx - 2), 0))
                                elif which == "h1":
                                    te.wait(s8v,
                                            snap.get(("mev", 14 + hidx), 0))
                                else:
                                    te.wait(s8s,
                                            snap.get(("h1f", 6 + hidx), 0))
                                lastm = None
                                for d in range(DC):
                                    lastm = te.matmul(
                                        ps[:, 0:512],
                                        hsrc[:, d * 512 + m * 128:
                                             d * 512 + m * 128 + 128],
                                        xn1f[:, d * 2048 + th * 1024 + nb2 * 512:
                                             d * 2048 + th * 1024 + nb2 * 512 + 512],
                                        start=(d == 0), stop=(d == DC - 1))
                                lastm.inc(s8p)
                                bcol = e * 16 + hb * 4 + m
                                if which == "h1":
                                    sc.wait(s8p)
                                    sc.activation(
                                        out=h1b[:, m * 1024 + nb2 * 512:
                                                m * 1024 + nb2 * 512 + 512],
                                        in_=ps[:, 0:512], func=AF.Identity,
                                        bias=self.b1m_cols[:, bcol:bcol + 1])\
                                        .inc(s8s)
                                    snap[("h1f", hidx)] = s8s.n
                                else:
                                    ve.wait(s8p)
                                    ve.tensor_scalar(
                                        out=h3b[:, m * 1024 + nb2 * 512:
                                                m * 1024 + nb2 * 512 + 512],
                                        in0=ps[:, 0:512],
                                        scalar1=self.b3m_cols[:, bcol:bcol + 1],
                                        scalar2=None, op0=ALU.add).inc(s8v)
                                    snap[("h3f", hidx)] = s8v.n
                                    if nb2 == 1:
                                        ve.wait(s8v)
                                        ve.wait(s8s, snap[("h1f", hidx)]
                                                if ("h1f", hidx) in snap else 0)
                                        ve.tensor_tensor(
                                            out=h1b[:, m * 1024:(m + 1) * 1024],
                                            in0=h1b[:, m * 1024:(m + 1) * 1024],
                                            in1=h3b[:, m * 1024:(m + 1) * 1024],
                                            op=ALU.mult).inc(s8v)
                                        sc.wait(s8v)
                                        sc.activation(
                                            out=h1b[:, m * 1024:(m + 1) * 1024],
                                            in_=h1b[:, m * 1024:(m + 1) * 1024],
                                            func=AF.Silu).inc(s8s)
                                        snap[("silu", m)] = s8s.n
                    for tloc in range(8):
                        tt = th * 8 + tloc
                        for nb in range(2):
                            widx = tloc * 2 + nb
                            ps = self.pS[widx % 2]
                            if widx >= 2:
                                te.wait(s8v, snap.get(("mev", widx - 2), 0))
                            else:
                                te.wait(s8v, snap.get(("h3f", 6 + widx), 0))
                            lastm = None
                            for m in range(4):
                                if widx == 0:
                                    te.wait(s8s, snap[("silu", m)])
                                lastm = te.matmul(
                                    ps[:, 0:512],
                                    h1b[:, m * 1024 + tloc * 128:
                                        m * 1024 + tloc * 128 + 128],
                                    W2b[buf][:, m * 1024 + nb * 512:
                                             m * 1024 + nb * 512 + 512],
                                    start=(m == 0), stop=(m == 3))
                            lastm.inc(s8p)
                            ve.wait(s8p)
                            dst = mixed[:, tt * 1024 + nb * 512:
                                        tt * 1024 + nb * 512 + 512]
                            wecol = self.we_mat[:, tt * 8 + e:tt * 8 + e + 1]
                            if first:
                                ve.tensor_scalar(out=dst, in0=ps[:, 0:512],
                                                 scalar1=wecol, scalar2=None,
                                                 op0=ALU.mult).inc(s8v)
                            else:
                                ve.scalar_tensor_tensor(
                                    out=dst, in0=ps[:, 0:512], scalar=wecol,
                                    in1=dst, op0=ALU.mult, op1=ALU.add)\
                                    .inc(s8v)
                            snap[("mev", widx)] = s8v.n
                    first = False
                snap[("wfree", step)] = s8p.n
                step += 1
        self.s_p8 = (s8p, s8s, s8v)

    def p9(self):
        kb = self.kb
        ve, sc, te, gp, sy = (kb.vector, kb.scalar, kb.tensor, kb.gpsimd,
                              kb.sync)
        s8p, s8s, s8v = self.s_p8
        s9v = kb.sem("s_p9v")
        s9s = kb.sem("s_p9s")
        s9p = kb.sem("s_p9p")
        s_wf = kb.sem("s_wf")
        s_x1r = [kb.sem("s_x1r0"), kb.sem("s_x1r1")]
        s_a1r = [kb.sem("s_a1r0"), kb.sem("s_a1r1")]
        s_out = [kb.sem("s_out0"), kb.sem("s_out1")]

        mixedb = self.tb(16384, 16384)
        mixed_fm = self.tb(24576, 16384)
        Wfb = self.tb(32768, 8192)
        P9t = contextlib.ExitStack()
        x1r = [self.tf(36864 + i * 1024, 1024) for i in range(2)]
        otile = [self.tf(38912 + i * 1024, 1024) for i in range(2)]
        a1r = [self.tb(16384 + i * 512, 1024) for i in range(2)]

        for eng in (gp, sy, ve, sc, te):
            eng.wait(s8p)
            eng.wait(s8v)
            eng.wait(s8s)
        for d in range(DC):
            gp.dma_start(out=Wfb[:, d * 1024:(d + 1) * 1024],
                         in_=self.din["Wf"][d * 128:(d + 1) * 128, :])\
                .inc(s_wf, 16)
        ve.wait(s8v)
        ve.tensor_copy(out=mixedb[:], in_=self.mixed[:]).inc(s9v)
        snap = {}
        for t in range(NT):
            te.wait(s9v)
            if t >= 1:
                te.wait(s9v, snap.get(("ev", t - 1), 0))
            for d in range(DC):
                te.transpose(self.ptb[:, d * 128:(d + 1) * 128],
                             mixedb[:, t * 1024 + d * 128:
                                    t * 1024 + d * 128 + 128],
                             self.IdB[:]).inc(s9p)
            ve.wait(s9p)
            for d in range(DC):
                ve.tensor_copy(
                    out=mixed_fm[:, d * 2048 + t * 128: d * 2048 + t * 128 + 128],
                    in_=self.ptb[:, d * 128:(d + 1) * 128]).inc(s9v)
            snap[("ev", t)] = s9v.n
        te.wait(s_wf)
        te.wait(s9v)
        sy.wait(s8v)
        sy.wait(s9p)
        sy.wait(s9v)
        for t in range(NT):
            buf = t % 2
            if t >= 2:
                sy.wait(s9v, snap[("ot", t - 2)])
            sy.dma_start(out=x1r[buf][:],
                         in_=self.dscr["x1_spill"][t * 128:(t + 1) * 128, :])\
                .inc(s_x1r[buf], 16)
            sy.dma_start(out=a1r[buf][:],
                         in_=self.dscr["attn1_spill"][t * 128:(t + 1) * 128, :])\
                .inc(s_a1r[buf], 16)
            if t >= 2:
                ve.wait(s_out[t % 2], snap[("od", t - 2)])
            for nb in range(2):
                pidx = t * 2 + nb
                ps = self.pS[pidx % 2]
                if pidx >= 2:
                    te.wait(s9v, snap.get(("fev", pidx - 2), 0))
                lastm = None
                for d in range(DC):
                    lastm = te.matmul(
                        ps[:, 0:512],
                        mixed_fm[:, d * 2048 + t * 128: d * 2048 + t * 128 + 128],
                        Wfb[:, d * 1024 + nb * 512: d * 1024 + nb * 512 + 512],
                        start=(d == 0), stop=False)
                lastm = te.matmul(
                    ps[:, 0:512], self.we_fm[:, t * 128:(t + 1) * 128],
                    self.b2m_b[:, nb * 512:(nb + 1) * 512],
                    start=False, stop=False)
                lastm = te.matmul(
                    ps[:, 0:512], self.ones_col[:],
                    self.bf_rb[0:1, nb * 512:(nb + 1) * 512],
                    start=False, stop=True)
                lastm.inc(s9p)
                ve.wait(s9p)
                ve.wait(s_x1r[buf])
                ve.tensor_tensor(out=otile[buf][:, nb * 512:(nb + 1) * 512],
                                 in0=ps[:, 0:512],
                                 in1=x1r[buf][:, nb * 512:(nb + 1) * 512],
                                 op=ALU.add).inc(s9v)
                ve.wait(s_a1r[buf])
                ve.wait(s9v)
                ve.tensor_tensor(out=otile[buf][:, nb * 512:(nb + 1) * 512],
                                 in0=otile[buf][:, nb * 512:(nb + 1) * 512],
                                 in1=a1r[buf][:, nb * 512:(nb + 1) * 512],
                                 op=ALU.add).inc(s9v)
                snap[("fev", pidx)] = s9v.n
            sy.wait(s9v)
            sy.dma_start(out=self.out[t * 128:(t + 1) * 128, :],
                         in_=otile[buf][:]).inc(s_out[t % 2], 16)
            snap[("ot", t)] = s9v.n
            snap[("od", t)] = s_out[t % 2].n
        P9t.close()

    def finish(self):
        for nm, sem in list(self.kb._sems.items()):
            if sem.n > 0:
                self.kb.gpsimd.wait(sem)
        self.kb.replay()
        return self.nc, self.es

    def dump(self, aps, waits=()):
        kb = self.kb
        s_fin = kb.sem("s_fin")
        for s in waits:
            kb.gpsimd.wait(kb.sem(s) if isinstance(s, str) else s)
        flat = self.out.rearrange("a b -> (a b)")
        off = 0
        for ap in aps:
            p, L = ap.shape
            kb.gpsimd.dma_start(
                out=flat[off:off + p * L].rearrange("(p l) -> p l", p=p),
                in_=ap).inc(s_fin, 16)
            off += p * L
        kb.gpsimd.wait(s_fin)

    def finish2(self):
        self.kb.replay()
        return self.nc, self.es


def build(upto=None, no_cc=False):
    pr = Prog(upto, no_cc=no_cc)
    pr.p0()
    pr.p1()
    kb = pr.kb
    if upto in ("P1", "P1s"):
        if upto == "P1":
            aps = [pr.ar_b16[:, d * N: d * N + 2048] for d in range(DC)]
        else:
            aps = [pr.stats[:, 48:51]]
        pr.dump(aps, waits=("s_p1ve", "s_p1sc"))
        return pr.finish()
    pr.p2()
    if upto in (None, "P3", "P3q", "P3m", "P3z", "P3t", "P4", "P5", "P5b", "P6a", "P6b", "P6d",
                "P6e", "P7"):
        pr.p3()
        if upto == "P3q":
            pr.dump([pr.tb(32768, 16384)], waits=("s_p3v", "s_p3s", "s_p3p"))
            return pr.finish()
        if upto == "P3z":
            pr.dump([pr.tb(9216, 4096)], waits=("s_p3v", "s_p3s"))
            return pr.finish()
        if upto == "P3t":
            pr.dump([pr.tb(11264, 4096)], waits=("s_p3v", "s_p3s"))
            return pr.finish()
        if upto == "P3m":
            s_fin = kb.sem("s_fin")
            for s_ in pr.s_hsp_m:
                kb.gpsimd.wait(s_)
            kb.gpsimd.dma_start(
                out=pr.out.rearrange("a b -> (a b)")[0:512 * 2048]
                .rearrange("(p l) -> p l", p=512),
                in_=pr.dscr["h_spill"][0:512, :]).inc(s_fin, 16)
            kb.gpsimd.wait(s_fin)
            return pr.finish()
        if upto == "P3":
            s_fin = kb.sem("s_fin")
            for s_ in pr.s_hsp_m:
                kb.gpsimd.wait(s_)
            kb.gpsimd.dma_start(
                out=pr.out.rearrange("a b -> (a b)")
                .rearrange("(p l) -> p l", p=1024),
                in_=pr.dscr["h_spill"][0:1024, :]).inc(s_fin, 16)
            kb.gpsimd.wait(s_fin)
            return pr.finish()
        pr.p4()
        if upto == "P4":
            pr.dump([pr.tf(0, 16384)], waits=("s_p4v",))
            return pr.finish()
        pr.p5()
        if upto == "P5":
            pr.dump([pr.tb(32768, 16384)], waits=("s_p5v", "s_p5s"))
            return pr.finish()
        if upto == "P5b":
            aps = [pr.we_mat[:], pr.tb(16384, 16256)]
            pr.dump(aps, waits=("s_p5v", "s_p5s"))
            return pr.finish()
        pr.p6()
        if upto in ("P6a", "P6b"):
            aps = [pr.tb(32768, 16384)]
            pr.dump(aps, waits=("s_p6v", "s_p6s", "s_p6p", "s_whld", "s_wqk"))
            return pr.finish()
        if upto == "P6e":
            aps = [pr.qkwb_t[:], pr.tb(40960, 256), pr.tb(32768, 14336)]
            pr.dump(aps, waits=("s_p6v", "s_p6s", "s_p6p"))
            return pr.finish()
        if upto == "P6d":
            L = pr.L1
            aps = [L["linkvT"], L["qq"], L["lq"], L["qk"], L["link"]]
            pr.dump(aps, waits=("s_p6v", "s_p6s", "s_p6p"))
            s_fin2 = kb.sem("s_fin2")
            kb.gpsimd.wait(kb.sem("s_vsp"))
            kb.gpsimd.dma_start(
                out=pr.out.rearrange("a b -> (a b)")
                [2048 * 512:2048 * 512 + 512 * 2048]
                .rearrange("(p l) -> p l", p=512),
                in_=pr.dscr["v_spill"][0:512, :]).inc(s_fin2, 16)
            kb.gpsimd.wait(s_fin2)
            return pr.finish()
        pr.p7()
        if upto is None:
            pr.p8()
            pr.p9()
            return pr.finish()
        if upto == "P7":
            s_fin = kb.sem("s_fin")
            kb.gpsimd.wait(kb.sem("s_a1"))
            kb.gpsimd.dma_start(out=pr.out, in_=pr.dscr["attn1_spill"])\
                .inc(s_fin, 16)
            kb.gpsimd.wait(s_fin)
            return pr.finish()
        raise NotImplementedError(upto)
    if upto == "P2":
        s_fin = kb.sem("s_fin")
        kb.gpsimd.wait(kb.sem("s_ast0"))
        kb.gpsimd.wait(kb.sem("s_ast1"))
        kb.gpsimd.dma_start(out=pr.out, in_=pr.dscr["attn0_spill"])\
            .inc(s_fin, 16)
        kb.gpsimd.wait(s_fin)
        return pr.finish()
    raise NotImplementedError(upto)


def _stage_inputs(inputs):
    x = np.ascontiguousarray(np.asarray(inputs["L0_attn_x"], np.float32))
    x2 = x.reshape(B, 2, TOK, D)
    com = {
        "W0": inputs["L0_attn_W"],
        "w1": inputs["L0_ffn_w1"], "b1": inputs["L0_ffn_b1"],
        "w2": inputs["L0_ffn_w2"], "b2": inputs["L0_ffn_b2"],
        "ln_w": inputs["L1_attn_ln_w"], "ln_b": inputs["L1_attn_ln_b"],
        "Wh": inputs["L1_attn_Wh"], "bh": inputs["L1_attn_bh"],
        "Wqk": inputs["L1_attn_Wqk"], "bqk": inputs["L1_attn_bqk"],
        "qk_w": inputs["L1_attn_qk_w"], "qk_b": inputs["L1_attn_qk_b"],
        "rel_table": inputs["L1_attn_rel_table"],
        "Wout": inputs["L1_attn_Wout"], "bout": inputs["L1_attn_bout"],
        "W1m": inputs["L1_ffn_W1"], "b1m": inputs["L1_ffn_b1"],
        "W3m": inputs["L1_ffn_W3"], "b3m": inputs["L1_ffn_b3"],
        "W2m": inputs["L1_ffn_W2"], "b2m": inputs["L1_ffn_b2"],
        "Wg": inputs["L1_ffn_Wg"], "bg": inputs["L1_ffn_bg"],
        "Wf": inputs["L1_ffn_Wf"], "bf": inputs["L1_ffn_bf"],
    }
    com = {k: np.ascontiguousarray(np.asarray(v, np.float32))
           for k, v in com.items()}
    maps = []
    for c in range(NCORES):
        b, h = c // 2, c % 2
        pm = np.zeros((1, 16), np.float32)
        pm[0, 0] = 1.0 if h == 1 else 0.0
        m = dict(com)
        m["x_loc"] = np.ascontiguousarray(x2[b, h])
        m["x_rem"] = np.ascontiguousarray(x2[b, 1 - h])
        m["w1shard"] = np.ascontiguousarray(com["w1"][c * 512:(c + 1) * 512])
        m["w2shard"] = np.ascontiguousarray(com["w2"][c * 128:(c + 1) * 128])
        m["pmask"] = pm
        maps.append(m)
    return maps


def _run(upto=None, inputs=None, core_ids=None):
    nc, es = build(upto, no_cc=(core_ids is not None and len(core_ids) == 1))
    with es:
        maps = _stage_inputs(inputs)
        if core_ids is None:
            core_ids = list(range(NCORES))
        res = run_bass_kernel_spmd(nc, [maps[c] for c in core_ids],
                                   core_ids=core_ids)
    return res


def kernel(**inputs):
    res = _run(upto=None, inputs=inputs)
    outs = [res.results[c]["out"] for c in range(NCORES)]
    full = np.stack(outs).reshape(B, 2, TOK, D).reshape(B, N, D)
    return np.ascontiguousarray(full.astype(np.float32))


# version 13
# speedup vs baseline: 1.0001x; 1.0001x over previous
import contextlib
import numpy as np
import ml_dtypes
import concourse.bass as bass
import concourse.mybir as mybir
from concourse.ap import AP
from concourse.bass_utils import run_bass_kernel_spmd

F32 = mybir.dt.float32
BF16 = mybir.dt.bfloat16
AF = mybir.ActivationFunctionType
ALU = mybir.AluOpType
AX = mybir.AxisListType

NCORES = 8
B, N, D = 4, 4096, 1024
TOK = 2048
NT = TOK // 128
NKC = N // 128
DC = D // 128
MAGIC = float(np.float32(1.5 * 2 ** 23))
LN_EPS = 1e-5
G = 256
NG = TOK // G
HID1 = 2048
QK = 128
MOE_H = 2048
NE = 8
BIAS_SCALE = float(128.0 ** 0.5)
NPAIR = 8

ARENA_F32 = 41472

ENGS = ("tensor", "scalar", "vector", "gpsimd", "sync")


class Sem:
    __slots__ = ("h", "n")

    def __init__(self, h):
        self.h = h
        self.n = 0


class Rec:
    __slots__ = ("incs", "kb", "eng")

    def __init__(self, kb, eng):
        self.incs = []
        self.kb = kb
        self.eng = eng

    def inc(self, sem, k=1):
        self.incs.append((sem, k))
        sem.n += k
        self.kb.last_inc[self.eng] = True
        return self


class EngRec:
    def __init__(self, kb, name):
        self._kb = kb
        self._name = name

    def wait(self, sem, val=None):
        v = sem.n if val is None else val
        if v <= 0:
            return
        h = sem.h
        self._kb.streams[self._name].append(lambda E: E.wait_ge(h, v))

    def __getattr__(self, meth):
        kb, name = self._kb, self._name

        def call(*a, **kw):
            r = Rec(kb, name)

            def do(E):
                ins = getattr(E, meth)(*a, **kw)
                for sem, k in r.incs:
                    ins.then_inc(sem.h, k)
                return ins

            kb.streams[name].append(do)
            kb.last_inc[name] = False
            return r

        return call


class KB:
    def __init__(self, nc, es):
        self.nc = nc
        self.es = es
        self.streams = {e: [] for e in ENGS}
        self.last_inc = {e: True for e in ENGS}
        for e in ENGS:
            setattr(self, e, EngRec(self, e))
        self._sems = {}

    def sem(self, name):
        if name not in self._sems:
            self._sems[name] = Sem(self.es.enter_context(self.nc.semaphore(name)))
        return self._sems[name]

    def sbuf(self, stack, name, shape, dtype, side="right"):
        return stack.enter_context(
            self.nc.sbuf_tensor(name, shape, dtype, side=side))

    def psum(self, stack, name, shape, dtype=F32):
        return stack.enter_context(self.nc.psum_tensor(name, shape, dtype))

    def replay(self):
        with self.nc.Block() as block:
            for e in ENGS:
                fns = self.streams[e]
                getattr(block, e)(lambda E, fns=fns: [f(E) for f in fns])


def _t5_bucket_ext():
    ks = np.arange(-255, 256)
    n = np.maximum(ks, 0).astype(np.int64)
    val = (np.log(np.maximum(n, 1).astype(np.float32) / np.float32(16.0))
           / np.float32(np.log(8.0)) * np.float32(16.0)).astype(np.int32)
    val = np.minimum(16 + val, 31)
    return np.where(n < 16, n, val).astype(np.int64)


def _onehot_511():
    b = _t5_bucket_ext()
    oh = np.zeros((32, 511), np.float32)
    oh[b, np.arange(511)] = 1.0
    return oh


class Prog:
    def __init__(self, upto=None, no_cc=False):
        self.no_cc = no_cc
        self.upto = upto
        nc = bass.Bass("TRN2", target_bir_lowering=False, debug=False)
        self.nc = nc
        self.es = contextlib.ExitStack()
        self.kb = KB(nc, self.es)
        self._declare_io()
        self._consts()
        self.longs = contextlib.ExitStack()
        self.es.enter_context(self.longs)

    def _declare_io(self):
        nc = self.nc
        self.din = {}
        for nm, shp in [
            ("x_loc", [TOK, D]), ("x_rem", [TOK, D]),
            ("W0", [D, 3072]),
            ("w1", [4096, D]), ("b1", [4096]), ("w2", [D, 4096]), ("b2", [D]),
            ("w1shard", [512, D]), ("w2shard", [128, 4096]),
            ("ln_w", [D]), ("ln_b", [D]),
            ("Wh", [D, 2 * HID1]), ("bh", [2 * HID1]),
            ("Wqk", [D, QK]), ("bqk", [QK]),
            ("qk_w", [4, QK]), ("qk_b", [4, QK]), ("rel_table", [32]),
            ("Wout", [HID1, D]), ("bout", [D]),
            ("W1m", [NE, D, MOE_H]), ("b1m", [NE, MOE_H]),
            ("W3m", [NE, D, MOE_H]), ("b3m", [NE, MOE_H]),
            ("W2m", [NE, MOE_H, D]), ("b2m", [NE, D]),
            ("Wg", [D, NE]), ("bg", [NE]),
            ("Wf", [D, D]), ("bf", [D]),
            ("pmask", [1, 16]),
        ]:
            self.din[nm] = nc.dram_tensor(nm, shp, F32, kind="ExternalInput").ap()
        self.out = nc.dram_tensor("out", [TOK, D], F32, kind="ExternalOutput").ap()
        d = {}
        d["h_spill"] = nc.dram_tensor("h_spill", [4096, TOK], F32).ap()
        d["x1_spill"] = nc.dram_tensor("x1_spill", [TOK, D], F32).ap()
        d["attn0_spill"] = nc.dram_tensor("attn0_spill", [TOK, D], BF16).ap()
        d["attn1_spill"] = nc.dram_tensor("attn1_spill", [TOK, D], BF16).ap()
        d["v_spill"] = nc.dram_tensor("v_spill", [TOK, HID1], BF16).ap()
        d["red1_in"] = nc.dram_tensor("red1_in", [1, 16], F32).ap()
        d["red1_out"] = nc.dram_tensor("red1_out", [1, 16], F32).ap()
        d["red2_in"] = nc.dram_tensor("red2_in", [1, 16], F32).ap()
        d["red2_out"] = nc.dram_tensor("red2_out", [1, 16], F32).ap()
        d["pair_in"] = nc.dram_tensor("pair_in", [128, HID1], F32).ap()
        d["pair_out"] = nc.dram_tensor("pair_out", [256, HID1], F32).ap()
        d["wprime"] = nc.dram_tensor("wprime", [511], F32).ap()
        d["xn1fm_spill"] = nc.dram_tensor("xn1fm_spill", [128, 16384], BF16).ap()
        d["xn1_spill"] = nc.dram_tensor("xn1_spill", [TOK, D], BF16).ap()
        self.dscr = d

    def _consts(self):
        nc = self.nc
        self.ident_b = nc.inline_tensor(
            np.eye(128, dtype=ml_dtypes.bfloat16), "identb").ap()
        self.ident_f = nc.inline_tensor(np.eye(128, dtype=np.float32), "identf").ap()
        self.oh511 = nc.inline_tensor(_onehot_511(), "oh511").ap()
        m = np.zeros((256, 256), np.float32)
        jj, ii = np.meshgrid(np.arange(256), np.arange(256), indexing="ij")
        m[jj > ii] = -1.0e4
        m2 = np.zeros((128, 512), np.float32)
        for jc in range(2):
            m2[:, jc * 256:(jc + 1) * 256] = m[jc * 128:(jc + 1) * 128, :]
        self.maskT_add = nc.inline_tensor(m2, "masktadd").ap()

    def _arena(self):
        self.ar_f32 = self.kb.sbuf(self.longs, "arena", [128, ARENA_F32], F32,
                                   side="left")
        self.ar_b16 = self.ar_f32.bitcast(BF16)

    def tf(self, off, cols):
        assert off + cols <= ARENA_F32, (off, cols)
        return self.ar_f32[:, off:off + cols]

    def tb(self, off, cols):
        assert 2 * off + cols <= 2 * ARENA_F32, (off, cols)
        return self.ar_b16[:, 2 * off:2 * off + cols]

    def p0(self):
        kb = self.kb
        LS = self.longs
        self.P0 = contextlib.ExitStack()
        self.es.enter_context(self.P0)
        s_id = kb.sem("s_id")
        s_pe = kb.sem("s_pe0")
        s_ve = kb.sem("s_ve0")
        s_sc = kb.sem("s_sc0")

        self._arena()
        self.ones_col = kb.sbuf(LS, "ones_col", [1, 128], BF16, side="left")
        kb.vector.memset(self.ones_col[:], 1.0).inc(s_ve)
        self.ones_f = kb.sbuf(LS, "ones_f", [1, 128], F32, side="left")
        kb.vector.memset(self.ones_f[:], 1.0).inc(s_ve)
        self.IdB = kb.sbuf(LS, "IdB", [128, 128], BF16, side="left")
        self.IdF = kb.sbuf(LS, "IdF", [128, 128], F32, side="left")
        kb.sync.dma_start(out=self.IdB[:], in_=self.ident_b).inc(s_id, 16)
        kb.sync.dma_start(out=self.IdF[:], in_=self.ident_f).inc(s_id, 16)
        self.stats = kb.sbuf(LS, "stats", [128, 64], F32, side="left")
        kb.vector.memset(self.stats[:], 0.0).inc(s_ve)
        self.gmax1 = kb.sbuf(LS, "gmax1", [1, 16], F32, side="left")
        self.gmax2 = kb.sbuf(LS, "gmax2", [1, 16], F32, side="left")
        self.scl = kb.sbuf(LS, "scl", [128, 16], F32, side="left")
        self.sclr = kb.sbuf(LS, "sclr", [1, 16], F32, side="left")
        self.sclb = kb.sbuf(LS, "sclb", [1, 16], BF16, side="left")

        self.pS = [kb.psum(LS, f"pS{i}", [128, 1024], F32) for i in range(2)]
        self.pav = [kb.psum(LS, f"pav{i}", [128, 260], F32) for i in range(2)]
        self.ptb = kb.psum(LS, "ptb", [128, 1024], BF16)
        self.pu0 = kb.psum(LS, "pu0", [128, 512], F32)

        def bias_cols(name, src_flat_ap, L):
            c = L // 128
            s_b = kb.sem("s_bc_" + name)
            nat = kb.sbuf(self.P0, name + "_nat", [c, 128], F32)
            cols = kb.sbuf(LS, name + "_cols", [128, c], F32, side="left")
            kb.sync.dma_start(out=nat[:],
                              in_=src_flat_ap.rearrange("(c p) -> c p", p=128))\
                .inc(s_b, 16)
            kb.tensor.wait(s_b)
            kb.tensor.wait(s_id)
            kb.tensor.wait(s_ve)
            kb.tensor.transpose(self.pu0[:, 0:c], nat[:], self.IdF[0:c, 0:c])\
                .inc(s_pe)
            kb.vector.wait(s_pe)
            kb.vector.tensor_copy(out=cols[:], in_=self.pu0[:, 0:c]).inc(s_ve)
            return cols

        self.lnw_cols = bias_cols("lnw", self.din["ln_w"], D)
        self.lnb_cols = bias_cols("lnb", self.din["ln_b"], D)
        self.b1_cols = bias_cols("b1q", self.din["b1"], 4096)
        self.b2_cols = bias_cols("b2q", self.din["b2"], D)
        self.b1m_cols = bias_cols("b1m",
                                  self.din["b1m"].rearrange("e h -> (e h)"),
                                  NE * MOE_H)
        self.b3m_cols = bias_cols("b3m",
                                  self.din["b3m"].rearrange("e h -> (e h)"),
                                  NE * MOE_H)
        self.bh_cols = bias_cols("bhc", self.din["bh"], 2 * HID1)

        s_gp0 = kb.sem("s_gp0")
        self.s_gp0 = s_gp0

        def row_b(name, src_ap, L):
            tbf = kb.sbuf(LS, name + "_rb", [1, L], BF16, side="left")
            kb.gpsimd.dma_start(out=tbf[:],
                                in_=src_ap.rearrange("(o l) -> o l", o=1))\
                .inc(s_gp0, 16)
            return tbf

        self.bh_rb = row_b("bh", self.din["bh"], 2 * HID1)
        self.bqk_rb = row_b("bqk", self.din["bqk"], QK)
        self.bout_rb = row_b("bout", self.din["bout"], D)
        self.bg_rb = row_b("bg", self.din["bg"], NE)
        self.bf_rb = row_b("bf", self.din["bf"], D)
        self.b2_rb = row_b("b2r", self.din["b2"], D)
        self.qkw_rb = row_b("qkw", self.din["qk_w"].rearrange("a b -> (a b)"),
                            4 * QK)
        self.qkb_rb = row_b("qkb", self.din["qk_b"].rearrange("a b -> (a b)"),
                            4 * QK)
        self.b2m_b = kb.sbuf(LS, "b2m_b", [NE, D], BF16, side="left")
        kb.gpsimd.dma_start(out=self.b2m_b[:], in_=self.din["b2m"])\
            .inc(s_gp0, 16)
        self.pmask_sb = kb.sbuf(LS, "pmask_sb", [1, 16], F32, side="left")
        kb.sync.dma_start(out=self.pmask_sb[:], in_=self.din["pmask"])\
            .inc(kb.sem("s_pmld"), 16)

        self.qkwb_t = kb.sbuf(LS, "qkwb_t", [128, 8 * 128], F32, side="left")
        kb.tensor.wait(s_gp0)
        kb.tensor.wait(s_ve)
        for i in range(8):
            rowsrc = self.qkw_rb if i < 4 else self.qkb_rb
            rs = rowsrc[0:1, (i % 4) * 128:(i % 4) * 128 + 128]
            pb = self.pS[0] if i % 2 == 0 else self.pS[1]
            if i >= 2:
                kb.tensor.wait(s_ve)
            kb.tensor.matmul(pb[:, 0:128], self.ones_col[:], rs,
                             start=True, stop=True).inc(s_pe)
            kb.vector.wait(s_pe)
            kb.vector.tensor_copy(out=self.qkwb_t[:, i * 128:(i + 1) * 128],
                                  in_=pb[:, 0:128]).inc(s_ve)

        rel_sb = kb.sbuf(self.P0, "rel_sb", [32, 1], F32)
        oh_sb = kb.sbuf(self.P0, "oh_sb", [32, 511], F32)
        s_t5 = kb.sem("s_t5")
        kb.sync.dma_start(out=rel_sb[:],
                          in_=self.din["rel_table"].rearrange("(r o) -> r o", o=1))\
            .inc(s_t5, 16)
        kb.sync.dma_start(out=oh_sb[:], in_=self.oh511).inc(s_t5, 16)
        kb.tensor.wait(s_t5)
        kb.tensor.wait(s_ve)
        kb.tensor.matmul(self.pu0[0:1, 0:511], rel_sb[:], oh_sb[:],
                         start=True, stop=True).inc(s_pe)
        wrow = kb.sbuf(self.P0, "wrow", [1, 511], F32)
        kb.scalar.wait(s_pe)
        kb.scalar.activation(out=wrow[:], in_=self.pu0[0:1, 0:511],
                             func=AF.Identity, scale=BIAS_SCALE).inc(s_sc)
        kb.scalar.wait(s_sc)
        kb.scalar.dma_start(
            out=self.dscr["wprime"].rearrange("(o l) -> o l", o=1),
            in_=wrow[:]).inc(kb.sem("s_st"), 16)
        M1 = kb.sbuf(self.P0, "M1", [128, 2 * 256], F32)
        s_m1 = kb.sem("s_m1")
        kb.sync.wait(kb.sem("s_st"))
        for ci in range(2):
            kb.sync.dma_start(
                out=M1[:, ci * 256:(ci + 1) * 256],
                in_=AP(tensor=self.dscr["wprime"].tensor, offset=ci * 128,
                       ap=[[1, 128], [1, 256]])).inc(s_m1, 16)
        BiasIJ = kb.sbuf(self.P0, "BiasIJ", [128, 2 * 256], F32)
        kb.vector.wait(s_m1)
        for ci in range(2):
            rev = AP(tensor=M1, offset=ci * 256 + 255,
                     ap=[[2 * 256, 128], [-1, 256]])
            kb.vector.tensor_copy(out=BiasIJ[:, ci * 256:(ci + 1) * 256],
                                  in_=rev).inc(s_ve)
        self.BiasT = kb.sbuf(LS, "BiasT", [128, 2 * 256], F32, side="left")
        MaskT = kb.sbuf(self.P0, "MaskT", [128, 2 * 256], F32)
        s_mask = kb.sem("s_mask")
        kb.sync.dma_start(out=MaskT[:], in_=self.maskT_add).inc(s_mask, 16)
        for jc in range(2):
            kb.tensor.wait(s_ve)
            for icb in range(2):
                kb.tensor.transpose(
                    self.pu0[:, icb * 128:icb * 128 + 128],
                    BiasIJ[:, icb * 256 + jc * 128: icb * 256 + jc * 128 + 128],
                    self.IdF[:]).inc(s_pe)
            kb.vector.wait(s_pe)
            kb.vector.wait(s_mask)
            for icb in range(2):
                kb.vector.scalar_tensor_tensor(
                    out=self.BiasT[:, jc * 256 + icb * 128:
                                   jc * 256 + icb * 128 + 128],
                    in0=self.pu0[:, icb * 128:icb * 128 + 128], scalar=1.0,
                    in1=MaskT[:, jc * 256 + icb * 128: jc * 256 + icb * 128 + 128],
                    op0=ALU.mult, op1=ALU.add).inc(s_ve)
        self.p0_done = (s_pe, s_sc, s_ve)
        self.P0.close()

    def bcast_col(self, src11_f32, dst_col_f32, tag):
        kb = self.kb
        s_pe = kb.sem("s_bc_pe")
        s_ve = kb.sem("s_bc_ve")
        kb.vector.tensor_copy(out=self.sclb[0:1, 0:1], in_=src11_f32).inc(s_ve)
        kb.tensor.wait(s_ve)
        kb.tensor.matmul(self.pu0[:, 0:1], self.ones_col[:],
                         self.sclb[0:1, 0:1], start=True, stop=True).inc(s_pe)
        kb.vector.wait(s_pe)
        kb.vector.tensor_copy(out=dst_col_f32, in_=self.pu0[:, 0:1]).inc(s_ve)

    def p1(self):
        kb = self.kb
        XN = 0
        self.XN = XN
        xn_fm = self.tb(0, 2 * 32768)
        self.xn_fm = xn_fm

        P1 = contextlib.ExitStack()
        xld = [kb.sbuf(P1, f"xld{i}", [128, D], F32) for i in range(2)]
        xld += [self.tf(32768 + i * 1024, 1024) for i in range(4)]
        sqscr = kb.sbuf(P1, "sqscr", [128, D], F32)
        xnt = [kb.sbuf(P1, f"xnt{i}", [128, D], BF16) for i in range(2)]
        zb = kb.sbuf(self.longs, "zb", [128, 512], BF16, side="left")
        s_xld = [kb.sem("s_xld0"), kb.sem("s_xld1"),
                 kb.sem("s_xld2"), kb.sem("s_xld3")]
        s_sc = kb.sem("s_p1sc")
        s_ve = kb.sem("s_p1ve")
        s_pe = kb.sem("s_p1pe")
        snap_ldA = {}
        snap_A = {}
        snap_B = {}
        snap_apply = {}
        snap_xnt_done = {}
        snap_evac = {}
        SLOT = (0, 8, 48, 56)

        p0pe, p0sc, p0ve = self.p0_done
        kb.vector.wait(p0sc)
        kb.scalar.wait(p0ve)
        kb.tensor.wait(p0ve)
        kb.sync.wait(p0pe)
        kb.sync.wait(p0ve)
        kb.sync.wait(p0sc)

        ve, sc, te = kb.vector, kb.scalar, kb.tensor
        ve.memset(zb[:], 0.0).inc(s_ve)
        self.zb = zb
        snap_zb = s_ve.n

        def issue_load(t):
            buf = t % 6
            src = self.din["x_loc"] if t < NT else self.din["x_rem"]
            row0 = (t % NT) * 128
            if t >= 4:
                kb.sync.wait(s_sc, snap_A[t - 4][0])
                kb.sync.wait(s_ve, snap_A[t - 4][1])
            if t >= 6:
                kb.sync.wait(s_sc, snap_apply[t - 6])
            kb.sync.dma_start(out=xld[buf][:], in_=src[row0:row0 + 128, :])\
                .inc(s_xld[t % 4], 16)
            snap_ldA[t] = s_xld[t % 4].n

        def stageA(t):
            buf = t % 6
            cb = SLOT[t % 4]
            sc.wait(s_xld[t % 4], snap_ldA[t])
            sc.wait(s_sc)
            if t >= 4:
                sc.wait(s_ve, snap_B[t - 4][1])
            sc.activation(out=sqscr[:], in_=xld[buf][:], func=AF.Square,
                          accum_out=self.stats[:, cb + 1:cb + 2]).inc(s_sc)
            ve.wait(s_xld[t % 4], snap_ldA[t])
            ve.wait(s_ve)
            if t >= 4:
                ve.wait(s_sc, snap_B[t - 4][0])
            ve.reduce_sum(out=self.stats[:, cb + 0:cb + 1], in_=xld[buf][:],
                          axis=AX.X).inc(s_ve)
            snap_A[t] = (s_sc.n, s_ve.n)

        def stageB(t):
            cb = SLOT[t % 4]
            ve.wait(s_ve)
            ve.tensor_scalar(out=self.stats[:, cb + 2:cb + 3],
                             in0=self.stats[:, cb + 0:cb + 1],
                             scalar1=1.0 / D, scalar2=None, op0=ALU.mult)\
                .inc(s_ve)
            ve.wait(s_ve)
            ve.tensor_tensor(out=self.stats[:, cb + 3:cb + 4],
                             in0=self.stats[:, cb + 2:cb + 3],
                             in1=self.stats[:, cb + 2:cb + 3], op=ALU.mult)\
                .inc(s_ve)
            ve.wait(s_ve)
            ve.tensor_scalar(out=self.stats[:, cb + 4:cb + 5],
                             in0=self.stats[:, cb + 3:cb + 4],
                             scalar1=-1.0, scalar2=LN_EPS, op0=ALU.mult,
                             op1=ALU.add).inc(s_ve)
            sc.wait(s_sc)
            sc.wait(s_ve)
            sc.activation(out=self.stats[:, cb + 5:cb + 6],
                          in_=self.stats[:, cb + 1:cb + 2], func=AF.Ln,
                          scale=1.0 / D, bias=self.stats[:, cb + 4:cb + 5])\
                .inc(s_sc)
            sc.wait(s_sc)
            sc.activation(out=self.stats[:, cb + 6:cb + 7],
                          in_=self.stats[:, cb + 5:cb + 6], func=AF.Exp,
                          scale=-0.5).inc(s_sc)
            snap_ln = s_sc.n
            ve.wait(s_sc, snap_ln)
            ve.scalar_tensor_tensor(out=self.stats[:, cb + 7:cb + 8],
                                    in0=self.stats[:, cb + 2:cb + 3],
                                    scalar=-1.0,
                                    in1=self.stats[:, cb + 6:cb + 7],
                                    op0=ALU.mult, op1=ALU.mult).inc(s_ve)
            if t < NT:
                ve.wait(s_ve)
                ve.tensor_copy(out=self.stats[:, 16 + t:17 + t],
                               in_=self.stats[:, cb + 6:cb + 7]).inc(s_ve)
                ve.wait(s_ve)
                ve.tensor_copy(out=self.stats[:, 32 + t:33 + t],
                               in_=self.stats[:, cb + 7:cb + 8]).inc(s_ve)
            snap_B[t] = (s_sc.n, s_ve.n)

        def stageC(t):
            buf = t % 6
            cb = SLOT[t % 4]
            buf2 = t % 2
            if t >= 2:
                a, b = snap_xnt_done[t - 2]
                sc.wait(s_ve, a)
                sc.wait(s_pe, b)
            sc.wait(s_ve, snap_B[t][1])
            sc.wait(s_sc)
            sc.activation(out=xnt[buf2][:], in_=xld[buf][:],
                          func=AF.Identity,
                          scale=self.stats[:, cb + 6:cb + 7],
                          bias=self.stats[:, cb + 7:cb + 8]).inc(s_sc)
            snap_apply[t] = s_sc.n
            te.wait(s_sc, snap_apply[t])
            if t >= 1:
                te.wait(s_ve, snap_evac[t - 1])
            for d in range(DC):
                te.transpose(self.ptb[:, d * 128:(d + 1) * 128],
                             xnt[buf2][:, d * 128:(d + 1) * 128], self.IdB[:])\
                    .inc(s_pe)
            ve.wait(s_pe)
            if t == 0:
                ve.wait(s_ve, snap_zb)
            for d in range(DC):
                ve.tensor_tensor(
                    out=self.ar_b16[:, d * N + t * 128: d * N + t * 128 + 128],
                    in0=self.ptb[:, d * 128:(d + 1) * 128],
                    in1=zb[:, 0:128], op=ALU.add).inc(s_ve)
            snap_evac[t] = s_ve.n
            snap_xnt_done[t] = (s_ve.n, s_pe.n)

        issue_load(0)
        issue_load(1)
        for t in range(NKC + 4):
            if t >= 4:
                stageC(t - 4)
            if 2 <= t and t - 2 < NKC:
                stageB(t - 2)
            if t + 2 < NKC:
                issue_load(t + 2)
            if t < NKC:
                stageA(t)

        P1.close()
        self.s_p1 = (s_sc, s_ve, s_pe)

    def xn(self, d, a, w):
        return self.ar_b16[:, d * N + a: d * N + a + w]

    def p2(self):
        kb = self.kb
        s_sc1, s_ve1, s_pe1 = self.s_p1
        ve, sc, te, gp, sy = kb.vector, kb.scalar, kb.tensor, kb.gpsimd, kb.sync

        P2w = contextlib.ExitStack()
        xld2 = [kb.sbuf(P2w, f"xld2_{i}", [128, D], F32) for i in range(2)]
        xn32 = kb.sbuf(P2w, "xn32", [128, D], F32)
        s_x2 = [kb.sem("s_x2_0"), kb.sem("s_x2_1")]
        s_15s = kb.sem("s_15s")
        s_15v = kb.sem("s_15v")
        snap_x2 = {}
        sy.wait(s_ve1)
        sy.wait(s_sc1)
        for t in range(NT):
            buf = t % 2
            if t >= 2:
                sy.wait(s_15s, snap_x2[t - 2])
            sy.dma_start(out=xld2[buf][:],
                         in_=self.din["x_loc"][t * 128:(t + 1) * 128, :])\
                .inc(s_x2[buf], 16)
            sc.wait(s_x2[buf])
            sc.wait(s_15v)
            sc.activation(out=xn32[:], in_=xld2[buf][:], func=AF.Identity,
                          scale=self.stats[:, 16 + t:17 + t],
                          bias=self.stats[:, 32 + t:33 + t]).inc(s_15s)
            snap_x2[t] = s_15s.n
            ve.wait(s_15s)
            ve.wait(s_15v)
            ve.reduce_max(out=self.stats[:, 49:50], in_=xn32[:], axis=AX.X,
                          apply_absolute_value=True).inc(s_15v)
            ve.wait(s_15v)
            if t == 0:
                ve.tensor_copy(out=self.stats[:, 48:49],
                               in_=self.stats[:, 49:50]).inc(s_15v)
            else:
                ve.tensor_tensor(out=self.stats[:, 48:49],
                                 in0=self.stats[:, 48:49],
                                 in1=self.stats[:, 49:50], op=ALU.max)\
                    .inc(s_15v)
        te.wait(s_15v)
        te.wait(s_ve1)
        te.transpose(self.pu0[0:1, 0:128], self.stats[:, 48:49], self.IdF[:])\
            .inc(kb.sem("s_15p"))
        ve.wait(kb.sem("s_15p"))
        ve.reduce_max(out=self.stats[0:1, 50:51], in_=self.pu0[0:1, 0:128],
                      axis=AX.X).inc(s_15v)

        wsh = kb.sbuf(P2w, "wsh", [128, 1024], F32)
        s_wld = kb.sem("s_wld")
        s_wve = kb.sem("s_wve")
        s_wpe = kb.sem("s_wpe")
        sy.wait(s_sc1)
        sy.wait(s_ve1)
        sy.wait(s_pe1)
        ve.wait(s_ve1)
        for j, (col, srcname) in enumerate(((51, "w1shard"), (52, "w2shard"))):
            for i in range(4):
                if j + i > 0:
                    sy.wait(s_wve)
                if srcname == "w1shard":
                    srcap = self.din[srcname][i * 128:(i + 1) * 128, :]
                else:
                    srcap = self.din[srcname][:, i * 1024:(i + 1) * 1024]
                sy.dma_start(out=wsh[:], in_=srcap).inc(s_wld, 16)
                ve.wait(s_wld)
                ve.wait(s_wve)
                ve.reduce_max(out=self.stats[:, 55:56], in_=wsh[:], axis=AX.X,
                              apply_absolute_value=True).inc(s_wve)
                ve.wait(s_wve)
                if i == 0:
                    ve.tensor_copy(out=self.stats[:, col:col + 1],
                                   in_=self.stats[:, 55:56]).inc(s_wve)
                else:
                    ve.tensor_tensor(out=self.stats[:, col:col + 1],
                                     in0=self.stats[:, col:col + 1],
                                     in1=self.stats[:, 55:56], op=ALU.max)\
                        .inc(s_wve)
        P2w.close()
        wred = kb.sbuf(self.longs, "wred", [2, 1], F32, side="left")
        te.wait(s_wve)
        te.wait(s_ve1)
        te.wait(kb.sem("s_15v"))
        te.transpose(self.pu0[0:2, 0:128], self.stats[:, 51:53], self.IdF[:])\
            .inc(s_wpe)
        ve.wait(s_wpe)
        ve.reduce_max(out=wred[:], in_=self.pu0[0:2, 0:128], axis=AX.X)\
            .inc(s_wve)
        te.wait(s_wve)
        te.transpose(self.pu0[0:1, 0:2], wred[:], self.IdF[0:2, 0:2]).inc(s_wpe)
        ve.wait(s_wpe)
        ve.wait(s_ve1)
        ve.wait(kb.sem("s_15v"))
        ve.memset(self.sclr[:], 0.0).inc(s_wve)
        ve.wait(s_wve)
        ve.tensor_copy(out=self.sclr[0:1, 0:1], in_=self.stats[0:1, 50:51])\
            .inc(s_wve)
        ve.wait(s_wve)
        ve.tensor_copy(out=self.sclr[0:1, 1:3], in_=self.pu0[0:1, 0:2])\
            .inc(s_wve)
        s_gp = kb.sem("s_gp")
        s_cc1 = kb.sem("s_cc1")
        gp.wait(s_wve)
        gp.dma_start(out=self.dscr["red1_in"], in_=self.sclr[:]).inc(s_gp, 16)
        gp.wait(s_gp)
        if self.no_cc:
            gp.dma_start(out=self.dscr["red1_out"], in_=self.dscr["red1_in"])\
                .inc(s_cc1, 16)
        else:
            gp.collective_compute(
                "AllReduce", ALU.max, replica_groups=[list(range(NCORES))],
                ins=[self.dscr["red1_in"].opt()],
                outs=[self.dscr["red1_out"].opt()],
            ).inc(s_cc1, 1)
        self.s_cc1 = s_cc1

        O_W, O_KF, O_QF, O_VT, O_EB, O_AG = (32768, 34304, 36352, 37376,
                                             39456, 40484)
        assert O_AG + 512 <= ARENA_F32
        Wq = [self.tb(O_W, 1024)] * 2
        Wk = [self.tb(O_W + 512, 1024)] * 2
        Wv = [self.tb(O_W + 1024, 1024)] * 2
        k_fm = self.tb(O_KF, 4096)
        q_fm = self.tb(O_QF, 2048)
        v_tm = self.tb(O_VT, 4160)
        Eb = [self.tb(O_EB + i * 512, 1024) for i in range(2)]
        att_g = [self.tb(O_AG + i * 256, 512) for i in range(2)]
        self.zt = kb.sbuf(self.longs, "zt", [128, 8], F32, side="left")
        self.zr = kb.sbuf(self.longs, "zr", [128, 8], F32, side="left")

        s_wp_ = kb.sem("s_wp")
        s_wp = [s_wp_, s_wp_]
        s_prj = kb.sem("s_prj")
        s_pse = kb.sem("s_pse")
        s_pve = kb.sem("s_pve")
        s_ast = [kb.sem("s_ast0"), kb.sem("s_ast1")]

        ve.wait(s_ve1)
        vt3 = v_tm.rearrange("p (k c) -> p k c", c=130)
        ve.memset(vt3[:, :, 64:65], 1.0).inc(s_pve)
        ve.memset(vt3[:, :, 129:130], 1.0).inc(s_pve)

        def load_pair(p):
            buf = p % 2
            if p >= 1:
                gp.wait(s_prj, self._wfree[p - 1])
            ha, hb = 2 * p, 2 * p + 1
            for dsttile, off in ((Wq[buf], 0), (Wk[buf], 64), (Wv[buf], 128)):
                for ri, h in enumerate((ha, hb)):
                    c0 = 192 * h + off
                    for d in range(DC):
                        gp.dma_start(
                            out=dsttile[:, d * 128 + ri * 64:
                                        d * 128 + ri * 64 + 64],
                            in_=self.din["W0"][d * 128:(d + 1) * 128,
                                               c0:c0 + 64]).inc(s_wp[buf], 16)

        self._wfree = {}
        snap_exp = {}
        snap_av = {}
        snap_st = {}
        snap_pavfree = 0
        gp.wait(s_sc1)
        gp.wait(s_ve1)
        load_pair(0)

        for p in range(NPAIR):
            buf = p % 2
            te.wait(s_wp[buf])
            te.wait(s_sc1)
            te.wait(s_ve1)
            if p >= 1:
                te.wait(s_pse)
                te.wait(s_pve)
            self._wfree[p] = None
            for blk in range(8):
                ps = self.pS[blk % 2]
                if blk >= 2:
                    te.wait(s_pse, snap_kq.get(blk - 2, 0))
                last = None
                for d in range(DC):
                    last = te.matmul(ps[:, 0:512],
                                     Wk[buf][:, d * 128:(d + 1) * 128],
                                     self.xn(d, blk * 512, 512),
                                     start=(d == 0), stop=(d == DC - 1))
                last.inc(s_prj)
                sc.wait(s_prj)
                sc.activation(out=k_fm[:, blk * 512:(blk + 1) * 512],
                              in_=ps[:, 0:512], func=AF.Identity).inc(s_pse)
                if blk == 0:
                    snap_kq = {}
                snap_kq[blk] = s_pse.n
            for blk in range(4):
                ps = self.pS[blk % 2]
                te.wait(s_pse, snap_kq[6 + (blk % 2)] if blk < 2 else
                        snap_q.get(blk - 2, 0))
                last = None
                for d in range(DC):
                    last = te.matmul(ps[:, 0:512],
                                     Wq[buf][:, d * 128:(d + 1) * 128],
                                     self.xn(d, blk * 512, 512),
                                     start=(d == 0), stop=(d == DC - 1))
                last.inc(s_prj)
                sc.wait(s_prj)
                sc.activation(out=q_fm[:, blk * 512:(blk + 1) * 512],
                              in_=ps[:, 0:512], func=AF.Identity).inc(s_pse)
                if blk == 0:
                    snap_q = {}
                snap_q[blk] = s_pse.n
            snap_vt = {}
            for kc in range(NKC):
                pv = self.pav[kc % 2]
                if kc >= 2:
                    te.wait(s_pve, snap_vt[kc - 2])
                last = None
                for d in range(DC):
                    last = te.matmul(
                        pv[:, 0:128],
                        self.xn(d, kc * 128, 128),
                        Wv[buf][:, d * 128:(d + 1) * 128],
                        start=(d == 0), stop=(d == DC - 1))
                last.inc(s_prj)
                if kc == 0:
                    self._wfree[p] = s_prj.n
                ve.wait(s_prj)
                ve.tensor_copy(out=vt3[:, kc, 0:64], in_=pv[:, 0:64]).inc(s_pve)
                ve.tensor_copy(out=vt3[:, kc, 65:129], in_=pv[:, 64:128])\
                    .inc(s_pve)
                snap_vt[kc] = s_pve.n
            self._wfree[p] = s_prj.n
            if p + 1 < NPAIR:
                load_pair(p + 1)
            te.wait(s_pse)
            te.wait(s_pve)
            flat = [(qb, kc) for qb in range(4) for kc in range(NKC)]

            def emit_S(j):
                qb_, kc_ = flat[j]
                ps_ = self.pS[j % 2]
                if j >= 2:
                    te.wait(s_pse, snap_exp[j - 2])
                te.matmul(ps_[:, 0:512],
                          k_fm[0:64, kc_ * 128:kc_ * 128 + 128],
                          q_fm[0:64, qb_ * 512:(qb_ + 1) * 512],
                          start=True, stop=True, tile_position=(0, 0))
                r_ = te.matmul(ps_[:, 512:1024],
                               k_fm[64:128, kc_ * 128:kc_ * 128 + 128],
                               q_fm[64:128, qb_ * 512:(qb_ + 1) * 512],
                               start=True, stop=True, tile_position=(64, 0))
                r_.inc(s_prj)
                snap_st[j] = s_prj.n

            emit_S(0)
            for idx, (qb, kc) in enumerate(flat):
                if idx + 1 < len(flat):
                    emit_S(idx + 1)
                ps = self.pS[idx % 2]
                sc.wait(s_prj, snap_st[idx])
                if idx >= 2:
                    sc.wait(s_prj, snap_av[idx - 2])
                sc.activation(out=Eb[idx % 2][:], in_=ps[:], func=AF.Exp,
                              scale=0.125).inc(s_pse)
                snap_exp[idx] = s_pse.n
                te.wait(s_pse, snap_exp[idx])
                if kc == 0:
                    te.wait(s_pve, snap_pavfree)
                r = None
                for h in range(2):
                    for tq in range(4):
                        r = te.matmul(
                            self.pav[h][:, tq * 65:(tq + 1) * 65],
                            Eb[idx % 2][:, h * 512 + tq * 128:
                                        h * 512 + tq * 128 + 128],
                            v_tm[:, kc * 130 + h * 65: kc * 130 + h * 65 + 65],
                            start=(kc == 0 and tq == 0),
                            stop=(kc == NKC - 1 and tq == 3))
                r.inc(s_prj)
                snap_av[idx] = s_prj.n
                if kc < NKC - 1:
                    continue
                abuf = att_g[qb % 2]
                ve.wait(s_prj, snap_av[qb * NKC + NKC - 1])
                if p * 4 + qb >= 2:
                    ve.wait(s_ast[qb % 2], self._astd.get((p * 4 + qb) - 2, 0))
                for h in range(2):
                    zap = AP(tensor=self.pav[h], offset=64,
                             ap=[[260, 128], [65, 4]])
                    ve.wait(s_pve)
                    ve.tensor_copy(out=self.zt[:, h * 4:(h + 1) * 4],
                                   in_=zap).inc(s_pve)
                ve.wait(s_pve)
                ve.reciprocal(out=self.zr[:], in_=self.zt[:]).inc(s_pve)
                ve.wait(s_pve)
                for h in range(2):
                    for tq in range(4):
                        ve.tensor_scalar(
                            out=abuf[:, tq * 128 + h * 64: tq * 128 + h * 64 + 64],
                            in0=self.pav[h][:, tq * 65: tq * 65 + 64],
                            scalar1=self.zr[:, h * 4 + tq: h * 4 + tq + 1],
                            scalar2=None, op0=ALU.mult).inc(s_pve)
                snap_pavfree = s_pve.n
                sy.wait(s_pve)
                for tq in range(4):
                    row0 = qb * 512 + tq * 128
                    sy.dma_start(
                        out=self.dscr["attn0_spill"][row0:row0 + 128,
                                                     2 * p * 64: 2 * p * 64 + 128],
                        in_=abuf[:, tq * 128: tq * 128 + 128])\
                        .inc(s_ast[qb % 2], 16)
                if not hasattr(self, "_astd"):
                    self._astd = {}
                self._astd[p * 4 + qb] = s_ast[qb % 2].n
        self.s_p2 = (s_prj, s_pse, s_pve, s_ast)
        self.s_ast_fin = (s_ast[0].n, s_ast[1].n)

    def p3(self):
        kb = self.kb
        ve, sc, te, gp, sy = (kb.vector, kb.scalar, kb.tensor, kb.gpsimd,
                              kb.sync)
        s_prj, s_pse, s_pve, s_ast = self.s_p2

        s_gm1 = kb.sem("s_gm1")
        gp.wait(self.s_cc1)
        gp.dma_start(out=self.gmax1[:], in_=self.dscr["red1_out"]).inc(s_gm1, 16)

        s3v = kb.sem("s_p3v")
        s3s = kb.sem("s_p3s")
        s3p = kb.sem("s_p3p")
        s3ld = kb.sem("s_p3ld")
        ve.wait(s_gm1)
        ve.wait(s_pve)
        te.wait(s_gm1)
        te.wait(s_prj)
        te.wait(kb.sem("s_wve"))
        te.wait(kb.sem("s_15v"))
        for i in range(3):
            r = te.matmul(self.pu0[:, i:i + 1], self.ones_f[:],
                          self.gmax1[0:1, i:i + 1], start=True, stop=True)
            r.inc(s3p)
            ve.wait(s3p)
            ve.tensor_copy(out=self.scl[:, i:i + 1], in_=self.pu0[:, i:i + 1])\
                .inc(s3v)
        for (srcc, dstc) in ((0, 3), (1, 4), (2, 5)):
            ve.wait(s3v)
            ve.reciprocal(out=self.scl[:, 8:9], in_=self.scl[:, srcc:srcc + 1])\
                .inc(s3v)
            ve.wait(s3v)
            ve.tensor_scalar(out=self.scl[:, dstc:dstc + 1],
                             in0=self.scl[:, 8:9], scalar1=127.0, scalar2=None,
                             op0=ALU.mult).inc(s3v)
        for (a, b, dstc) in ((0, 1, 6), (0, 2, 7)):
            ve.wait(s3v)
            ve.tensor_tensor(out=self.scl[:, 8:9], in0=self.scl[:, a:a + 1],
                             in1=self.scl[:, b:b + 1], op=ALU.mult).inc(s3v)
            ve.wait(s3v)
            ve.tensor_scalar(out=self.scl[:, dstc:dstc + 1],
                             in0=self.scl[:, 8:9], scalar1=1.0 / 16129.0,
                             scalar2=None, op0=ALU.mult).inc(s3v)
        ve.wait(s3v)
        ve.memset(self.scl[:, 15:16], -MAGIC).inc(s3v)
        self.b1s = kb.sbuf(self.longs, "b1s", [128, 32], F32, side="left")
        ve.wait(s3v)
        ve.tensor_scalar(out=self.b1s[:], in0=self.b1_cols[:],
                         scalar1=self.scl[:, 6:7], scalar2=None, op0=ALU.mult)\
            .inc(s3v)

        OXQ = 32768
        xq = self.tb(OXQ, 16384)
        self.OXQ = OXQ
        P3t = contextlib.ExitStack()
        x3 = [kb.sbuf(P3t, f"x3_{i}", [128, D], F32) for i in range(2)]
        xn3 = kb.sbuf(P3t, "xn3", [128, D], F32)
        xq3 = kb.sbuf(P3t, "xq3", [128, D], BF16)
        s_x3 = [kb.sem("s_x3_0"), kb.sem("s_x3_1")]
        ve.wait(s_pse)
        ve.wait(kb.sem("s_ast0"))
        ve.wait(kb.sem("s_ast1"))
        te.wait(kb.sem("s_ast0"))
        te.wait(kb.sem("s_ast1"))
        te.wait(s_pse)
        te.wait(s_pve)
        sc.wait(s_pse)
        snap_q3 = {}
        for t in range(NT):
            buf = t % 2
            if t >= 2:
                sy.wait(kb.sem("s_p3s"), snap_q3[t - 2][0])
            sy.dma_start(out=x3[buf][:],
                         in_=self.din["x_loc"][t * 128:(t + 1) * 128, :])\
                .inc(s_x3[buf], 16)
            sc.wait(s_x3[buf])
            sc.wait(s3v)
            sc.activation(out=xn3[:], in_=x3[buf][:], func=AF.Identity,
                          scale=self.stats[:, 16 + t:17 + t],
                          bias=self.stats[:, 32 + t:33 + t]).inc(s3s)
            snap_q3[t] = (s3s.n, s3v.n)
            ve.wait(s3s)
            ve.wait(s3v)
            ve.tensor_scalar(out=xn3[:], in0=xn3[:],
                             scalar1=self.scl[:, 3:4], scalar2=MAGIC,
                             op0=ALU.mult, op1=ALU.add).inc(s3v)
            snap_s1x = s3v.n
            sc.wait(s3v, snap_s1x)
            if t >= 1:
                sc.wait(s3p, snap_q3t)
            sc.wait(s3s)
            sc.activation(out=xq3[:], in_=xn3[:], func=AF.Identity,
                          bias=self.scl[:, 15:16]).inc(s3s)
            snap_s2x = s3s.n
            te.wait(s3s, snap_s2x)
            if t >= 1:
                te.wait(s3v, snap_q3e)
            for d in range(DC):
                te.transpose(self.ptb[:, d * 128:(d + 1) * 128],
                             xq3[:, d * 128:(d + 1) * 128], self.IdB[:])\
                    .inc(s3p)
            snap_q3t = s3p.n
            ve.wait(s3p)
            for d in range(DC):
                ve.tensor_copy(
                    out=xq[:, d * 2048 + t * 128: d * 2048 + t * 128 + 128],
                    in_=self.ptb[:, d * 128:(d + 1) * 128]).inc(s3v)
            snap_q3e = s3v.n
        P3t.close()
        if self.upto == "P3q":
            return

        w1b = [self.tf(i * 4096, 4096) for i in range(2)]
        tq = self.tf(8192, 1024)
        w1q = self.tb(9216, 4096)
        w1qT = self.tb(11264, 4096)
        hblk = self.tf(13312, 8192)
        s_w1ld = [kb.sem("s_w1ld0"), kb.sem("s_w1ld1")]
        s_hsp = kb.sem("s_hsp")
        snap_w1free = {}
        snap_hblk = {}

        sy.wait(s_pse)
        sy.wait(s_pve)
        sy.wait(s_prj)
        sy.wait(s3v)
        sy.wait(s3p)
        hmaxcol = 56

        s_w1p = [s_w1ld[0], s_w1ld[1], kb.sem("s_x2_0"), kb.sem("s_x2_1")]
        s_hsp_m = [s_hsp, kb.sem("s_wld"), kb.sem("s_x3_0"), kb.sem("s_x3_1")]
        self.s_hsp_m = s_hsp_m
        snap_w1p = {}
        snap_s1last = {}

        def load_w1(hb):
            buf = hb % 2
            if hb >= 1:
                sy.wait(s3v, snap_s1last[hb - 1])
            for i in range(4):
                sy.dma_start(
                    out=w1b[buf][:, i * 1024:(i + 1) * 1024],
                    in_=self.din["w1"][hb * 512 + i * 128:
                                       hb * 512 + i * 128 + 128, :])\
                    .inc(s_w1p[i], 16)
                snap_w1p[(hb, i)] = s_w1p[i].n

        tq2 = [self.tf(21504 + j * 1024, 1024) for j in range(2)]
        w1q2 = [w1q, self.tb(23552, 4096)]
        snap_step1 = {}
        snap_step2 = {}
        snap_tg = {}
        snap_evg_sc = {}
        snap_evg_ve = {}
        snap_mm_done = {}
        snap_ev = {}
        snap_absm = {}
        snap_spill = {}

        def quant_piece(hb, i):
            buf = hb % 2
            pc = hb * 4 + i
            ve.wait(s_w1p[i], snap_w1p[(hb, i)])
            if pc >= 2:
                ve.wait(s3s, snap_step2[pc - 2])
            ve.wait(s3v)
            ve.tensor_scalar(out=tq2[pc % 2][:],
                             in0=w1b[buf][:, i * 1024:(i + 1) * 1024],
                             scalar1=self.scl[:, 4:5], scalar2=MAGIC,
                             op0=ALU.mult, op1=ALU.add).inc(s3v)
            snap_step1[pc] = s3v.n
            if i == 3:
                snap_s1last[hb] = s3v.n
            sc.wait(s3v, snap_step1[pc])
            if hb >= 2 and i == 0:
                sc.wait(s3p, snap_tg[(hb - 2, 3)])
            sc.wait(s3s)
            sc.activation(out=w1q2[buf][:, i * 1024:(i + 1) * 1024],
                          in_=tq2[pc % 2][:], func=AF.Identity,
                          bias=self.scl[:, 15:16]).inc(s3s)
            snap_step2[pc] = s3s.n

        def tg(hb, i):
            buf = hb % 2
            te.wait(s3s, snap_step2[hb * 4 + i])
            prev = (hb, i - 1) if i > 0 else (hb - 1, 3)
            if prev in snap_evg_sc:
                te.wait(s3s, snap_evg_sc[prev])
            te.wait(s3p)
            for d in range(DC):
                te.transpose(self.ptb[:, d * 128:(d + 1) * 128],
                             w1q2[buf][:, i * 1024 + d * 128:
                                       i * 1024 + d * 128 + 128],
                             self.IdB[:]).inc(s3p)
            snap_tg[(hb, i)] = s3p.n
            sc.wait(s3p, snap_tg[(hb, i)])
            if hb >= 1 and i == 0:
                sc.wait(s3p, snap_mm_done[hb - 1])
            sc.wait(s3s)
            sc.activation(out=w1qT[:, i * 1024:(i + 1) * 1024],
                          in_=self.ptb[:], func=AF.Identity).inc(s3s)
            snap_evg_sc[(hb, i)] = s3s.n

        load_w1(0)
        for i in range(4):
            quant_piece(0, i)
        NHB = 0 if self.upto == "P3z" else (
            1 if self.upto in ("P3m", "P3t") else 8)
        for hb in range(NHB):
            if hb + 1 < NHB:
                load_w1(hb + 1)
            tg(hb, 0)
            tg(hb, 1)
            if self.upto == "P3t":
                tg(hb, 2)
                tg(hb, 3)
                return
            for m in range(4):
                te.wait(s3s, snap_evg_sc[(hb, m)])
                for tb2 in range(4):
                    idx = m * 4 + tb2
                    ps = self.pS[idx % 2]
                    if hb * 16 + idx >= 2:
                        te.wait(s3s, snap_ev.get(hb * 16 + idx - 2, 0))
                    lastm = None
                    for d in range(DC):
                        lastm = te.matmul(
                            ps[:, 0:512],
                            w1qT[:, m * 1024 + d * 128: m * 1024 + d * 128 + 128],
                            xq[:, d * 2048 + tb2 * 512:
                               d * 2048 + tb2 * 512 + 512],
                            start=(d == 0), stop=(d == DC - 1))
                    lastm.inc(s3p)
                    sc.wait(s3p)
                    sc.wait(s3s)
                    if hb >= 1 and tb2 == 0:
                        sc.wait(s_hsp_m[m], snap_spill[(hb - 1, m)])
                        sc.wait(s3v, snap_absm[(hb - 1, m)])
                    bcol = hb * 4 + m
                    sc.activation(out=hblk[:, m * 2048 + tb2 * 512:
                                           m * 2048 + tb2 * 512 + 512],
                                  in_=ps[:, 0:512], func=AF.Gelu,
                                  scale=self.scl[:, 6:7],
                                  bias=self.b1s[:, bcol:bcol + 1]).inc(s3s)
                    snap_ev[hb * 16 + idx] = s3s.n
                    if tb2 == 0 and m + 2 < 4:
                        tg(hb, m + 2)
                if hb + 1 < NHB:
                    quant_piece(hb + 1, m)
                ve.wait(s3s, snap_ev[hb * 16 + m * 4 + 3])
                ve.wait(s3v)
                ve.reduce_max(out=self.stats[:, 57:58],
                              in_=hblk[:, m * 2048:(m + 1) * 2048], axis=AX.X,
                              apply_absolute_value=True).inc(s3v)
                ve.wait(s3v)
                if hb == 0 and m == 0:
                    ve.tensor_copy(out=self.stats[:, hmaxcol:hmaxcol + 1],
                                   in_=self.stats[:, 57:58]).inc(s3v)
                else:
                    ve.tensor_tensor(out=self.stats[:, hmaxcol:hmaxcol + 1],
                                     in0=self.stats[:, hmaxcol:hmaxcol + 1],
                                     in1=self.stats[:, 57:58], op=ALU.max)\
                        .inc(s3v)
                snap_absm[(hb, m)] = s3v.n
                sy.wait(s3s, snap_ev[hb * 16 + m * 4 + 3])
                sy.dma_start(
                    out=self.dscr["h_spill"][hb * 512 + m * 128:
                                             hb * 512 + m * 128 + 128, :],
                    in_=hblk[:, m * 2048:(m + 1) * 2048]).inc(s_hsp_m[m], 16)
                snap_spill[(hb, m)] = s_hsp_m[m].n
            snap_mm_done[hb] = s3p.n

        if self.upto in ("P3m", "P3z"):
            return
        te.wait(s3v)
        te.wait(s3p)
        te.transpose(self.pu0[0:1, 0:128],
                     self.stats[:, hmaxcol:hmaxcol + 1], self.IdF[:]).inc(s3p)
        ve.wait(s3p)
        ve.reduce_max(out=self.stats[0:1, 58:59], in_=self.pu0[0:1, 0:128],
                      axis=AX.X).inc(s3v)
        ve.wait(s3v)
        ve.memset(self.sclr[:], 0.0).inc(s3v)
        ve.wait(s3v)
        ve.tensor_copy(out=self.sclr[0:1, 0:1], in_=self.stats[0:1, 58:59])\
            .inc(s3v)
        s_cc2 = kb.sem("s_cc2")
        s_gp2 = kb.sem("s_gp2")
        gp.wait(s3v)
        gp.dma_start(out=self.dscr["red2_in"], in_=self.sclr[:]).inc(s_gp2, 16)
        gp.wait(s_gp2)
        if self.no_cc:
            gp.dma_start(out=self.dscr["red2_out"], in_=self.dscr["red2_in"])\
                .inc(s_cc2, 16)
        else:
            gp.collective_compute(
                "AllReduce", ALU.max, replica_groups=[list(range(NCORES))],
                ins=[self.dscr["red2_in"].opt()],
                outs=[self.dscr["red2_out"].opt()],
            ).inc(s_cc2, 1)
        gp.wait(s_cc2)
        gp.dma_start(out=self.gmax2[:], in_=self.dscr["red2_out"]).inc(s_gp2, 16)
        self.s_p3 = (s3p, s3s, s3v, s_hsp, s_gp2)

    def p4(self):
        kb = self.kb
        ve, sc, te, gp, sy = (kb.vector, kb.scalar, kb.tensor, kb.gpsimd,
                              kb.sync)
        s3p, s3s, s3v, s_hsp, s_gp2 = self.s_p3
        s4v = kb.sem("s_p4v")
        s4s = kb.sem("s_p4s")
        s4p = kb.sem("s_p4p")

        ve.wait(s_gp2)
        ve.wait(s3v)
        te.wait(s_gp2)
        te.wait(s3p)
        te.wait(s3v)
        r = te.matmul(self.pu0[:, 3:4], self.ones_f[:],
                      self.gmax2[0:1, 0:1], start=True, stop=True).inc(s4p)
        ve.wait(s4p)
        ve.tensor_copy(out=self.scl[:, 9:10], in_=self.pu0[:, 3:4]).inc(s4v)
        ve.wait(s4v)
        ve.reciprocal(out=self.scl[:, 8:9], in_=self.scl[:, 9:10]).inc(s4v)
        ve.wait(s4v)
        ve.tensor_scalar(out=self.scl[:, 10:11], in0=self.scl[:, 8:9],
                         scalar1=127.0, scalar2=None, op0=ALU.mult).inc(s4v)
        ve.wait(s4v)
        ve.tensor_tensor(out=self.scl[:, 8:9], in0=self.scl[:, 9:10],
                         in1=self.scl[:, 2:3], op=ALU.mult).inc(s4v)
        ve.wait(s4v)
        ve.tensor_scalar(out=self.scl[:, 11:12], in0=self.scl[:, 8:9],
                         scalar1=1.0 / 16129.0, scalar2=None, op0=ALU.mult)\
            .inc(s4v)
        B2S = self.tf(35840, 1024)
        te.wait(s4v)
        for nb in range(2):
            r = te.matmul(self.pS[0][:, nb * 512:(nb + 1) * 512],
                          self.ones_col[:],
                          self.b2_rb[0:1, nb * 512:(nb + 1) * 512],
                          start=True, stop=True).inc(s4p)
        ve.wait(s4p)
        for nb in range(2):
            ve.tensor_scalar(out=B2S[:, nb * 512:(nb + 1) * 512],
                             in0=self.pS[0][:, nb * 512:(nb + 1) * 512],
                             scalar1=self.scl[:, 11:12], scalar2=None,
                             op0=ALU.mult).inc(s4v)

        x1 = self.tf(0, 16384)
        hq = self.tb(16384, 16384)
        w2qT = self.tb(24576, 8192)
        P4t = contextlib.ExitStack()
        w2cb = [self.tf(28672 + i * 1024, 1024) for i in range(2)]
        hcb = [self.tf(30720 + i * 2048, 2048) for i in range(2)]
        w2qn = [self.tb(34816 + i * 512, 1024) for i in range(2)]
        s_w2ld = [kb.sem("s_w2ld0"), kb.sem("s_w2ld1")]
        s_hld = [kb.sem("s_hld0"), kb.sem("s_hld1")]

        def load_hc(hcx):
            buf = hcx % 2
            if hcx >= 2:
                gp.wait(s4s, self._w2free[hcx - 2])
                sy.wait(s4s, self._hfree[hcx - 2])
            for i in range(DC):
                gp.dma_start(
                    out=w2cb[buf][:, i * 128:(i + 1) * 128],
                    in_=self.din["w2"][i * 128:(i + 1) * 128,
                                       hcx * 128: hcx * 128 + 128])\
                    .inc(s_w2ld[buf], 16)
            sy.dma_start(out=hcb[buf][:],
                         in_=self.dscr["h_spill"][hcx * 128:(hcx + 1) * 128, :])\
                .inc(s_hld[buf], 16)

        self._w2free = {}
        self._hfree = {}
        snap_x1ev = {}
        snap_mmg = {}
        ve.wait(s3s)
        ve.wait(s3p)
        sc.wait(s3p)
        sc.wait(s3v)
        te.wait(s3s)
        gp.wait(s3v)
        gp.wait(s3s)
        sy.wait(s3v)
        for s_ in self.s_hsp_m:
            ve.wait(s_)
            sc.wait(s_)
            sy.wait(s_)
        load_hc(0)
        snap_s1h = {}
        snap_s2h = {}
        snap_s1w = {}
        snap_s2w = {}
        snap_tgw = {}
        snap_evw_sc = {}
        snap_evw_ve = {}
        for hg in range(4):
            for c8 in range(8):
                hcx = hg * 8 + c8
                buf = hcx % 2
                if hcx + 1 < 32:
                    load_hc(hcx + 1)
                ve.wait(s_hld[buf])
                ve.wait(s4v)
                ve.tensor_scalar(out=hcb[buf][:], in0=hcb[buf][:],
                                 scalar1=self.scl[:, 10:11], scalar2=MAGIC,
                                 op0=ALU.mult, op1=ALU.add).inc(s4v)
                snap_s1h[hcx] = s4v.n
                sc.wait(s4v, snap_s1h[hcx])
                if hg >= 1 and c8 == 0:
                    sc.wait(s4p, snap_mmg[hg - 1])
                sc.wait(s4s)
                sc.activation(out=hq[:, c8 * 2048:(c8 + 1) * 2048],
                              in_=hcb[buf][:], func=AF.Identity,
                              bias=self.scl[:, 15:16]).inc(s4s)
                snap_s2h[hcx] = s4s.n
                self._hfree[hcx] = s4s.n
                ve.wait(s_w2ld[buf])
                ve.wait(s4v)
                ve.tensor_scalar(out=w2cb[buf][:], in0=w2cb[buf][:],
                                 scalar1=self.scl[:, 5:6], scalar2=MAGIC,
                                 op0=ALU.mult, op1=ALU.add).inc(s4v)
                snap_s1w[hcx] = s4v.n
                sc.wait(s4v, snap_s1w[hcx])
                if hcx >= 2:
                    sc.wait(s4p, snap_tgw[hcx - 2])
                sc.wait(s4s)
                sc.activation(out=w2qn[buf][:], in_=w2cb[buf][:],
                              func=AF.Identity, bias=self.scl[:, 15:16]).inc(s4s)
                snap_s2w[hcx] = s4s.n
                self._w2free[hcx] = s4s.n
                te.wait(s4s, snap_s2w[hcx])
                if hcx >= 1:
                    te.wait(s4s, snap_evw_sc[hcx - 1])
                te.wait(s4p)
                for i in range(DC):
                    te.transpose(self.ptb[:, i * 128:(i + 1) * 128],
                                 w2qn[buf][:, i * 128:(i + 1) * 128],
                                 self.IdB[:]).inc(s4p)
                snap_tgw[hcx] = s4p.n
                sc.wait(s4p, snap_tgw[hcx])
                sc.wait(s4s)
                sc.activation(out=w2qT[:, c8 * 1024:(c8 + 1) * 1024],
                              in_=self.ptb[:], func=AF.Identity).inc(s4s)
                snap_evw_sc[hcx] = s4s.n
            te.wait(s4s, snap_evw_sc[hg * 8 + 7])
            for tt in range(16):
                for nb in range(2):
                    idx = tt * 2 + nb
                    ps = self.pS[idx % 2]
                    gidx = hg * 32 + idx
                    if gidx >= 2:
                        te.wait(s4v, snap_x1ev.get(gidx - 2, 0))
                    lastm = None
                    for c8 in range(8):
                        lastm = te.matmul(
                            ps[:, 0:512],
                            hq[:, c8 * 2048 + tt * 128: c8 * 2048 + tt * 128 + 128],
                            w2qT[:, c8 * 1024 + nb * 512: c8 * 1024 + nb * 512 + 512],
                            start=(c8 == 0), stop=(c8 == 7))
                    lastm.inc(s4p)
                    ve.wait(s4p)
                    dst = x1[:, tt * 1024 + nb * 512: tt * 1024 + nb * 512 + 512]
                    if hg == 0:
                        ve.tensor_copy(out=dst, in_=ps[:, 0:512]).inc(s4v)
                    else:
                        ve.scalar_tensor_tensor(out=dst, in0=ps[:, 0:512],
                                                scalar=1.0, in1=dst,
                                                op0=ALU.mult, op1=ALU.add)\
                            .inc(s4v)
                    snap_x1ev[gidx] = s4v.n
            snap_mmg[hg] = s4p.n
        P4t.close()

        P4f = contextlib.ExitStack()
        xf = [kb.sbuf(P4f, f"xf{i}", [128, 1024], F32) for i in range(2)]
        af = [kb.sbuf(P4f, f"af{i}", [128, 1024], BF16) for i in range(2)]
        s_xf = [kb.sem("s_xf0"), kb.sem("s_xf1")]
        s_af = [kb.sem("s_af0"), kb.sem("s_af1")]
        snap_fin = {}
        for tt in range(NT):
            buf = tt % 2
            if tt >= 2:
                sy.wait(s4v, snap_fin[tt - 2])
            sy.dma_start(out=xf[buf][:],
                         in_=self.din["x_loc"][tt * 128:(tt + 1) * 128, :])\
                .inc(s_xf[buf], 16)
            sy.dma_start(out=af[buf][:],
                         in_=self.dscr["attn0_spill"][tt * 128:(tt + 1) * 128, :])\
                .inc(s_af[buf], 16)
            dst = x1[:, tt * 1024:(tt + 1) * 1024]
            ve.wait(s4v)
            ve.tensor_scalar(out=dst, in0=dst, scalar1=self.scl[:, 11:12],
                             scalar2=None, op0=ALU.mult).inc(s4v)
            ve.wait(s4v)
            ve.tensor_tensor(out=dst, in0=dst, in1=B2S[:], op=ALU.add).inc(s4v)
            ve.wait(s_af[buf])
            ve.wait(s4v)
            ve.tensor_tensor(out=dst, in0=dst, in1=af[buf][:], op=ALU.add)\
                .inc(s4v)
            ve.wait(s_xf[buf])
            ve.wait(s4v)
            ve.tensor_tensor(out=dst, in0=dst, in1=xf[buf][:], op=ALU.add)\
                .inc(s4v)
            snap_fin[tt] = s4v.n
        P4f.close()
        self.x1 = x1
        self.s_p4 = (s4p, s4s, s4v)
        self.s_p4_dma = (s_xf[0], s_xf[1], s_af[0], s_af[1])

    def p5(self):
        kb = self.kb
        ve, sc, te, gp, sy = (kb.vector, kb.scalar, kb.tensor, kb.gpsimd,
                              kb.sync)
        s4p, s4s, s4v = self.s_p4
        s5v = kb.sem("s_p5v")
        s5s = kb.sem("s_p5s")
        s5p = kb.sem("s_p5p")
        s_x1sp = kb.sem("s_x1sp")
        s_xn1sp = kb.sem("s_xn1sp")

        x1 = self.x1
        xn1_tm = self.tb(16384, 16384)
        xn1_fm = self.tb(24576, 16384)
        normed = self.tb(32768, 16384)
        self.xn1_fm_off = 24576
        self.normed_off = 32768

        P5t = contextlib.ExitStack()
        nt5 = kb.sbuf(P5t, "nt5", [128, D], BF16)
        sq5 = kb.sbuf(P5t, "sq5", [128, D], F32)

        sc.wait(s4v)
        sc.wait(s4p)
        ve.wait(s4v)
        ve.wait(s4p)
        te.wait(s4v)
        sy.wait(s4v)

        def ln_stats(src_ap_f32orbf, cb, is_f32):
            sc.wait(s5v)
            sc.wait(s5s)
            sc.activation(out=sq5[:], in_=src_ap_f32orbf, func=AF.Square,
                          accum_out=self.stats[:, cb + 1:cb + 2]).inc(s5s)
            ve.wait(s5v)
            ve.wait(s5s)
            ve.reduce_sum(out=self.stats[:, cb + 0:cb + 1], in_=src_ap_f32orbf,
                          axis=AX.X).inc(s5v)
            ve.wait(s5v)
            ve.tensor_scalar(out=self.stats[:, cb + 2:cb + 3],
                             in0=self.stats[:, cb + 0:cb + 1],
                             scalar1=1.0 / D, scalar2=None, op0=ALU.mult)\
                .inc(s5v)
            ve.wait(s5v)
            ve.tensor_tensor(out=self.stats[:, cb + 3:cb + 4],
                             in0=self.stats[:, cb + 2:cb + 3],
                             in1=self.stats[:, cb + 2:cb + 3], op=ALU.mult)\
                .inc(s5v)
            ve.wait(s5v)
            ve.tensor_scalar(out=self.stats[:, cb + 4:cb + 5],
                             in0=self.stats[:, cb + 3:cb + 4],
                             scalar1=-1.0, scalar2=LN_EPS, op0=ALU.mult,
                             op1=ALU.add).inc(s5v)
            sc.wait(s5s)
            sc.wait(s5v)
            sc.activation(out=self.stats[:, cb + 5:cb + 6],
                          in_=self.stats[:, cb + 1:cb + 2], func=AF.Ln,
                          scale=1.0 / D, bias=self.stats[:, cb + 4:cb + 5])\
                .inc(s5s)
            sc.wait(s5s)
            sc.activation(out=self.stats[:, cb + 6:cb + 7],
                          in_=self.stats[:, cb + 5:cb + 6], func=AF.Exp,
                          scale=-0.5).inc(s5s)
            ve.wait(s5s)
            ve.scalar_tensor_tensor(out=self.stats[:, cb + 7:cb + 8],
                                    in0=self.stats[:, cb + 2:cb + 3],
                                    scalar=-1.0,
                                    in1=self.stats[:, cb + 6:cb + 7],
                                    op0=ALU.mult, op1=ALU.mult).inc(s5v)

        snap_tr5 = 0
        for t in range(NT):
            cb = (t % 2) * 8
            cb2 = 16 + (t % 2) * 8
            x1t = x1[:, t * 1024:(t + 1) * 1024]
            xn1t = xn1_tm[:, t * 1024:(t + 1) * 1024]
            ln_stats(x1t, cb, True)
            sc.wait(s5v)
            sc.activation(out=xn1t, in_=x1t, func=AF.Identity,
                          scale=self.stats[:, cb + 6:cb + 7],
                          bias=self.stats[:, cb + 7:cb + 8]).inc(s5s)
            sy.wait(s5s)
            sy.dma_start(out=self.dscr["x1_spill"][t * 128:(t + 1) * 128, :],
                         in_=x1t).inc(s_x1sp, 16)
            sy.dma_start(out=self.dscr["xn1_spill"][t * 128:(t + 1) * 128, :],
                         in_=xn1t).inc(s_x1sp, 16)
            ln_stats(xn1t, cb2, False)
            sc.wait(s5v)
            sc.wait(s5s)
            if t >= 1:
                sc.wait(s5p, snap_nt5)
            sc.activation(out=nt5[:], in_=xn1t, func=AF.Identity,
                          scale=self.stats[:, cb2 + 6:cb2 + 7],
                          bias=self.stats[:, cb2 + 7:cb2 + 8]).inc(s5s)
            te.wait(s5s)
            if snap_tr5:
                te.wait(s5v, snap_tr5)
            for d in range(DC):
                te.transpose(self.ptb[:, d * 128:(d + 1) * 128],
                             xn1t[:, d * 128:(d + 1) * 128], self.IdB[:])\
                    .inc(s5p)
            ve.wait(s5p)
            for d in range(DC):
                ve.tensor_copy(
                    out=xn1_fm[:, d * TOK + t * 128: d * TOK + t * 128 + 128],
                    in_=self.ptb[:, d * 128:(d + 1) * 128]).inc(s5v)
            te.wait(s5v)
            for d in range(DC):
                te.transpose(self.ptb[:, d * 128:(d + 1) * 128],
                             nt5[:, d * 128:(d + 1) * 128], self.IdB[:])\
                    .inc(s5p)
            snap_nt5 = s5p.n
            ve.wait(s5p)
            for d in range(DC):
                ve.tensor_scalar(
                    out=normed[:, d * TOK + t * 128: d * TOK + t * 128 + 128],
                    in0=self.ptb[:, d * 128:(d + 1) * 128],
                    scalar1=self.lnw_cols[:, d:d + 1],
                    scalar2=self.lnb_cols[:, d:d + 1],
                    op0=ALU.mult, op1=ALU.add).inc(s5v)
            snap_tr5 = s5v.n

        sy.wait(s5v)
        sy.dma_start(out=self.dscr["xn1fm_spill"], in_=xn1_fm).inc(s_xn1sp, 16)

        wg_sb = kb.sbuf(self.longs, "wg_sb", [128, 64], F32, side="left")
        s_wg = kb.sem("s_wg")
        for d in range(DC):
            sy.dma_start(out=wg_sb[:, d * 8:(d + 1) * 8],
                         in_=self.din["Wg"][d * 128:(d + 1) * 128, :])\
                .inc(s_wg, 16)
        wgb_sb = kb.sbuf(self.longs, "wgb_sb", [128, 64], BF16, side="left")
        ve.wait(s_wg)
        ve.wait(s5v)
        ve.tensor_copy(out=wgb_sb[:], in_=wg_sb[:]).inc(s5v)
        glog = kb.sbuf(self.longs, "glog", [128, 128], F32, side="left")
        gext = kb.sbuf(self.longs, "gext", [128, 128], F32, side="left")
        zg = kb.sbuf(self.longs, "zg", [128, 48], F32, side="left")
        self.we_mat = kb.sbuf(self.longs, "we_mat", [128, 128], F32,
                              side="left")
        self.we_fm = kb.sbuf(self.longs, "we_fm", [8, 2048], BF16, side="left")
        te.wait(s5v)
        for t in range(NT):
            pvx = self.pav[t % 2]
            if t >= 2:
                te.wait(s5v, snap_gev.get(t - 2, 0))
            lastm = None
            for d in range(DC):
                lastm = te.matmul(pvx[:, 0:8],
                                  xn1_fm[:, d * TOK + t * 128:
                                         d * TOK + t * 128 + 128],
                                  wgb_sb[:, d * 8:(d + 1) * 8],
                                  start=(d == 0), stop=False)
            lastm = te.matmul(pvx[:, 0:8], self.ones_col[:],
                              self.bg_rb[:], start=False, stop=True)
            lastm.inc(s5p)
            ve.wait(s5p)
            ve.tensor_copy(out=glog[:, t * 8:(t + 1) * 8], in_=pvx[:, 0:8])\
                .inc(s5v)
            if t == 0:
                snap_gev = {}
            snap_gev[t] = s5v.n
        sc.wait(s5v)
        sc.activation(out=gext[:], in_=glog[:], func=AF.Exp).inc(s5s)
        ve.wait(s5s)
        g3 = gext.rearrange("p (t e) -> p t e", e=8)
        ve.reduce_sum(out=zg[:, 0:16], in_=g3, axis=AX.X).inc(s5v)
        ve.wait(s5v)
        ve.reduce_max(out=zg[:, 16:32], in_=g3, axis=AX.X).inc(s5v)
        ve.wait(s5v)
        ve.reciprocal(out=zg[:, 32:48], in_=zg[:, 0:16]).inc(s5v)
        gt2 = kb.sbuf(P5t, "gt2", [128, 16], F32)
        for t in range(NT):
            sl = slice(t * 8, (t + 1) * 8)
            ve.wait(s5v)
            ve.tensor_single_scalar(out=gt2[:, 0:8], in_=gext[:, sl],
                                    scalar=zg[:, 16 + t:17 + t],
                                    op=ALU.is_equal).inc(s5v)
            ve.wait(s5v)
            ve.scalar_tensor_tensor(out=gt2[:, 0:8], in0=gt2[:, 0:8],
                                    scalar=-1.0e30, in1=gext[:, sl],
                                    op0=ALU.mult, op1=ALU.add).inc(s5v)
            ve.wait(s5v)
            ve.reduce_max(out=gt2[:, 8:9], in_=gt2[:, 0:8], axis=AX.X)\
                .inc(s5v)
            ve.wait(s5v)
            ve.tensor_single_scalar(out=gt2[:, 0:8], in_=gext[:, sl],
                                    scalar=gt2[:, 8:9], op=ALU.is_ge)\
                .inc(s5v)
            ve.wait(s5v)
            ve.tensor_tensor(out=gt2[:, 0:8], in0=gt2[:, 0:8],
                             in1=gext[:, sl], op=ALU.mult).inc(s5v)
            ve.wait(s5v)
            ve.tensor_scalar(out=self.we_mat[:, sl], in0=gt2[:, 0:8],
                             scalar1=zg[:, 32 + t:33 + t], scalar2=None,
                             op0=ALU.mult).inc(s5v)
        web = kb.sbuf(P5t, "web", [128, 128], BF16)
        ve.wait(s5v)
        ve.tensor_copy(out=web[:], in_=self.we_mat[:]).inc(s5v)
        te.wait(s5v)
        for t in range(NT):
            pvx = self.pav[t % 2]
            if t >= 2:
                te.wait(s5v, snap_wev.get(t - 2, 0))
            te.transpose(pvx.bitcast(BF16)[0:8, 0:128],
                         web[:, t * 8:(t + 1) * 8], self.IdB[:]).inc(s5p)
            ve.wait(s5p)
            ve.tensor_copy(out=self.we_fm[:, t * 128:(t + 1) * 128],
                           in_=pvx.bitcast(BF16)[0:8, 0:128]).inc(s5v)
            if t == 0:
                snap_wev = {}
            snap_wev[t] = s5v.n
        P5t.close()
        self.s_p5 = (s5p, s5s, s5v, s_x1sp, s_xn1sp)

    def p6(self):
        kb = self.kb
        ve, sc, te, gp, sy = (kb.vector, kb.scalar, kb.tensor, kb.gpsimd,
                              kb.sync)
        s5p, s5s, s5v, s_x1sp, s_xn1sp = self.s_p5
        s6v = kb.sem("s_p6v")
        s6s = kb.sem("s_p6s")
        s6p = kb.sem("s_p6p")
        s_whld = kb.sem("s_whld")
        s_vsp = kb.sem("s_vsp")

        Whv = self.tb(0, 16384)
        Whg = self.tb(8192, 16384)
        v_g = self.tb(16384, 4096)
        qq_fm = self.tb(18432, 2048)
        lq_fm = self.tb(19456, 2048)
        qk_fm = self.tb(20480, 2048)
        link = self.tb(21504, 2048)
        linkvT = self.tf(22528, 2048)
        qk_tm = self.tb(40960, 256)
        qk4 = self.tb(41088, 256)

        self.L1 = dict(Whv=Whv, Whg=Whg, v_g=v_g, qq=qq_fm, lq=lq_fm,
                       qk=qk_fm, link=link, linkvT=linkvT)

        gp.wait(s_x1sp)
        gp.wait(s5s)
        gp.wait(s5v)
        for s_ in self.s_p4_dma:
            gp.wait(s_)
        wqk_b = kb.sbuf(self.longs, "wqk_b", [128, 1024], BF16, side="left")
        s_wqk = kb.sem("s_wqk")
        for d in range(DC):
            gp.dma_start(out=wqk_b[:, d * 128:(d + 1) * 128],
                         in_=self.din["Wqk"][d * 128:(d + 1) * 128, :])\
                .inc(s_wqk, 16)
        for d in range(DC):
            gp.dma_start(out=Whv[:, d * 2048:(d + 1) * 2048],
                         in_=self.din["Wh"][d * 128:(d + 1) * 128, 0:2048])\
                .inc(s_whld, 16)
            gp.dma_start(out=Whg[:, d * 2048:(d + 1) * 2048],
                         in_=self.din["Wh"][d * 128:(d + 1) * 128, 2048:4096])\
                .inc(s_whld, 16)

        def nslice(d, tok0, w):
            return self.tb(self.normed_off, 16384)[:, d * TOK + tok0:
                                                   d * TOK + tok0 + w]

        te.wait(s5v)
        te.wait(s5s)
        te.wait(s_wqk)
        if self.upto == "P6a":
            return
        snap_qkev = 0
        snap_vev = {}
        snap_lkv = {}
        ngroups = 1 if self.upto == "P6b" else NG
        for g in range(ngroups):
            t0 = g * G
            for half in range(2):
                pvx = self.pav[half]
                if g > 0 or half > 0:
                    te.wait(s6v, snap_qkev)
                lastm = None
                for d in range(DC):
                    lastm = te.matmul(pvx[:, 0:128],
                                      nslice(d, t0 + half * 128, 128),
                                      wqk_b[:, d * 128:(d + 1) * 128],
                                      start=(d == 0), stop=False)
                lastm = te.matmul(pvx[:, 0:128], self.ones_col[:],
                                  self.bqk_rb[:], start=False, stop=True)
                lastm.inc(s6p)
                sc.wait(s6p)
                if g > 0 or half > 0:
                    sc.wait(s6v, snap_qkev)
                sc.activation(out=qk_tm[:, half * 128:(half + 1) * 128],
                              in_=pvx[:, 0:128], func=AF.Silu).inc(s6s)
            for vx, dst, is_fm in ((0, qq_fm, True), (1, lq_fm, True),
                                   (2, qk_fm, True), (3, link, False)):
                for half in range(2):
                    ve.wait(s6s)
                    ve.wait(s6v)
                    ve.tensor_tensor(out=qk4[:, half * 128:(half + 1) * 128],
                                     in0=qk_tm[:, half * 128:(half + 1) * 128],
                                     in1=self.qkwb_t[:, vx * 128:(vx + 1) * 128],
                                     op=ALU.mult).inc(s6v)
                    ve.wait(s6v)
                    ve.tensor_tensor(out=qk4[:, half * 128:(half + 1) * 128],
                                     in0=qk4[:, half * 128:(half + 1) * 128],
                                     in1=self.qkwb_t[:, (4 + vx) * 128:
                                                     (5 + vx) * 128],
                                     op=ALU.add).inc(s6v)
                if not is_fm:
                    ve.wait(s6v)
                    ve.tensor_copy(out=link[:, g * 256:(g + 1) * 256],
                                   in_=qk4[:]).inc(s6v)
                else:
                    te.wait(s6v)
                    for half in range(2):
                        te.transpose(
                            self.ptb[:, half * 128:half * 128 + 128],
                            qk4[:, half * 128:(half + 1) * 128],
                            self.IdB[:]).inc(s6p)
                    ve.wait(s6p)
                    for half in range(2):
                        ve.tensor_copy(
                            out=dst[:, g * 256 + half * 128:
                                    g * 256 + half * 128 + 128],
                            in_=self.ptb[:, half * 128:half * 128 + 128])\
                            .inc(s6v)
            snap_qkev = s6v.n
            te.wait(s_whld)
            te.wait(s6v)
            for half in range(2):
                for nb in range(4):
                    vidx = half * 4 + nb
                    ps = self.pS[vidx % 2]
                    if vidx >= 2:
                        te.wait(s6s, snap_vev[vidx - 2])
                    lastm = None
                    for d in range(DC):
                        lastm = te.matmul(
                            ps[:, 0:512],
                            nslice(d, t0 + half * 128, 128),
                            Whv[:, d * 2048 + nb * 512: d * 2048 + nb * 512 + 512],
                            start=(d == 0), stop=False)
                    lastm = te.matmul(ps[:, 0:512], self.ones_col[:],
                                      self.bh_rb[0:1, nb * 512:(nb + 1) * 512],
                                      start=False, stop=True)
                    lastm.inc(s6p)
                    sc.wait(s6p)
                    sc.activation(out=v_g[:, half * 2048 + nb * 512:
                                          half * 2048 + nb * 512 + 512],
                                  in_=ps[:, 0:512], func=AF.Silu).inc(s6s)
                    snap_vev[vidx] = s6s.n
            sy.wait(s6s)
            for half in range(2):
                sy.dma_start(
                    out=self.dscr["v_spill"][t0 + half * 128:
                                             t0 + half * 128 + 128, :],
                    in_=v_g[:, half * 2048:(half + 1) * 2048]).inc(s_vsp, 16)
            snap_vspill = s_vsp.n
            te.wait(s6v)
            te.wait(s6s)
            for nb in range(4):
                lidx = g * 4 + nb
                ps = self.pS[nb % 2]
                if nb >= 2:
                    te.wait(s6v, snap_lkv[lidx - 2])
                lastm = None
                for half in range(2):
                    lastm = te.matmul(
                        ps[:, 0:512],
                        link[:, g * 256 + half * 128: g * 256 + half * 128 + 128],
                        v_g[:, half * 2048 + nb * 512:
                            half * 2048 + nb * 512 + 512],
                        start=(half == 0), stop=(half == 1))
                lastm.inc(s6p)
                ve.wait(s6p)
                dstv = linkvT[:, nb * 512:(nb + 1) * 512]
                if g == 0:
                    ve.tensor_scalar(out=dstv, in0=ps[:, 0:512],
                                     scalar1=1.0 / G, scalar2=None,
                                     op0=ALU.mult).inc(s6v)
                else:
                    ve.scalar_tensor_tensor(out=dstv, in0=ps[:, 0:512],
                                            scalar=1.0 / G, in1=dstv,
                                            op0=ALU.mult, op1=ALU.add)\
                        .inc(s6v)
                snap_lkv[lidx] = s6v.n
            sc.wait(s_vsp, snap_vspill)
            sc.wait(s6v)

        if self.upto in ("P6b",):
            return
        s_cc3 = kb.sem("s_cc3")
        s_gp3 = kb.sem("s_gp3")
        gp.wait(s6v)
        gp.dma_start(out=self.dscr["pair_in"], in_=linkvT).inc(s_gp3, 16)
        gp.wait(s_gp3)
        if self.no_cc:
            gp.dma_start(out=self.dscr["pair_out"][0:128, :],
                         in_=self.dscr["pair_in"]).inc(s_cc3, 16)
            gp.dma_start(out=self.dscr["pair_out"][128:256, :],
                         in_=self.dscr["pair_in"]).inc(s_cc3, 16)
        else:
            gp.collective_compute(
                "AllGather", ALU.bypass,
                replica_groups=[[2 * i, 2 * i + 1] for i in range(4)],
                ins=[self.dscr["pair_in"].opt()],
                outs=[self.dscr["pair_out"].opt()],
            ).inc(s_cc3, 1)
        gp.wait(s_cc3)
        gp.dma_start(out=linkvT, in_=self.dscr["pair_out"][0:128, :])\
            .inc(s_gp3, 16)
        te.wait(kb.sem("s_pmld"))
        te.wait(s6p)
        r = te.matmul(self.pu0[:, 12:13], self.ones_f[:],
                      self.pmask_sb[0:1, 0:1], start=True, stop=True).inc(s6p)
        ve.wait(s6p)
        ve.wait(s6v)
        ve.tensor_copy(out=self.scl[:, 12:13], in_=self.pu0[:, 12:13])\
            .inc(s6v)
        ve.wait(s_gp3)
        ve.wait(s6v)
        ve.tensor_scalar(out=linkvT, in0=linkvT,
                         scalar1=self.scl[:, 12:13], scalar2=None,
                         op0=ALU.mult).inc(s6v)
        self.s_p6 = (s6p, s6s, s6v, s_vsp)

    def p7(self):
        kb = self.kb
        ve, sc, te, gp, sy = (kb.vector, kb.scalar, kb.tensor, kb.gpsimd,
                              kb.sync)
        s6p, s6s, s6v, s_vsp = self.s_p6
        s7v = kb.sem("s_p7v")
        s7s = kb.sem("s_p7s")
        s7p = kb.sem("s_p7p")
        s_wo = kb.sem("s_wo")
        s_vrl = kb.sem("s_vrl")
        s_xnr = kb.sem("s_xnr")
        s_a1 = kb.sem("s_a1")

        L = self.L1
        v_g = L["v_g"]
        prefix = L["linkvT"]
        Wout_b = self.tb(0, 16384)
        gate = self.tb(24576 * 2 // 2, 0)
        gate = self.tb(24576, 4096)
        houtT = self.tb(26624, 4096)
        a1g = self.tb(28672, 2048)
        attnT = self.tb(29696, 512)
        prefixb = self.tb(30720, 2048)
        xn1g = self.tb(31744, 2048)
        sim_s = self.tf(40960, 512)

        sxsp = self.s_p5[4]
        for eng in (sc, ve, te, sy, gp):
            eng.wait(sxsp)
        gp.wait(s6p)
        gp.wait(s6s)
        for m in range(16):
            gp.dma_start(out=Wout_b[:, m * 1024:(m + 1) * 1024],
                         in_=self.din["Wout"][m * 128:(m + 1) * 128, :])\
                .inc(s_wo, 16)

        def nslice(d, tok0, w):
            return self.tb(self.normed_off, 16384)[:, d * TOK + tok0:
                                                   d * TOK + tok0 + w]

        snap = {}
        for g in range(NG):
            t0 = g * G
            sy.wait(s6s)
            if g == 0:
                sy.wait(s_vsp)
                sy.wait(s6p)
                sy.wait(self.s_p5[4])
            if g >= 1:
                sy.wait(s7p, snap[("vfree", g - 1)])
                sy.wait(s7v, snap[("xnfree", g - 1)])
            for half in range(2):
                sy.dma_start(out=v_g[:, half * 2048:(half + 1) * 2048],
                             in_=self.dscr["v_spill"][t0 + half * 128:
                                                      t0 + half * 128 + 128, :])\
                    .inc(s_vrl, 16)
                sy.dma_start(out=xn1g[:, half * 1024:(half + 1) * 1024],
                             in_=self.dscr["xn1_spill"][t0 + half * 128:
                                                        t0 + half * 128 + 128, :])\
                    .inc(s_xnr, 16)
            ve.wait(s6v)
            ve.wait(s7v)
            if g >= 1:
                ve.wait(s7p, snap[("pfb_free", g - 1)])
            ve.tensor_copy(out=prefixb[:], in_=prefix[:]).inc(s7v)
            te.wait(s_wo)
            te.wait(s7v)
            te.wait(s6v)
            for m in range(16):
                pvx = self.pS[m % 2]
                if g * 16 + m >= 2:
                    te.wait(s7s, snap.get(("gev", g * 16 + m - 2), 0))
                lastm = None
                for d in range(DC):
                    lastm = te.matmul(
                        pvx[:, 0:256],
                        L["Whg"][:, d * 2048 + m * 128: d * 2048 + m * 128 + 128],
                        nslice(d, t0, 256),
                        start=(d == 0), stop=(d == DC - 1))
                lastm.inc(s7p)
                sc.wait(s7p)
                sc.activation(out=gate[:, m * 256:(m + 1) * 256],
                              in_=pvx[:, 0:256], func=AF.Silu,
                              bias=self.bh_cols[:, 16 + m:17 + m]).inc(s7s)
                snap[("gev", g * 16 + m)] = s7s.n
            te.wait(s7s)
            if g >= 1:
                te.wait(s7v, snap[("qo_done", g - 1)])
            for half in range(2):
                pvx = self.pav[half]
                r = te.matmul(pvx[:, 0:256],
                              L["qk"][:, g * 256 + half * 128:
                                      g * 256 + half * 128 + 128],
                              L["qq"][:, g * 256:(g + 1) * 256],
                              start=True, stop=True).inc(s7p)
            ve.wait(s7p)
            for half in range(2):
                ve.scalar_tensor_tensor(
                    out=sim_s[:, half * 256:(half + 1) * 256],
                    in0=self.pav[half][:, 0:256], scalar=1.0 / G,
                    in1=self.BiasT[:, half * 256:(half + 1) * 256],
                    op0=ALU.mult, op1=ALU.add).inc(s7v)
            sc.wait(s7v)
            sc.activation(out=sim_s[:], in_=sim_s[:], func=AF.Relu).inc(s7s)
            ve.wait(s7s)
            ve.tensor_tensor(out=attnT[:], in0=sim_s[:], in1=sim_s[:],
                             op=ALU.mult).inc(s7v)
            te.wait(s7v)
            ve.wait(s_vrl)
            te.wait(s_vrl)
            for m in range(16):
                pvx = self.pav[m % 2]
                if m >= 2:
                    te.wait(s7v, snap[("qev", m - 2)])
                lastm = te.matmul(pvx[:, 0:256],
                                  v_g[:, 0 * 2048 + m * 128: 0 * 2048 + m * 128 + 128],
                                  attnT[:, 0:256], start=True, stop=False)
                lastm = te.matmul(pvx[:, 0:256],
                                  v_g[:, 1 * 2048 + m * 128: 1 * 2048 + m * 128 + 128],
                                  attnT[:, 256:512], start=False, stop=False)
                lastm = te.matmul(pvx[:, 0:256],
                                  prefixb[:, m * 128:(m + 1) * 128],
                                  L["lq"][:, g * 256:(g + 1) * 256],
                                  start=False, stop=True)
                lastm.inc(s7p)
                ve.wait(s7p)
                ve.wait(s7s)
                ve.tensor_tensor(out=houtT[:, m * 256:(m + 1) * 256],
                                 in0=pvx[:, 0:256],
                                 in1=gate[:, m * 256:(m + 1) * 256],
                                 op=ALU.mult).inc(s7v)
                snap[("qev", m)] = s7v.n
            snap[("qo_done", g)] = s7v.n
            te.wait(s7v)
            for half in range(2):
                for nb in range(2):
                    widx = half * 2 + nb
                    ps = self.pS[widx % 2]
                    if widx >= 2:
                        te.wait(s7v, snap[("wev", widx - 2)])
                    lastm = None
                    for m in range(16):
                        lastm = te.matmul(
                            ps[:, 0:512],
                            houtT[:, m * 256 + half * 128: m * 256 + half * 128 + 128],
                            Wout_b[:, m * 1024 + nb * 512: m * 1024 + nb * 512 + 512],
                            start=(m == 0), stop=False)
                    lastm = te.matmul(ps[:, 0:512], self.ones_col[:],
                                      self.bout_rb[0:1, nb * 512:(nb + 1) * 512],
                                      start=False, stop=True)
                    lastm.inc(s7p)
                    ve.wait(s7p)
                    ve.wait(s_xnr)
                    ve.tensor_tensor(
                        out=a1g[:, half * 1024 + nb * 512:
                                half * 1024 + nb * 512 + 512],
                        in0=ps[:, 0:512],
                        in1=xn1g[:, half * 1024 + nb * 512:
                                 half * 1024 + nb * 512 + 512],
                        op=ALU.add).inc(s7v)
                    snap[("wev", widx)] = s7v.n
            snap[("xnfree", g)] = s7v.n
            sy.wait(s7v)
            for half in range(2):
                sy.dma_start(
                    out=self.dscr["attn1_spill"][t0 + half * 128:
                                                 t0 + half * 128 + 128, :],
                    in_=a1g[:, half * 1024:(half + 1) * 1024]).inc(s_a1, 16)
            if True:
                snap[("a1", g)] = s_a1.n
            ve.wait(s_a1, snap[("a1", g)]) if False else None
            te.wait(s7v)
            for nb in range(4):
                ps = self.pS[nb % 2]
                if nb >= 2:
                    te.wait(s7v, snap[("lkv2", nb - 2)])
                lastm = None
                for half in range(2):
                    lastm = te.matmul(
                        ps[:, 0:512],
                        L["link"][:, g * 256 + half * 128:
                                  g * 256 + half * 128 + 128],
                        v_g[:, half * 2048 + nb * 512:
                            half * 2048 + nb * 512 + 512],
                        start=(half == 0), stop=(half == 1))
                lastm.inc(s7p)
                ve.wait(s7p)
                ve.scalar_tensor_tensor(
                    out=prefix[:, nb * 512:(nb + 1) * 512],
                    in0=ps[:, 0:512], scalar=1.0 / G,
                    in1=prefix[:, nb * 512:(nb + 1) * 512],
                    op0=ALU.mult, op1=ALU.add).inc(s7v)
                snap[("lkv2", nb)] = s7v.n
            snap[("vfree", g)] = s7p.n
            snap[("pfb_free", g)] = s7p.n
            ve.wait(s_a1, snap[("a1", g)])
        self.s_p7 = (s7p, s7s, s7v, s_a1)

    def p8(self):
        kb = self.kb
        ve, sc, te, gp, sy = (kb.vector, kb.scalar, kb.tensor, kb.gpsimd,
                              kb.sync)
        s7p, s7s, s7v, s_a1 = self.s_p7
        s8v = kb.sem("s_p8v")
        s8s = kb.sem("s_p8s")
        s8p = kb.sem("s_p8p")
        s_xnl = kb.sem("s_xnl")
        s_wm = [kb.sem("s_wm0"), kb.sem("s_wm1")]

        mixed = self.tf(0, 16384)
        xn1f = self.tb(16384, 16384)
        W1b = [self.tb(24576 + i * 2048, 4096) for i in range(2)]
        W3b = [self.tb(28672 + i * 2048, 4096) for i in range(2)]
        W2b = [self.tb(32768 + i * 2048, 4096) for i in range(2)]
        h1b = self.tb(36864, 4096)
        h3b = self.tb(38912, 4096)
        self.mixed = mixed

        for eng in (gp, sy, ve, sc, te):
            eng.wait(s7p)
            eng.wait(s7v)
            eng.wait(s7s)
        gp.wait(s_a1)
        gp.dma_start(out=xn1f[:], in_=self.dscr["xn1fm_spill"]).inc(s_xnl, 16)

        def load_ehb(e, hb, buf):
            for d in range(DC):
                gp.dma_start(
                    out=W1b[buf][:, d * 512:(d + 1) * 512],
                    in_=self.din["W1m"][e, d * 128:(d + 1) * 128,
                                        hb * 512:(hb + 1) * 512])\
                    .inc(s_wm[buf], 16)
                gp.dma_start(
                    out=W3b[buf][:, d * 512:(d + 1) * 512],
                    in_=self.din["W3m"][e, d * 128:(d + 1) * 128,
                                        hb * 512:(hb + 1) * 512])\
                    .inc(s_wm[buf], 16)
            for hc in range(4):
                gp.dma_start(
                    out=W2b[buf][:, hc * 1024:(hc + 1) * 1024],
                    in_=self.din["W2m"][e, hb * 512 + hc * 128:
                                        hb * 512 + hc * 128 + 128, :])\
                    .inc(s_wm[buf], 16)

        load_ehb(0, 0, 0)
        te.wait(s_xnl)
        snap = {}
        first = True
        step = 0
        for e in range(NE):
            for hb in range(4):
                buf = step % 2
                if step + 1 < 32:
                    ne, nhb = (e, hb + 1) if hb < 3 else (e + 1, 0)
                    if step >= 1:
                        gp.wait(s8p, snap[("wfree", step - 1)])
                    load_ehb(ne, nhb, (step + 1) % 2)
                te.wait(s_wm[buf])
                for th in range(2):
                    for which in ("h1", "h3"):
                        hsrc = W1b[buf] if which == "h1" else W3b[buf]
                        for m in range(4):
                            for nb2 in range(2):
                                hidx = m * 2 + nb2
                                ps = self.pS[hidx % 2]
                                if hidx >= 2:
                                    if which == "h1":
                                        te.wait(s8s, snap.get(("h1f", hidx - 2), 0))
                                    else:
                                        te.wait(s8v, snap.get(("h3f", hidx - 2), 0))
                                elif which == "h1":
                                    te.wait(s8v,
                                            snap.get(("mev", 14 + hidx), 0))
                                else:
                                    te.wait(s8s,
                                            snap.get(("h1f", 6 + hidx), 0))
                                lastm = None
                                for d in range(DC):
                                    lastm = te.matmul(
                                        ps[:, 0:512],
                                        hsrc[:, d * 512 + m * 128:
                                             d * 512 + m * 128 + 128],
                                        xn1f[:, d * 2048 + th * 1024 + nb2 * 512:
                                             d * 2048 + th * 1024 + nb2 * 512 + 512],
                                        start=(d == 0), stop=(d == DC - 1))
                                lastm.inc(s8p)
                                bcol = e * 16 + hb * 4 + m
                                if which == "h1":
                                    sc.wait(s8p)
                                    sc.activation(
                                        out=h1b[:, m * 1024 + nb2 * 512:
                                                m * 1024 + nb2 * 512 + 512],
                                        in_=ps[:, 0:512], func=AF.Identity,
                                        bias=self.b1m_cols[:, bcol:bcol + 1])\
                                        .inc(s8s)
                                    snap[("h1f", hidx)] = s8s.n
                                else:
                                    ve.wait(s8p)
                                    ve.tensor_scalar(
                                        out=h3b[:, m * 1024 + nb2 * 512:
                                                m * 1024 + nb2 * 512 + 512],
                                        in0=ps[:, 0:512],
                                        scalar1=self.b3m_cols[:, bcol:bcol + 1],
                                        scalar2=None, op0=ALU.add).inc(s8v)
                                    snap[("h3f", hidx)] = s8v.n
                                    if nb2 == 1:
                                        ve.wait(s8v)
                                        ve.wait(s8s, snap[("h1f", hidx)]
                                                if ("h1f", hidx) in snap else 0)
                                        ve.tensor_tensor(
                                            out=h1b[:, m * 1024:(m + 1) * 1024],
                                            in0=h1b[:, m * 1024:(m + 1) * 1024],
                                            in1=h3b[:, m * 1024:(m + 1) * 1024],
                                            op=ALU.mult).inc(s8v)
                                        sc.wait(s8v)
                                        sc.activation(
                                            out=h1b[:, m * 1024:(m + 1) * 1024],
                                            in_=h1b[:, m * 1024:(m + 1) * 1024],
                                            func=AF.Silu).inc(s8s)
                                        snap[("silu", m)] = s8s.n
                    for tloc in range(8):
                        tt = th * 8 + tloc
                        for nb in range(2):
                            widx = tloc * 2 + nb
                            ps = self.pS[widx % 2]
                            if widx >= 2:
                                te.wait(s8v, snap.get(("mev", widx - 2), 0))
                            else:
                                te.wait(s8v, snap.get(("h3f", 6 + widx), 0))
                            lastm = None
                            for m in range(4):
                                if widx == 0:
                                    te.wait(s8s, snap[("silu", m)])
                                lastm = te.matmul(
                                    ps[:, 0:512],
                                    h1b[:, m * 1024 + tloc * 128:
                                        m * 1024 + tloc * 128 + 128],
                                    W2b[buf][:, m * 1024 + nb * 512:
                                             m * 1024 + nb * 512 + 512],
                                    start=(m == 0), stop=(m == 3))
                            lastm.inc(s8p)
                            ve.wait(s8p)
                            dst = mixed[:, tt * 1024 + nb * 512:
                                        tt * 1024 + nb * 512 + 512]
                            wecol = self.we_mat[:, tt * 8 + e:tt * 8 + e + 1]
                            if first:
                                ve.tensor_scalar(out=dst, in0=ps[:, 0:512],
                                                 scalar1=wecol, scalar2=None,
                                                 op0=ALU.mult).inc(s8v)
                            else:
                                ve.scalar_tensor_tensor(
                                    out=dst, in0=ps[:, 0:512], scalar=wecol,
                                    in1=dst, op0=ALU.mult, op1=ALU.add)\
                                    .inc(s8v)
                            snap[("mev", widx)] = s8v.n
                    first = False
                snap[("wfree", step)] = s8p.n
                step += 1
        self.s_p8 = (s8p, s8s, s8v)

    def p9(self):
        kb = self.kb
        ve, sc, te, gp, sy = (kb.vector, kb.scalar, kb.tensor, kb.gpsimd,
                              kb.sync)
        s8p, s8s, s8v = self.s_p8
        s9v = kb.sem("s_p9v")
        s9s = kb.sem("s_p9s")
        s9p = kb.sem("s_p9p")
        s_wf = kb.sem("s_wf")
        s_x1r = [kb.sem("s_x1r0"), kb.sem("s_x1r1")]
        s_a1r = [kb.sem("s_a1r0"), kb.sem("s_a1r1")]
        s_out = [kb.sem("s_out0"), kb.sem("s_out1")]

        mixedb = self.tb(16384, 16384)
        mixed_fm = self.tb(24576, 16384)
        Wfb = self.tb(32768, 8192)
        P9t = contextlib.ExitStack()
        x1r = [self.tf(36864 + i * 1024, 1024) for i in range(2)]
        otile = [self.tf(38912 + i * 1024, 1024) for i in range(2)]
        a1r = [self.tb(16384 + i * 512, 1024) for i in range(2)]

        for eng in (gp, sy, ve, sc, te):
            eng.wait(s8p)
            eng.wait(s8v)
            eng.wait(s8s)
        for d in range(DC):
            gp.dma_start(out=Wfb[:, d * 1024:(d + 1) * 1024],
                         in_=self.din["Wf"][d * 128:(d + 1) * 128, :])\
                .inc(s_wf, 16)
        ve.wait(s8v)
        ve.tensor_copy(out=mixedb[:], in_=self.mixed[:]).inc(s9v)
        snap = {}
        for t in range(NT):
            te.wait(s9v)
            if t >= 1:
                te.wait(s9v, snap.get(("ev", t - 1), 0))
            for d in range(DC):
                te.transpose(self.ptb[:, d * 128:(d + 1) * 128],
                             mixedb[:, t * 1024 + d * 128:
                                    t * 1024 + d * 128 + 128],
                             self.IdB[:]).inc(s9p)
            ve.wait(s9p)
            for d in range(DC):
                ve.tensor_copy(
                    out=mixed_fm[:, d * 2048 + t * 128: d * 2048 + t * 128 + 128],
                    in_=self.ptb[:, d * 128:(d + 1) * 128]).inc(s9v)
            snap[("ev", t)] = s9v.n
        te.wait(s_wf)
        te.wait(s9v)
        sy.wait(s8v)
        sy.wait(s9p)
        sy.wait(s9v)
        for t in range(NT):
            buf = t % 2
            if t >= 2:
                sy.wait(s9v, snap[("ot", t - 2)])
            sy.dma_start(out=x1r[buf][:],
                         in_=self.dscr["x1_spill"][t * 128:(t + 1) * 128, :])\
                .inc(s_x1r[buf], 16)
            sy.dma_start(out=a1r[buf][:],
                         in_=self.dscr["attn1_spill"][t * 128:(t + 1) * 128, :])\
                .inc(s_a1r[buf], 16)
            if t >= 2:
                ve.wait(s_out[t % 2], snap[("od", t - 2)])
            for nb in range(2):
                pidx = t * 2 + nb
                ps = self.pS[pidx % 2]
                if pidx >= 2:
                    te.wait(s9v, snap.get(("fev", pidx - 2), 0))
                lastm = None
                for d in range(DC):
                    lastm = te.matmul(
                        ps[:, 0:512],
                        mixed_fm[:, d * 2048 + t * 128: d * 2048 + t * 128 + 128],
                        Wfb[:, d * 1024 + nb * 512: d * 1024 + nb * 512 + 512],
                        start=(d == 0), stop=False)
                lastm = te.matmul(
                    ps[:, 0:512], self.we_fm[:, t * 128:(t + 1) * 128],
                    self.b2m_b[:, nb * 512:(nb + 1) * 512],
                    start=False, stop=False)
                lastm = te.matmul(
                    ps[:, 0:512], self.ones_col[:],
                    self.bf_rb[0:1, nb * 512:(nb + 1) * 512],
                    start=False, stop=True)
                lastm.inc(s9p)
                ve.wait(s9p)
                ve.wait(s_x1r[buf])
                ve.tensor_tensor(out=otile[buf][:, nb * 512:(nb + 1) * 512],
                                 in0=ps[:, 0:512],
                                 in1=x1r[buf][:, nb * 512:(nb + 1) * 512],
                                 op=ALU.add).inc(s9v)
                ve.wait(s_a1r[buf])
                ve.wait(s9v)
                ve.tensor_tensor(out=otile[buf][:, nb * 512:(nb + 1) * 512],
                                 in0=otile[buf][:, nb * 512:(nb + 1) * 512],
                                 in1=a1r[buf][:, nb * 512:(nb + 1) * 512],
                                 op=ALU.add).inc(s9v)
                snap[("fev", pidx)] = s9v.n
            sy.wait(s9v)
            sy.dma_start(out=self.out[t * 128:(t + 1) * 128, :],
                         in_=otile[buf][:]).inc(s_out[t % 2], 16)
            snap[("ot", t)] = s9v.n
            snap[("od", t)] = s_out[t % 2].n
        P9t.close()

    def finish(self):
        for nm, sem in list(self.kb._sems.items()):
            if sem.n > 0:
                self.kb.gpsimd.wait(sem)
        self.kb.replay()
        return self.nc, self.es

    def dump(self, aps, waits=()):
        kb = self.kb
        s_fin = kb.sem("s_fin")
        for s in waits:
            kb.gpsimd.wait(kb.sem(s) if isinstance(s, str) else s)
        flat = self.out.rearrange("a b -> (a b)")
        off = 0
        for ap in aps:
            p, L = ap.shape
            kb.gpsimd.dma_start(
                out=flat[off:off + p * L].rearrange("(p l) -> p l", p=p),
                in_=ap).inc(s_fin, 16)
            off += p * L
        kb.gpsimd.wait(s_fin)

    def finish2(self):
        self.kb.replay()
        return self.nc, self.es


def build(upto=None, no_cc=False):
    pr = Prog(upto, no_cc=no_cc)
    pr.p0()
    pr.p1()
    kb = pr.kb
    if upto in ("P1", "P1s"):
        if upto == "P1":
            aps = [pr.ar_b16[:, d * N: d * N + 2048] for d in range(DC)]
        else:
            aps = [pr.stats[:, 48:51]]
        pr.dump(aps, waits=("s_p1ve", "s_p1sc"))
        return pr.finish()
    pr.p2()
    if upto in (None, "P3", "P3q", "P3m", "P3z", "P3t", "P4", "P5", "P5b", "P6a", "P6b", "P6d",
                "P6e", "P7"):
        pr.p3()
        if upto == "P3q":
            pr.dump([pr.tb(32768, 16384)], waits=("s_p3v", "s_p3s", "s_p3p"))
            return pr.finish()
        if upto == "P3z":
            pr.dump([pr.tb(9216, 4096)], waits=("s_p3v", "s_p3s"))
            return pr.finish()
        if upto == "P3t":
            pr.dump([pr.tb(11264, 4096)], waits=("s_p3v", "s_p3s"))
            return pr.finish()
        if upto == "P3m":
            s_fin = kb.sem("s_fin")
            for s_ in pr.s_hsp_m:
                kb.gpsimd.wait(s_)
            kb.gpsimd.dma_start(
                out=pr.out.rearrange("a b -> (a b)")[0:512 * 2048]
                .rearrange("(p l) -> p l", p=512),
                in_=pr.dscr["h_spill"][0:512, :]).inc(s_fin, 16)
            kb.gpsimd.wait(s_fin)
            return pr.finish()
        if upto == "P3":
            s_fin = kb.sem("s_fin")
            for s_ in pr.s_hsp_m:
                kb.gpsimd.wait(s_)
            kb.gpsimd.dma_start(
                out=pr.out.rearrange("a b -> (a b)")
                .rearrange("(p l) -> p l", p=1024),
                in_=pr.dscr["h_spill"][0:1024, :]).inc(s_fin, 16)
            kb.gpsimd.wait(s_fin)
            return pr.finish()
        pr.p4()
        if upto == "P4":
            pr.dump([pr.tf(0, 16384)], waits=("s_p4v",))
            return pr.finish()
        pr.p5()
        if upto == "P5":
            pr.dump([pr.tb(32768, 16384)], waits=("s_p5v", "s_p5s"))
            return pr.finish()
        if upto == "P5b":
            aps = [pr.we_mat[:], pr.tb(16384, 16256)]
            pr.dump(aps, waits=("s_p5v", "s_p5s"))
            return pr.finish()
        pr.p6()
        if upto in ("P6a", "P6b"):
            aps = [pr.tb(32768, 16384)]
            pr.dump(aps, waits=("s_p6v", "s_p6s", "s_p6p", "s_whld", "s_wqk"))
            return pr.finish()
        if upto == "P6e":
            aps = [pr.qkwb_t[:], pr.tb(40960, 256), pr.tb(32768, 14336)]
            pr.dump(aps, waits=("s_p6v", "s_p6s", "s_p6p"))
            return pr.finish()
        if upto == "P6d":
            L = pr.L1
            aps = [L["linkvT"], L["qq"], L["lq"], L["qk"], L["link"]]
            pr.dump(aps, waits=("s_p6v", "s_p6s", "s_p6p"))
            s_fin2 = kb.sem("s_fin2")
            kb.gpsimd.wait(kb.sem("s_vsp"))
            kb.gpsimd.dma_start(
                out=pr.out.rearrange("a b -> (a b)")
                [2048 * 512:2048 * 512 + 512 * 2048]
                .rearrange("(p l) -> p l", p=512),
                in_=pr.dscr["v_spill"][0:512, :]).inc(s_fin2, 16)
            kb.gpsimd.wait(s_fin2)
            return pr.finish()
        pr.p7()
        if upto is None:
            pr.p8()
            pr.p9()
            return pr.finish()
        if upto == "P7":
            s_fin = kb.sem("s_fin")
            kb.gpsimd.wait(kb.sem("s_a1"))
            kb.gpsimd.dma_start(out=pr.out, in_=pr.dscr["attn1_spill"])\
                .inc(s_fin, 16)
            kb.gpsimd.wait(s_fin)
            return pr.finish()
        raise NotImplementedError(upto)
    if upto == "P2":
        s_fin = kb.sem("s_fin")
        kb.gpsimd.wait(kb.sem("s_ast0"))
        kb.gpsimd.wait(kb.sem("s_ast1"))
        kb.gpsimd.dma_start(out=pr.out, in_=pr.dscr["attn0_spill"])\
            .inc(s_fin, 16)
        kb.gpsimd.wait(s_fin)
        return pr.finish()
    raise NotImplementedError(upto)


def _stage_inputs(inputs):
    x = np.ascontiguousarray(np.asarray(inputs["L0_attn_x"], np.float32))
    x2 = x.reshape(B, 2, TOK, D)
    com = {
        "W0": inputs["L0_attn_W"],
        "w1": inputs["L0_ffn_w1"], "b1": inputs["L0_ffn_b1"],
        "w2": inputs["L0_ffn_w2"], "b2": inputs["L0_ffn_b2"],
        "ln_w": inputs["L1_attn_ln_w"], "ln_b": inputs["L1_attn_ln_b"],
        "Wh": inputs["L1_attn_Wh"], "bh": inputs["L1_attn_bh"],
        "Wqk": inputs["L1_attn_Wqk"], "bqk": inputs["L1_attn_bqk"],
        "qk_w": inputs["L1_attn_qk_w"], "qk_b": inputs["L1_attn_qk_b"],
        "rel_table": inputs["L1_attn_rel_table"],
        "Wout": inputs["L1_attn_Wout"], "bout": inputs["L1_attn_bout"],
        "W1m": inputs["L1_ffn_W1"], "b1m": inputs["L1_ffn_b1"],
        "W3m": inputs["L1_ffn_W3"], "b3m": inputs["L1_ffn_b3"],
        "W2m": inputs["L1_ffn_W2"], "b2m": inputs["L1_ffn_b2"],
        "Wg": inputs["L1_ffn_Wg"], "bg": inputs["L1_ffn_bg"],
        "Wf": inputs["L1_ffn_Wf"], "bf": inputs["L1_ffn_bf"],
    }
    com = {k: np.ascontiguousarray(np.asarray(v, np.float32))
           for k, v in com.items()}
    maps = []
    for c in range(NCORES):
        b, h = c // 2, c % 2
        pm = np.zeros((1, 16), np.float32)
        pm[0, 0] = 1.0 if h == 1 else 0.0
        m = dict(com)
        m["x_loc"] = np.ascontiguousarray(x2[b, h])
        m["x_rem"] = np.ascontiguousarray(x2[b, 1 - h])
        m["w1shard"] = np.ascontiguousarray(com["w1"][c * 512:(c + 1) * 512])
        m["w2shard"] = np.ascontiguousarray(com["w2"][c * 128:(c + 1) * 128])
        m["pmask"] = pm
        maps.append(m)
    return maps


def _run(upto=None, inputs=None, core_ids=None):
    nc, es = build(upto, no_cc=(core_ids is not None and len(core_ids) == 1))
    with es:
        maps = _stage_inputs(inputs)
        if core_ids is None:
            core_ids = list(range(NCORES))
        res = run_bass_kernel_spmd(nc, [maps[c] for c in core_ids],
                                   core_ids=core_ids)
    return res


def kernel(**inputs):
    res = _run(upto=None, inputs=inputs)
    outs = [res.results[c]["out"] for c in range(NCORES)]
    full = np.stack(outs).reshape(B, 2, TOK, D).reshape(B, N, D)
    return np.ascontiguousarray(full.astype(np.float32))


# version 14 (final)
# speedup vs baseline: 1.0215x; 1.0214x over previous
import contextlib
import numpy as np
import ml_dtypes
import concourse.bass as bass
import concourse.mybir as mybir
from concourse.ap import AP
from concourse.bass_utils import run_bass_kernel_spmd

F32 = mybir.dt.float32
BF16 = mybir.dt.bfloat16
AF = mybir.ActivationFunctionType
ALU = mybir.AluOpType
AX = mybir.AxisListType

NCORES = 8
B, N, D = 4, 4096, 1024
TOK = 2048
NT = TOK // 128
NKC = N // 128
DC = D // 128
MAGIC = float(np.float32(1.5 * 2 ** 23))
LN_EPS = 1e-5
G = 256
NG = TOK // G
HID1 = 2048
QK = 128
MOE_H = 2048
NE = 8
BIAS_SCALE = float(128.0 ** 0.5)
NPAIR = 8

ARENA_F32 = 41472

ENGS = ("tensor", "scalar", "vector", "gpsimd", "sync")


class Sem:
    __slots__ = ("h", "n")

    def __init__(self, h):
        self.h = h
        self.n = 0


class Rec:
    __slots__ = ("incs", "kb", "eng")

    def __init__(self, kb, eng):
        self.incs = []
        self.kb = kb
        self.eng = eng

    def inc(self, sem, k=1):
        self.incs.append((sem, k))
        sem.n += k
        self.kb.last_inc[self.eng] = True
        return self


class EngRec:
    def __init__(self, kb, name):
        self._kb = kb
        self._name = name

    def wait(self, sem, val=None):
        v = sem.n if val is None else val
        if v <= 0:
            return
        h = sem.h
        self._kb.streams[self._name].append(lambda E: E.wait_ge(h, v))

    def __getattr__(self, meth):
        kb, name = self._kb, self._name

        def call(*a, **kw):
            r = Rec(kb, name)

            def do(E):
                ins = getattr(E, meth)(*a, **kw)
                for sem, k in r.incs:
                    ins.then_inc(sem.h, k)
                return ins

            kb.streams[name].append(do)
            kb.last_inc[name] = False
            return r

        return call


class KB:
    def __init__(self, nc, es):
        self.nc = nc
        self.es = es
        self.streams = {e: [] for e in ENGS}
        self.last_inc = {e: True for e in ENGS}
        for e in ENGS:
            setattr(self, e, EngRec(self, e))
        self._sems = {}

    def sem(self, name):
        if name not in self._sems:
            self._sems[name] = Sem(self.es.enter_context(self.nc.semaphore(name)))
        return self._sems[name]

    def sbuf(self, stack, name, shape, dtype, side="right"):
        return stack.enter_context(
            self.nc.sbuf_tensor(name, shape, dtype, side=side))

    def psum(self, stack, name, shape, dtype=F32):
        return stack.enter_context(self.nc.psum_tensor(name, shape, dtype))

    def replay(self):
        with self.nc.Block() as block:
            for e in ENGS:
                fns = self.streams[e]
                getattr(block, e)(lambda E, fns=fns: [f(E) for f in fns])


def _t5_bucket_ext():
    ks = np.arange(-255, 256)
    n = np.maximum(ks, 0).astype(np.int64)
    val = (np.log(np.maximum(n, 1).astype(np.float32) / np.float32(16.0))
           / np.float32(np.log(8.0)) * np.float32(16.0)).astype(np.int32)
    val = np.minimum(16 + val, 31)
    return np.where(n < 16, n, val).astype(np.int64)


def _onehot_511():
    b = _t5_bucket_ext()
    oh = np.zeros((32, 511), np.float32)
    oh[b, np.arange(511)] = 1.0
    return oh


class Prog:
    def __init__(self, upto=None, no_cc=False):
        self.no_cc = no_cc
        self.upto = upto
        nc = bass.Bass("TRN2", target_bir_lowering=False, debug=False)
        self.nc = nc
        self.es = contextlib.ExitStack()
        self.kb = KB(nc, self.es)
        self._declare_io()
        self._consts()
        self.longs = contextlib.ExitStack()
        self.es.enter_context(self.longs)

    def _declare_io(self):
        nc = self.nc
        self.din = {}
        for nm, shp in [
            ("x_loc", [TOK, D]), ("x_rem", [TOK, D]),
            ("W0", [D, 3072]),
            ("w1", [4096, D]), ("b1", [4096]), ("w2", [D, 4096]), ("b2", [D]),
            ("w1shard", [512, D]), ("w2shard", [128, 4096]),
            ("ln_w", [D]), ("ln_b", [D]),
            ("Wh", [D, 2 * HID1]), ("bh", [2 * HID1]),
            ("Wqk", [D, QK]), ("bqk", [QK]),
            ("qk_w", [4, QK]), ("qk_b", [4, QK]), ("rel_table", [32]),
            ("Wout", [HID1, D]), ("bout", [D]),
            ("W1m", [NE, D, MOE_H]), ("b1m", [NE, MOE_H]),
            ("W3m", [NE, D, MOE_H]), ("b3m", [NE, MOE_H]),
            ("W2m", [NE, MOE_H, D]), ("b2m", [NE, D]),
            ("Wg", [D, NE]), ("bg", [NE]),
            ("Wf", [D, D]), ("bf", [D]),
            ("pmask", [1, 16]),
        ]:
            self.din[nm] = nc.dram_tensor(nm, shp, F32, kind="ExternalInput").ap()
        self.out = nc.dram_tensor("out", [TOK, D], F32, kind="ExternalOutput").ap()
        d = {}
        d["h_spill"] = nc.dram_tensor("h_spill", [4096, TOK], F32).ap()
        d["x1_spill"] = nc.dram_tensor("x1_spill", [TOK, D], F32).ap()
        d["attn0_spill"] = nc.dram_tensor("attn0_spill", [TOK, D], BF16).ap()
        d["attn1_spill"] = nc.dram_tensor("attn1_spill", [TOK, D], BF16).ap()
        d["v_spill"] = nc.dram_tensor("v_spill", [TOK, HID1], BF16).ap()
        d["red1_in"] = nc.dram_tensor("red1_in", [1, 16], F32).ap()
        d["red1_out"] = nc.dram_tensor("red1_out", [1, 16], F32).ap()
        d["red2_in"] = nc.dram_tensor("red2_in", [1, 16], F32).ap()
        d["red2_out"] = nc.dram_tensor("red2_out", [1, 16], F32).ap()
        d["pair_in"] = nc.dram_tensor("pair_in", [128, HID1], F32).ap()
        d["pair_out"] = nc.dram_tensor("pair_out", [256, HID1], F32).ap()
        d["wprime"] = nc.dram_tensor("wprime", [511], F32).ap()
        d["xn1fm_spill"] = nc.dram_tensor("xn1fm_spill", [128, 16384], BF16).ap()
        d["xn1_spill"] = nc.dram_tensor("xn1_spill", [TOK, D], BF16).ap()
        self.dscr = d

    def _consts(self):
        nc = self.nc
        self.ident_b = nc.inline_tensor(
            np.eye(128, dtype=ml_dtypes.bfloat16), "identb").ap()
        self.ident_f = nc.inline_tensor(np.eye(128, dtype=np.float32), "identf").ap()
        self.oh511 = nc.inline_tensor(_onehot_511(), "oh511").ap()
        m = np.zeros((256, 256), np.float32)
        jj, ii = np.meshgrid(np.arange(256), np.arange(256), indexing="ij")
        m[jj > ii] = -1.0e4
        m2 = np.zeros((128, 512), np.float32)
        for jc in range(2):
            m2[:, jc * 256:(jc + 1) * 256] = m[jc * 128:(jc + 1) * 128, :]
        self.maskT_add = nc.inline_tensor(m2, "masktadd").ap()

    def _arena(self):
        self.ar_f32 = self.kb.sbuf(self.longs, "arena", [128, ARENA_F32], F32,
                                   side="left")
        self.ar_b16 = self.ar_f32.bitcast(BF16)

    def tf(self, off, cols):
        assert off + cols <= ARENA_F32, (off, cols)
        return self.ar_f32[:, off:off + cols]

    def tb(self, off, cols):
        assert 2 * off + cols <= 2 * ARENA_F32, (off, cols)
        return self.ar_b16[:, 2 * off:2 * off + cols]

    def p0(self):
        kb = self.kb
        LS = self.longs
        self.P0 = contextlib.ExitStack()
        self.es.enter_context(self.P0)
        s_id = kb.sem("s_id")
        s_pe = kb.sem("s_pe0")
        s_ve = kb.sem("s_ve0")
        s_sc = kb.sem("s_sc0")

        self._arena()
        self.ones_col = kb.sbuf(LS, "ones_col", [1, 128], BF16, side="left")
        kb.vector.memset(self.ones_col[:], 1.0).inc(s_ve)
        self.ones_f = kb.sbuf(LS, "ones_f", [1, 128], F32, side="left")
        kb.vector.memset(self.ones_f[:], 1.0).inc(s_ve)
        self.IdB = kb.sbuf(LS, "IdB", [128, 128], BF16, side="left")
        self.IdF = kb.sbuf(LS, "IdF", [128, 128], F32, side="left")
        kb.sync.dma_start(out=self.IdB[:], in_=self.ident_b).inc(s_id, 16)
        kb.sync.dma_start(out=self.IdF[:], in_=self.ident_f).inc(s_id, 16)
        self.stats = kb.sbuf(LS, "stats", [128, 64], F32, side="left")
        kb.vector.memset(self.stats[:], 0.0).inc(s_ve)
        self.gmax1 = kb.sbuf(LS, "gmax1", [1, 16], F32, side="left")
        self.gmax2 = kb.sbuf(LS, "gmax2", [1, 16], F32, side="left")
        self.scl = kb.sbuf(LS, "scl", [128, 16], F32, side="left")
        self.sclr = kb.sbuf(LS, "sclr", [1, 16], F32, side="left")
        self.sclb = kb.sbuf(LS, "sclb", [1, 16], BF16, side="left")

        self.pS = [kb.psum(LS, f"pS{i}", [128, 1024], F32) for i in range(2)]
        self.pav = [kb.psum(LS, f"pav{i}", [128, 260], F32) for i in range(2)]
        self.ptb = kb.psum(LS, "ptb", [128, 1024], BF16)
        self.pu0 = kb.psum(LS, "pu0", [128, 512], F32)

        def bias_cols(name, src_flat_ap, L):
            c = L // 128
            s_b = kb.sem("s_bc_" + name)
            nat = kb.sbuf(self.P0, name + "_nat", [c, 128], F32)
            cols = kb.sbuf(LS, name + "_cols", [128, c], F32, side="left")
            kb.sync.dma_start(out=nat[:],
                              in_=src_flat_ap.rearrange("(c p) -> c p", p=128))\
                .inc(s_b, 16)
            kb.tensor.wait(s_b)
            kb.tensor.wait(s_id)
            kb.tensor.wait(s_ve)
            kb.tensor.transpose(self.pu0[:, 0:c], nat[:], self.IdF[0:c, 0:c])\
                .inc(s_pe)
            kb.vector.wait(s_pe)
            kb.vector.tensor_copy(out=cols[:], in_=self.pu0[:, 0:c]).inc(s_ve)
            return cols

        self.lnw_cols = bias_cols("lnw", self.din["ln_w"], D)
        self.lnb_cols = bias_cols("lnb", self.din["ln_b"], D)
        self.b1_cols = bias_cols("b1q", self.din["b1"], 4096)
        self.b2_cols = bias_cols("b2q", self.din["b2"], D)
        self.b1m_cols = bias_cols("b1m",
                                  self.din["b1m"].rearrange("e h -> (e h)"),
                                  NE * MOE_H)
        self.b3m_cols = bias_cols("b3m",
                                  self.din["b3m"].rearrange("e h -> (e h)"),
                                  NE * MOE_H)
        self.bh_cols = bias_cols("bhc", self.din["bh"], 2 * HID1)

        s_gp0 = kb.sem("s_gp0")
        self.s_gp0 = s_gp0

        def row_b(name, src_ap, L):
            tbf = kb.sbuf(LS, name + "_rb", [1, L], BF16, side="left")
            kb.gpsimd.dma_start(out=tbf[:],
                                in_=src_ap.rearrange("(o l) -> o l", o=1))\
                .inc(s_gp0, 16)
            return tbf

        self.bh_rb = row_b("bh", self.din["bh"], 2 * HID1)
        self.bqk_rb = row_b("bqk", self.din["bqk"], QK)
        self.bout_rb = row_b("bout", self.din["bout"], D)
        self.bg_rb = row_b("bg", self.din["bg"], NE)
        self.bf_rb = row_b("bf", self.din["bf"], D)
        self.b2_rb = row_b("b2r", self.din["b2"], D)
        self.qkw_rb = row_b("qkw", self.din["qk_w"].rearrange("a b -> (a b)"),
                            4 * QK)
        self.qkb_rb = row_b("qkb", self.din["qk_b"].rearrange("a b -> (a b)"),
                            4 * QK)
        self.b2m_b = kb.sbuf(LS, "b2m_b", [NE, D], BF16, side="left")
        kb.gpsimd.dma_start(out=self.b2m_b[:], in_=self.din["b2m"])\
            .inc(s_gp0, 16)
        self.pmask_sb = kb.sbuf(LS, "pmask_sb", [1, 16], F32, side="left")
        kb.sync.dma_start(out=self.pmask_sb[:], in_=self.din["pmask"])\
            .inc(kb.sem("s_pmld"), 16)

        self.qkwb_t = kb.sbuf(LS, "qkwb_t", [128, 8 * 128], F32, side="left")
        kb.tensor.wait(s_gp0)
        kb.tensor.wait(s_ve)
        for i in range(8):
            rowsrc = self.qkw_rb if i < 4 else self.qkb_rb
            rs = rowsrc[0:1, (i % 4) * 128:(i % 4) * 128 + 128]
            pb = self.pS[0] if i % 2 == 0 else self.pS[1]
            if i >= 2:
                kb.tensor.wait(s_ve)
            kb.tensor.matmul(pb[:, 0:128], self.ones_col[:], rs,
                             start=True, stop=True).inc(s_pe)
            kb.vector.wait(s_pe)
            kb.vector.tensor_copy(out=self.qkwb_t[:, i * 128:(i + 1) * 128],
                                  in_=pb[:, 0:128]).inc(s_ve)

        rel_sb = kb.sbuf(self.P0, "rel_sb", [32, 1], F32)
        oh_sb = kb.sbuf(self.P0, "oh_sb", [32, 511], F32)
        s_t5 = kb.sem("s_t5")
        kb.sync.dma_start(out=rel_sb[:],
                          in_=self.din["rel_table"].rearrange("(r o) -> r o", o=1))\
            .inc(s_t5, 16)
        kb.sync.dma_start(out=oh_sb[:], in_=self.oh511).inc(s_t5, 16)
        kb.tensor.wait(s_t5)
        kb.tensor.wait(s_ve)
        kb.tensor.matmul(self.pu0[0:1, 0:511], rel_sb[:], oh_sb[:],
                         start=True, stop=True).inc(s_pe)
        wrow = kb.sbuf(self.P0, "wrow", [1, 511], F32)
        kb.scalar.wait(s_pe)
        kb.scalar.activation(out=wrow[:], in_=self.pu0[0:1, 0:511],
                             func=AF.Identity, scale=BIAS_SCALE).inc(s_sc)
        kb.scalar.wait(s_sc)
        kb.scalar.dma_start(
            out=self.dscr["wprime"].rearrange("(o l) -> o l", o=1),
            in_=wrow[:]).inc(kb.sem("s_st"), 16)
        M1 = kb.sbuf(self.P0, "M1", [128, 2 * 256], F32)
        s_m1 = kb.sem("s_m1")
        kb.sync.wait(kb.sem("s_st"))
        for ci in range(2):
            kb.sync.dma_start(
                out=M1[:, ci * 256:(ci + 1) * 256],
                in_=AP(tensor=self.dscr["wprime"].tensor, offset=ci * 128,
                       ap=[[1, 128], [1, 256]])).inc(s_m1, 16)
        BiasIJ = kb.sbuf(self.P0, "BiasIJ", [128, 2 * 256], F32)
        kb.vector.wait(s_m1)
        for ci in range(2):
            rev = AP(tensor=M1, offset=ci * 256 + 255,
                     ap=[[2 * 256, 128], [-1, 256]])
            kb.vector.tensor_copy(out=BiasIJ[:, ci * 256:(ci + 1) * 256],
                                  in_=rev).inc(s_ve)
        self.BiasT = kb.sbuf(LS, "BiasT", [128, 2 * 256], F32, side="left")
        MaskT = kb.sbuf(self.P0, "MaskT", [128, 2 * 256], F32)
        s_mask = kb.sem("s_mask")
        kb.sync.dma_start(out=MaskT[:], in_=self.maskT_add).inc(s_mask, 16)
        for jc in range(2):
            kb.tensor.wait(s_ve)
            for icb in range(2):
                kb.tensor.transpose(
                    self.pu0[:, icb * 128:icb * 128 + 128],
                    BiasIJ[:, icb * 256 + jc * 128: icb * 256 + jc * 128 + 128],
                    self.IdF[:]).inc(s_pe)
            kb.vector.wait(s_pe)
            kb.vector.wait(s_mask)
            for icb in range(2):
                kb.vector.scalar_tensor_tensor(
                    out=self.BiasT[:, jc * 256 + icb * 128:
                                   jc * 256 + icb * 128 + 128],
                    in0=self.pu0[:, icb * 128:icb * 128 + 128], scalar=1.0,
                    in1=MaskT[:, jc * 256 + icb * 128: jc * 256 + icb * 128 + 128],
                    op0=ALU.mult, op1=ALU.add).inc(s_ve)
        self.p0_done = (s_pe, s_sc, s_ve)
        self.P0.close()

    def bcast_col(self, src11_f32, dst_col_f32, tag):
        kb = self.kb
        s_pe = kb.sem("s_bc_pe")
        s_ve = kb.sem("s_bc_ve")
        kb.vector.tensor_copy(out=self.sclb[0:1, 0:1], in_=src11_f32).inc(s_ve)
        kb.tensor.wait(s_ve)
        kb.tensor.matmul(self.pu0[:, 0:1], self.ones_col[:],
                         self.sclb[0:1, 0:1], start=True, stop=True).inc(s_pe)
        kb.vector.wait(s_pe)
        kb.vector.tensor_copy(out=dst_col_f32, in_=self.pu0[:, 0:1]).inc(s_ve)

    def p1(self):
        kb = self.kb
        XN = 0
        self.XN = XN
        xn_fm = self.tb(0, 2 * 32768)
        self.xn_fm = xn_fm

        P1 = contextlib.ExitStack()
        xld = [kb.sbuf(P1, f"xld{i}", [128, D], F32) for i in range(2)]
        xld += [self.tf(32768 + i * 1024, 1024) for i in range(4)]
        sqscr = kb.sbuf(P1, "sqscr", [128, D], F32)
        xnt = [kb.sbuf(P1, f"xnt{i}", [128, D], BF16) for i in range(2)]
        zb = kb.sbuf(self.longs, "zb", [128, 512], BF16, side="left")
        s_xld = [kb.sem("s_xld0"), kb.sem("s_xld1"),
                 kb.sem("s_xld2"), kb.sem("s_xld3")]
        s_sc = kb.sem("s_p1sc")
        s_ve = kb.sem("s_p1ve")
        s_pe = kb.sem("s_p1pe")
        snap_ldA = {}
        snap_A = {}
        snap_B = {}
        snap_apply = {}
        snap_xnt_done = {}
        snap_evac = {}
        SLOT = (0, 8, 48, 56)

        p0pe, p0sc, p0ve = self.p0_done
        kb.vector.wait(p0sc)
        kb.scalar.wait(p0ve)
        kb.tensor.wait(p0ve)
        kb.sync.wait(p0pe)
        kb.sync.wait(p0ve)
        kb.sync.wait(p0sc)

        ve, sc, te = kb.vector, kb.scalar, kb.tensor
        ve.memset(zb[:], 0.0).inc(s_ve)
        self.zb = zb
        snap_zb = s_ve.n

        def issue_load(t):
            buf = t % 6
            src = self.din["x_loc"] if t < NT else self.din["x_rem"]
            row0 = (t % NT) * 128
            if t >= 4:
                kb.sync.wait(s_sc, snap_A[t - 4][0])
                kb.sync.wait(s_ve, snap_A[t - 4][1])
            if t >= 6:
                kb.sync.wait(s_sc, snap_apply[t - 6])
            kb.sync.dma_start(out=xld[buf][:], in_=src[row0:row0 + 128, :])\
                .inc(s_xld[t % 4], 16)
            snap_ldA[t] = s_xld[t % 4].n

        def stageA(t):
            buf = t % 6
            cb = SLOT[t % 4]
            sc.wait(s_xld[t % 4], snap_ldA[t])
            sc.wait(s_sc)
            if t >= 4:
                sc.wait(s_ve, snap_B[t - 4][1])
            sc.activation(out=sqscr[:], in_=xld[buf][:], func=AF.Square,
                          accum_out=self.stats[:, cb + 1:cb + 2]).inc(s_sc)
            ve.wait(s_xld[t % 4], snap_ldA[t])
            ve.wait(s_ve)
            if t >= 4:
                ve.wait(s_sc, snap_B[t - 4][0])
            ve.reduce_sum(out=self.stats[:, cb + 0:cb + 1], in_=xld[buf][:],
                          axis=AX.X).inc(s_ve)
            snap_A[t] = (s_sc.n, s_ve.n)

        def stageB(t):
            cb = SLOT[t % 4]
            ve.wait(s_ve)
            ve.tensor_scalar(out=self.stats[:, cb + 2:cb + 3],
                             in0=self.stats[:, cb + 0:cb + 1],
                             scalar1=1.0 / D, scalar2=None, op0=ALU.mult)\
                .inc(s_ve)
            ve.wait(s_ve)
            ve.tensor_tensor(out=self.stats[:, cb + 3:cb + 4],
                             in0=self.stats[:, cb + 2:cb + 3],
                             in1=self.stats[:, cb + 2:cb + 3], op=ALU.mult)\
                .inc(s_ve)
            ve.wait(s_ve)
            ve.tensor_scalar(out=self.stats[:, cb + 4:cb + 5],
                             in0=self.stats[:, cb + 3:cb + 4],
                             scalar1=-1.0, scalar2=LN_EPS, op0=ALU.mult,
                             op1=ALU.add).inc(s_ve)
            sc.wait(s_sc)
            sc.wait(s_ve)
            sc.activation(out=self.stats[:, cb + 5:cb + 6],
                          in_=self.stats[:, cb + 1:cb + 2], func=AF.Ln,
                          scale=1.0 / D, bias=self.stats[:, cb + 4:cb + 5])\
                .inc(s_sc)
            sc.wait(s_sc)
            sc.activation(out=self.stats[:, cb + 6:cb + 7],
                          in_=self.stats[:, cb + 5:cb + 6], func=AF.Exp,
                          scale=-0.5).inc(s_sc)
            snap_ln = s_sc.n
            ve.wait(s_sc, snap_ln)
            ve.scalar_tensor_tensor(out=self.stats[:, cb + 7:cb + 8],
                                    in0=self.stats[:, cb + 2:cb + 3],
                                    scalar=-1.0,
                                    in1=self.stats[:, cb + 6:cb + 7],
                                    op0=ALU.mult, op1=ALU.mult).inc(s_ve)
            if t < NT:
                ve.wait(s_ve)
                ve.tensor_copy(out=self.stats[:, 16 + t:17 + t],
                               in_=self.stats[:, cb + 6:cb + 7]).inc(s_ve)
                ve.wait(s_ve)
                ve.tensor_copy(out=self.stats[:, 32 + t:33 + t],
                               in_=self.stats[:, cb + 7:cb + 8]).inc(s_ve)
            snap_B[t] = (s_sc.n, s_ve.n)

        def stageC(t):
            buf = t % 6
            cb = SLOT[t % 4]
            buf2 = t % 2
            if t >= 2:
                a, b = snap_xnt_done[t - 2]
                sc.wait(s_ve, a)
                sc.wait(s_pe, b)
            sc.wait(s_ve, snap_B[t][1])
            sc.wait(s_sc)
            sc.activation(out=xnt[buf2][:], in_=xld[buf][:],
                          func=AF.Identity,
                          scale=self.stats[:, cb + 6:cb + 7],
                          bias=self.stats[:, cb + 7:cb + 8]).inc(s_sc)
            snap_apply[t] = s_sc.n
            te.wait(s_sc, snap_apply[t])
            if t >= 1:
                te.wait(s_ve, snap_evac[t - 1])
            for d in range(DC):
                te.transpose(self.ptb[:, d * 128:(d + 1) * 128],
                             xnt[buf2][:, d * 128:(d + 1) * 128], self.IdB[:])\
                    .inc(s_pe)
            ve.wait(s_pe)
            if t == 0:
                ve.wait(s_ve, snap_zb)
            for d in range(DC):
                ve.tensor_tensor(
                    out=self.ar_b16[:, d * N + t * 128: d * N + t * 128 + 128],
                    in0=self.ptb[:, d * 128:(d + 1) * 128],
                    in1=zb[:, 0:128], op=ALU.add).inc(s_ve)
            snap_evac[t] = s_ve.n
            snap_xnt_done[t] = (s_ve.n, s_pe.n)

        issue_load(0)
        issue_load(1)
        for t in range(NKC + 4):
            if t >= 4:
                stageC(t - 4)
            if 2 <= t and t - 2 < NKC:
                stageB(t - 2)
            if t + 2 < NKC:
                issue_load(t + 2)
            if t < NKC:
                stageA(t)

        P1.close()
        self.s_p1 = (s_sc, s_ve, s_pe)

    def xn(self, d, a, w):
        return self.ar_b16[:, d * N + a: d * N + a + w]

    def p2(self):
        kb = self.kb
        s_sc1, s_ve1, s_pe1 = self.s_p1
        ve, sc, te, gp, sy = kb.vector, kb.scalar, kb.tensor, kb.gpsimd, kb.sync

        P2w = contextlib.ExitStack()
        xld2 = [kb.sbuf(P2w, f"xld2_{i}", [128, D], F32) for i in range(2)]
        xn32 = kb.sbuf(P2w, "xn32", [128, D], F32)
        s_x2 = [kb.sem("s_x2_0"), kb.sem("s_x2_1")]
        s_15s = kb.sem("s_15s")
        s_15v = kb.sem("s_15v")
        snap_x2 = {}
        sy.wait(s_ve1)
        sy.wait(s_sc1)
        for t in range(NT):
            buf = t % 2
            if t >= 2:
                sy.wait(s_15s, snap_x2[t - 2])
            sy.dma_start(out=xld2[buf][:],
                         in_=self.din["x_loc"][t * 128:(t + 1) * 128, :])\
                .inc(s_x2[buf], 16)
            sc.wait(s_x2[buf])
            sc.wait(s_15v)
            sc.activation(out=xn32[:], in_=xld2[buf][:], func=AF.Identity,
                          scale=self.stats[:, 16 + t:17 + t],
                          bias=self.stats[:, 32 + t:33 + t]).inc(s_15s)
            snap_x2[t] = s_15s.n
            ve.wait(s_15s)
            ve.wait(s_15v)
            ve.reduce_max(out=self.stats[:, 49:50], in_=xn32[:], axis=AX.X,
                          apply_absolute_value=True).inc(s_15v)
            ve.wait(s_15v)
            if t == 0:
                ve.tensor_copy(out=self.stats[:, 48:49],
                               in_=self.stats[:, 49:50]).inc(s_15v)
            else:
                ve.tensor_tensor(out=self.stats[:, 48:49],
                                 in0=self.stats[:, 48:49],
                                 in1=self.stats[:, 49:50], op=ALU.max)\
                    .inc(s_15v)
        te.wait(s_15v)
        te.wait(s_ve1)
        te.transpose(self.pu0[0:1, 0:128], self.stats[:, 48:49], self.IdF[:])\
            .inc(kb.sem("s_15p"))
        ve.wait(kb.sem("s_15p"))
        ve.reduce_max(out=self.stats[0:1, 50:51], in_=self.pu0[0:1, 0:128],
                      axis=AX.X).inc(s_15v)

        wsh = kb.sbuf(P2w, "wsh", [128, 1024], F32)
        s_wld = kb.sem("s_wld")
        s_wve = kb.sem("s_wve")
        s_wpe = kb.sem("s_wpe")
        sy.wait(s_sc1)
        sy.wait(s_ve1)
        sy.wait(s_pe1)
        ve.wait(s_ve1)
        for j, (col, srcname) in enumerate(((51, "w1shard"), (52, "w2shard"))):
            for i in range(4):
                if j + i > 0:
                    sy.wait(s_wve)
                if srcname == "w1shard":
                    srcap = self.din[srcname][i * 128:(i + 1) * 128, :]
                else:
                    srcap = self.din[srcname][:, i * 1024:(i + 1) * 1024]
                sy.dma_start(out=wsh[:], in_=srcap).inc(s_wld, 16)
                ve.wait(s_wld)
                ve.wait(s_wve)
                ve.reduce_max(out=self.stats[:, 55:56], in_=wsh[:], axis=AX.X,
                              apply_absolute_value=True).inc(s_wve)
                ve.wait(s_wve)
                if i == 0:
                    ve.tensor_copy(out=self.stats[:, col:col + 1],
                                   in_=self.stats[:, 55:56]).inc(s_wve)
                else:
                    ve.tensor_tensor(out=self.stats[:, col:col + 1],
                                     in0=self.stats[:, col:col + 1],
                                     in1=self.stats[:, 55:56], op=ALU.max)\
                        .inc(s_wve)
        P2w.close()
        wred = kb.sbuf(self.longs, "wred", [2, 1], F32, side="left")
        te.wait(s_wve)
        te.wait(s_ve1)
        te.wait(kb.sem("s_15v"))
        te.transpose(self.pu0[0:2, 0:128], self.stats[:, 51:53], self.IdF[:])\
            .inc(s_wpe)
        ve.wait(s_wpe)
        ve.reduce_max(out=wred[:], in_=self.pu0[0:2, 0:128], axis=AX.X)\
            .inc(s_wve)
        te.wait(s_wve)
        te.transpose(self.pu0[0:1, 0:2], wred[:], self.IdF[0:2, 0:2]).inc(s_wpe)
        ve.wait(s_wpe)
        ve.wait(s_ve1)
        ve.wait(kb.sem("s_15v"))
        ve.memset(self.sclr[:], 0.0).inc(s_wve)
        ve.wait(s_wve)
        ve.tensor_copy(out=self.sclr[0:1, 0:1], in_=self.stats[0:1, 50:51])\
            .inc(s_wve)
        ve.wait(s_wve)
        ve.tensor_copy(out=self.sclr[0:1, 1:3], in_=self.pu0[0:1, 0:2])\
            .inc(s_wve)
        s_gp = kb.sem("s_gp")
        s_cc1 = kb.sem("s_cc1")
        gp.wait(s_wve)
        gp.dma_start(out=self.dscr["red1_in"], in_=self.sclr[:]).inc(s_gp, 16)
        gp.wait(s_gp)
        if self.no_cc:
            gp.dma_start(out=self.dscr["red1_out"], in_=self.dscr["red1_in"])\
                .inc(s_cc1, 16)
        else:
            gp.collective_compute(
                "AllReduce", ALU.max, replica_groups=[list(range(NCORES))],
                ins=[self.dscr["red1_in"].opt()],
                outs=[self.dscr["red1_out"].opt()],
            ).inc(s_cc1, 1)
        self.s_cc1 = s_cc1

        O_W, O_KF, O_QF, O_VT, O_EB, O_AG = (32768, 34304, 36352, 37376,
                                             39456, 40484)
        assert O_AG + 512 <= ARENA_F32
        Wq = [self.tb(O_W, 1024)] * 2
        Wk = [self.tb(O_W + 512, 1024)] * 2
        Wv = [self.tb(O_W + 1024, 1024)] * 2
        k_fm = self.tb(O_KF, 4096)
        q_fm = self.tb(O_QF, 2048)
        v_tm = self.tb(O_VT, 4160)
        Eb = [self.tb(O_EB + i * 512, 1024) for i in range(2)]
        att_g = [self.tb(O_AG + i * 256, 512) for i in range(2)]
        self.zt = kb.sbuf(self.longs, "zt", [128, 8], F32, side="left")
        self.zr = kb.sbuf(self.longs, "zr", [128, 8], F32, side="left")

        s_wp_ = kb.sem("s_wp")
        s_wp = [s_wp_, s_wp_]
        s_prj = kb.sem("s_prj")
        s_pse = kb.sem("s_pse")
        s_pve = kb.sem("s_pve")
        s_ast = [kb.sem("s_ast0"), kb.sem("s_ast1")]

        ve.wait(s_ve1)
        vt3 = v_tm.rearrange("p (k c) -> p k c", c=130)
        ve.memset(vt3[:, :, 64:65], 1.0).inc(s_pve)
        ve.memset(vt3[:, :, 129:130], 1.0).inc(s_pve)

        def load_pair(p):
            buf = p % 2
            if p >= 1:
                gp.wait(s_prj, self._wfree[p - 1])
            ha, hb = 2 * p, 2 * p + 1
            for dsttile, off in ((Wq[buf], 0), (Wk[buf], 64), (Wv[buf], 128)):
                for ri, h in enumerate((ha, hb)):
                    c0 = 192 * h + off
                    for d in range(DC):
                        gp.dma_start(
                            out=dsttile[:, d * 128 + ri * 64:
                                        d * 128 + ri * 64 + 64],
                            in_=self.din["W0"][d * 128:(d + 1) * 128,
                                               c0:c0 + 64]).inc(s_wp[buf], 16)

        self._wfree = {}
        snap_exp = {}
        snap_av = {}
        snap_st = {}
        snap_pavfree = 0
        gp.wait(s_sc1)
        gp.wait(s_ve1)
        load_pair(0)

        for p in range(NPAIR):
            buf = p % 2
            te.wait(s_wp[buf])
            te.wait(s_sc1)
            te.wait(s_ve1)
            if p >= 1:
                te.wait(s_pse)
                te.wait(s_pve)
            self._wfree[p] = None
            for blk in range(8):
                ps = self.pS[blk % 2]
                if blk >= 2:
                    te.wait(s_pse, snap_kq.get(blk - 2, 0))
                last = None
                for d in range(DC):
                    last = te.matmul(ps[:, 0:512],
                                     Wk[buf][:, d * 128:(d + 1) * 128],
                                     self.xn(d, blk * 512, 512),
                                     start=(d == 0), stop=(d == DC - 1))
                last.inc(s_prj)
                sc.wait(s_prj)
                sc.activation(out=k_fm[:, blk * 512:(blk + 1) * 512],
                              in_=ps[:, 0:512], func=AF.Identity).inc(s_pse)
                if blk == 0:
                    snap_kq = {}
                snap_kq[blk] = s_pse.n
            for blk in range(4):
                ps = self.pS[blk % 2]
                te.wait(s_pse, snap_kq[6 + (blk % 2)] if blk < 2 else
                        snap_q.get(blk - 2, 0))
                last = None
                for d in range(DC):
                    last = te.matmul(ps[:, 0:512],
                                     Wq[buf][:, d * 128:(d + 1) * 128],
                                     self.xn(d, blk * 512, 512),
                                     start=(d == 0), stop=(d == DC - 1))
                last.inc(s_prj)
                sc.wait(s_prj)
                sc.activation(out=q_fm[:, blk * 512:(blk + 1) * 512],
                              in_=ps[:, 0:512], func=AF.Identity).inc(s_pse)
                if blk == 0:
                    snap_q = {}
                snap_q[blk] = s_pse.n
            snap_vt = {}
            for kc in range(NKC):
                pv = self.pav[kc % 2]
                if kc >= 2:
                    te.wait(s_pve, snap_vt[kc - 2])
                last = None
                for d in range(DC):
                    last = te.matmul(
                        pv[:, 0:128],
                        self.xn(d, kc * 128, 128),
                        Wv[buf][:, d * 128:(d + 1) * 128],
                        start=(d == 0), stop=(d == DC - 1))
                last.inc(s_prj)
                if kc == 0:
                    self._wfree[p] = s_prj.n
                ve.wait(s_prj)
                ve.tensor_copy(out=vt3[:, kc, 0:64], in_=pv[:, 0:64]).inc(s_pve)
                ve.tensor_copy(out=vt3[:, kc, 65:129], in_=pv[:, 64:128])\
                    .inc(s_pve)
                snap_vt[kc] = s_pve.n
            self._wfree[p] = s_prj.n
            if p + 1 < NPAIR:
                load_pair(p + 1)
            te.wait(s_pse)
            te.wait(s_pve)
            flat = [(qb, kc) for qb in range(4) for kc in range(NKC)]

            def emit_S(j):
                qb_, kc_ = flat[j]
                ps_ = self.pS[j % 2]
                if j >= 2:
                    te.wait(s_pse, snap_exp[j - 2])
                te.matmul(ps_[:, 0:512],
                          k_fm[0:64, kc_ * 128:kc_ * 128 + 128],
                          q_fm[0:64, qb_ * 512:(qb_ + 1) * 512],
                          start=True, stop=True, tile_position=(0, 0))
                r_ = te.matmul(ps_[:, 512:1024],
                               k_fm[64:128, kc_ * 128:kc_ * 128 + 128],
                               q_fm[64:128, qb_ * 512:(qb_ + 1) * 512],
                               start=True, stop=True, tile_position=(64, 0))
                r_.inc(s_prj)
                snap_st[j] = s_prj.n

            emit_S(0)
            for idx, (qb, kc) in enumerate(flat):
                if idx + 1 < len(flat):
                    emit_S(idx + 1)
                ps = self.pS[idx % 2]
                sc.wait(s_prj, snap_st[idx])
                if idx >= 2:
                    sc.wait(s_prj, snap_av[idx - 2])
                sc.activation(out=Eb[idx % 2][:], in_=ps[:], func=AF.Exp,
                              scale=0.125).inc(s_pse)
                snap_exp[idx] = s_pse.n
                te.wait(s_pse, snap_exp[idx])
                if kc == 0:
                    te.wait(s_pve, snap_pavfree)
                r = None
                for h in range(2):
                    for tq in range(4):
                        r = te.matmul(
                            self.pav[h][:, tq * 65:(tq + 1) * 65],
                            Eb[idx % 2][:, h * 512 + tq * 128:
                                        h * 512 + tq * 128 + 128],
                            v_tm[:, kc * 130 + h * 65: kc * 130 + h * 65 + 65],
                            start=(kc == 0 and tq == 0),
                            stop=(kc == NKC - 1 and tq == 3))
                r.inc(s_prj)
                snap_av[idx] = s_prj.n
                if kc < NKC - 1:
                    continue
                abuf = att_g[qb % 2]
                ve.wait(s_prj, snap_av[qb * NKC + NKC - 1])
                if p * 4 + qb >= 2:
                    ve.wait(s_ast[qb % 2], self._astd.get((p * 4 + qb) - 2, 0))
                for h in range(2):
                    zap = AP(tensor=self.pav[h], offset=64,
                             ap=[[260, 128], [65, 4]])
                    ve.wait(s_pve)
                    ve.tensor_copy(out=self.zt[:, h * 4:(h + 1) * 4],
                                   in_=zap).inc(s_pve)
                ve.wait(s_pve)
                ve.reciprocal(out=self.zr[:], in_=self.zt[:]).inc(s_pve)
                ve.wait(s_pve)
                for h in range(2):
                    for tq in range(4):
                        ve.tensor_scalar(
                            out=abuf[:, tq * 128 + h * 64: tq * 128 + h * 64 + 64],
                            in0=self.pav[h][:, tq * 65: tq * 65 + 64],
                            scalar1=self.zr[:, h * 4 + tq: h * 4 + tq + 1],
                            scalar2=None, op0=ALU.mult).inc(s_pve)
                snap_pavfree = s_pve.n
                sy.wait(s_pve)
                for tq in range(4):
                    row0 = qb * 512 + tq * 128
                    sy.dma_start(
                        out=self.dscr["attn0_spill"][row0:row0 + 128,
                                                     2 * p * 64: 2 * p * 64 + 128],
                        in_=abuf[:, tq * 128: tq * 128 + 128])\
                        .inc(s_ast[qb % 2], 16)
                if not hasattr(self, "_astd"):
                    self._astd = {}
                self._astd[p * 4 + qb] = s_ast[qb % 2].n
        self.s_p2 = (s_prj, s_pse, s_pve, s_ast)
        self.s_ast_fin = (s_ast[0].n, s_ast[1].n)

    def p3(self):
        kb = self.kb
        ve, sc, te, gp, sy = (kb.vector, kb.scalar, kb.tensor, kb.gpsimd,
                              kb.sync)
        s_prj, s_pse, s_pve, s_ast = self.s_p2

        s_gm1 = kb.sem("s_gm1")
        gp.wait(self.s_cc1)
        gp.dma_start(out=self.gmax1[:], in_=self.dscr["red1_out"]).inc(s_gm1, 16)

        s3v = kb.sem("s_p3v")
        s3s = kb.sem("s_p3s")
        s3p = kb.sem("s_p3p")
        s3ld = kb.sem("s_p3ld")
        ve.wait(s_gm1)
        ve.wait(s_pve)
        te.wait(s_gm1)
        te.wait(s_prj)
        te.wait(kb.sem("s_wve"))
        te.wait(kb.sem("s_15v"))
        for i in range(3):
            r = te.matmul(self.pu0[:, i:i + 1], self.ones_f[:],
                          self.gmax1[0:1, i:i + 1], start=True, stop=True)
            r.inc(s3p)
            ve.wait(s3p)
            ve.tensor_copy(out=self.scl[:, i:i + 1], in_=self.pu0[:, i:i + 1])\
                .inc(s3v)
        for (srcc, dstc) in ((0, 3), (1, 4), (2, 5)):
            ve.wait(s3v)
            ve.reciprocal(out=self.scl[:, 8:9], in_=self.scl[:, srcc:srcc + 1])\
                .inc(s3v)
            ve.wait(s3v)
            ve.tensor_scalar(out=self.scl[:, dstc:dstc + 1],
                             in0=self.scl[:, 8:9], scalar1=127.0, scalar2=None,
                             op0=ALU.mult).inc(s3v)
        for (a, b, dstc) in ((0, 1, 6), (0, 2, 7)):
            ve.wait(s3v)
            ve.tensor_tensor(out=self.scl[:, 8:9], in0=self.scl[:, a:a + 1],
                             in1=self.scl[:, b:b + 1], op=ALU.mult).inc(s3v)
            ve.wait(s3v)
            ve.tensor_scalar(out=self.scl[:, dstc:dstc + 1],
                             in0=self.scl[:, 8:9], scalar1=1.0 / 16129.0,
                             scalar2=None, op0=ALU.mult).inc(s3v)
        ve.wait(s3v)
        ve.memset(self.scl[:, 15:16], -MAGIC).inc(s3v)
        self.b1s = kb.sbuf(self.longs, "b1s", [128, 32], F32, side="left")
        ve.wait(s3v)
        ve.tensor_scalar(out=self.b1s[:], in0=self.b1_cols[:],
                         scalar1=self.scl[:, 6:7], scalar2=None, op0=ALU.mult)\
            .inc(s3v)

        OXQ = 32768
        xq = self.tb(OXQ, 16384)
        self.OXQ = OXQ
        P3t = contextlib.ExitStack()
        x3 = [kb.sbuf(P3t, f"x3_{i}", [128, D], F32) for i in range(2)]
        xn3 = kb.sbuf(P3t, "xn3", [128, D], F32)
        xq3 = kb.sbuf(P3t, "xq3", [128, D], BF16)
        s_x3 = [kb.sem("s_x3_0"), kb.sem("s_x3_1")]
        ve.wait(s_pse)
        ve.wait(kb.sem("s_ast0"))
        ve.wait(kb.sem("s_ast1"))
        te.wait(kb.sem("s_ast0"))
        te.wait(kb.sem("s_ast1"))
        te.wait(s_pse)
        te.wait(s_pve)
        sc.wait(s_pse)
        snap_q3 = {}
        for t in range(NT):
            buf = t % 2
            if t >= 2:
                sy.wait(kb.sem("s_p3s"), snap_q3[t - 2][0])
            sy.dma_start(out=x3[buf][:],
                         in_=self.din["x_loc"][t * 128:(t + 1) * 128, :])\
                .inc(s_x3[buf], 16)
            sc.wait(s_x3[buf])
            sc.wait(s3v)
            sc.activation(out=xn3[:], in_=x3[buf][:], func=AF.Identity,
                          scale=self.stats[:, 16 + t:17 + t],
                          bias=self.stats[:, 32 + t:33 + t]).inc(s3s)
            snap_q3[t] = (s3s.n, s3v.n)
            ve.wait(s3s)
            ve.wait(s3v)
            ve.tensor_scalar(out=xn3[:], in0=xn3[:],
                             scalar1=self.scl[:, 3:4], scalar2=MAGIC,
                             op0=ALU.mult, op1=ALU.add).inc(s3v)
            snap_s1x = s3v.n
            sc.wait(s3v, snap_s1x)
            if t >= 1:
                sc.wait(s3p, snap_q3t)
            sc.wait(s3s)
            sc.activation(out=xq3[:], in_=xn3[:], func=AF.Identity,
                          bias=self.scl[:, 15:16]).inc(s3s)
            snap_s2x = s3s.n
            te.wait(s3s, snap_s2x)
            if t >= 1:
                te.wait(s3v, snap_q3e)
            for d in range(DC):
                te.transpose(self.ptb[:, d * 128:(d + 1) * 128],
                             xq3[:, d * 128:(d + 1) * 128], self.IdB[:])\
                    .inc(s3p)
            snap_q3t = s3p.n
            ve.wait(s3p)
            for d in range(DC):
                ve.tensor_copy(
                    out=xq[:, d * 2048 + t * 128: d * 2048 + t * 128 + 128],
                    in_=self.ptb[:, d * 128:(d + 1) * 128]).inc(s3v)
            snap_q3e = s3v.n
        P3t.close()
        if self.upto == "P3q":
            return

        w1b = [self.tf(i * 4096, 4096) for i in range(2)]
        tq = self.tf(8192, 1024)
        w1q = self.tb(9216, 4096)
        w1qT = self.tb(11264, 4096)
        hblk = self.tf(13312, 8192)
        s_w1ld = [kb.sem("s_w1ld0"), kb.sem("s_w1ld1")]
        s_hsp = kb.sem("s_hsp")
        snap_w1free = {}
        snap_hblk = {}

        sy.wait(s_pse)
        sy.wait(s_pve)
        sy.wait(s_prj)
        sy.wait(s3v)
        sy.wait(s3p)
        hmaxcol = 56

        s_w1p = [s_w1ld[0], s_w1ld[1], kb.sem("s_x2_0"), kb.sem("s_x2_1")]
        s_hsp_m = [s_hsp, kb.sem("s_wld"), kb.sem("s_x3_0"), kb.sem("s_x3_1")]
        self.s_hsp_m = s_hsp_m
        snap_w1p = {}
        snap_s1last = {}

        def load_w1(hb):
            buf = hb % 2
            if hb >= 1:
                sy.wait(s3v, snap_s1last[hb - 1])
            for i in range(4):
                sy.dma_start(
                    out=w1b[buf][:, i * 1024:(i + 1) * 1024],
                    in_=self.din["w1"][hb * 512 + i * 128:
                                       hb * 512 + i * 128 + 128, :])\
                    .inc(s_w1p[i], 16)
                snap_w1p[(hb, i)] = s_w1p[i].n

        tq2 = [self.tf(21504 + j * 1024, 1024) for j in range(2)]
        w1q2 = [w1q, self.tb(23552, 4096)]
        snap_step1 = {}
        snap_step2 = {}
        snap_tg = {}
        snap_evg_sc = {}
        snap_evg_ve = {}
        snap_mm_done = {}
        snap_ev = {}
        snap_absm = {}
        snap_spill = {}

        def quant_piece(hb, i):
            buf = hb % 2
            pc = hb * 4 + i
            ve.wait(s_w1p[i], snap_w1p[(hb, i)])
            if pc >= 2:
                ve.wait(s3s, snap_step2[pc - 2])
            ve.wait(s3v)
            ve.tensor_scalar(out=tq2[pc % 2][:],
                             in0=w1b[buf][:, i * 1024:(i + 1) * 1024],
                             scalar1=self.scl[:, 4:5], scalar2=MAGIC,
                             op0=ALU.mult, op1=ALU.add).inc(s3v)
            snap_step1[pc] = s3v.n
            if i == 3:
                snap_s1last[hb] = s3v.n
            sc.wait(s3v, snap_step1[pc])
            if hb >= 2 and i == 0:
                sc.wait(s3p, snap_tg[(hb - 2, 3)])
            sc.wait(s3s)
            sc.activation(out=w1q2[buf][:, i * 1024:(i + 1) * 1024],
                          in_=tq2[pc % 2][:], func=AF.Identity,
                          bias=self.scl[:, 15:16]).inc(s3s)
            snap_step2[pc] = s3s.n

        def tg(hb, i):
            buf = hb % 2
            te.wait(s3s, snap_step2[hb * 4 + i])
            prev = (hb, i - 1) if i > 0 else (hb - 1, 3)
            if prev in snap_evg_sc:
                te.wait(s3s, snap_evg_sc[prev])
            te.wait(s3p)
            for d in range(DC):
                te.transpose(self.ptb[:, d * 128:(d + 1) * 128],
                             w1q2[buf][:, i * 1024 + d * 128:
                                       i * 1024 + d * 128 + 128],
                             self.IdB[:]).inc(s3p)
            snap_tg[(hb, i)] = s3p.n
            sc.wait(s3p, snap_tg[(hb, i)])
            if hb >= 1 and i == 0:
                sc.wait(s3p, snap_mm_done[hb - 1])
            sc.wait(s3s)
            sc.activation(out=w1qT[:, i * 1024:(i + 1) * 1024],
                          in_=self.ptb[:], func=AF.Identity).inc(s3s)
            snap_evg_sc[(hb, i)] = s3s.n

        load_w1(0)
        for i in range(4):
            quant_piece(0, i)
        NHB = 0 if self.upto == "P3z" else (
            1 if self.upto in ("P3m", "P3t") else 8)
        for hb in range(NHB):
            if hb + 1 < NHB:
                load_w1(hb + 1)
            tg(hb, 0)
            tg(hb, 1)
            if self.upto == "P3t":
                tg(hb, 2)
                tg(hb, 3)
                return
            for m in range(4):
                te.wait(s3s, snap_evg_sc[(hb, m)])
                for tb2 in range(4):
                    idx = m * 4 + tb2
                    ps = self.pS[idx % 2]
                    if hb * 16 + idx >= 2:
                        te.wait(s3s, snap_ev.get(hb * 16 + idx - 2, 0))
                    lastm = None
                    for d in range(DC):
                        lastm = te.matmul(
                            ps[:, 0:512],
                            w1qT[:, m * 1024 + d * 128: m * 1024 + d * 128 + 128],
                            xq[:, d * 2048 + tb2 * 512:
                               d * 2048 + tb2 * 512 + 512],
                            start=(d == 0), stop=(d == DC - 1))
                    lastm.inc(s3p)
                    sc.wait(s3p)
                    sc.wait(s3s)
                    if hb >= 1 and tb2 == 0:
                        sc.wait(s_hsp_m[m], snap_spill[(hb - 1, m)])
                        sc.wait(s3v, snap_absm[(hb - 1, m)])
                    bcol = hb * 4 + m
                    sc.activation(out=hblk[:, m * 2048 + tb2 * 512:
                                           m * 2048 + tb2 * 512 + 512],
                                  in_=ps[:, 0:512], func=AF.Gelu,
                                  scale=self.scl[:, 6:7],
                                  bias=self.b1s[:, bcol:bcol + 1]).inc(s3s)
                    snap_ev[hb * 16 + idx] = s3s.n
                    if tb2 == 0 and m + 2 < 4:
                        tg(hb, m + 2)
                if hb + 1 < NHB:
                    quant_piece(hb + 1, m)
                ve.wait(s3s, snap_ev[hb * 16 + m * 4 + 3])
                ve.wait(s3v)
                ve.reduce_max(out=self.stats[:, 57:58],
                              in_=hblk[:, m * 2048:(m + 1) * 2048], axis=AX.X,
                              apply_absolute_value=True).inc(s3v)
                ve.wait(s3v)
                if hb == 0 and m == 0:
                    ve.tensor_copy(out=self.stats[:, hmaxcol:hmaxcol + 1],
                                   in_=self.stats[:, 57:58]).inc(s3v)
                else:
                    ve.tensor_tensor(out=self.stats[:, hmaxcol:hmaxcol + 1],
                                     in0=self.stats[:, hmaxcol:hmaxcol + 1],
                                     in1=self.stats[:, 57:58], op=ALU.max)\
                        .inc(s3v)
                snap_absm[(hb, m)] = s3v.n
                sy.wait(s3s, snap_ev[hb * 16 + m * 4 + 3])
                sy.dma_start(
                    out=self.dscr["h_spill"][hb * 512 + m * 128:
                                             hb * 512 + m * 128 + 128, :],
                    in_=hblk[:, m * 2048:(m + 1) * 2048]).inc(s_hsp_m[m], 16)
                snap_spill[(hb, m)] = s_hsp_m[m].n
            snap_mm_done[hb] = s3p.n

        if self.upto in ("P3m", "P3z"):
            return
        te.wait(s3v)
        te.wait(s3p)
        te.transpose(self.pu0[0:1, 0:128],
                     self.stats[:, hmaxcol:hmaxcol + 1], self.IdF[:]).inc(s3p)
        ve.wait(s3p)
        ve.reduce_max(out=self.stats[0:1, 58:59], in_=self.pu0[0:1, 0:128],
                      axis=AX.X).inc(s3v)
        ve.wait(s3v)
        ve.memset(self.sclr[:], 0.0).inc(s3v)
        ve.wait(s3v)
        ve.tensor_copy(out=self.sclr[0:1, 0:1], in_=self.stats[0:1, 58:59])\
            .inc(s3v)
        s_cc2 = kb.sem("s_cc2")
        s_gp2 = kb.sem("s_gp2")
        gp.wait(s3v)
        gp.dma_start(out=self.dscr["red2_in"], in_=self.sclr[:]).inc(s_gp2, 16)
        gp.wait(s_gp2)
        if self.no_cc:
            gp.dma_start(out=self.dscr["red2_out"], in_=self.dscr["red2_in"])\
                .inc(s_cc2, 16)
        else:
            gp.collective_compute(
                "AllReduce", ALU.max, replica_groups=[list(range(NCORES))],
                ins=[self.dscr["red2_in"].opt()],
                outs=[self.dscr["red2_out"].opt()],
            ).inc(s_cc2, 1)
        gp.wait(s_cc2)
        gp.dma_start(out=self.gmax2[:], in_=self.dscr["red2_out"]).inc(s_gp2, 16)
        self.s_p3 = (s3p, s3s, s3v, s_hsp, s_gp2)

    def p4(self):
        kb = self.kb
        ve, sc, te, gp, sy = (kb.vector, kb.scalar, kb.tensor, kb.gpsimd,
                              kb.sync)
        s3p, s3s, s3v, s_hsp, s_gp2 = self.s_p3
        s4v = kb.sem("s_p4v")
        s4s = kb.sem("s_p4s")
        s4p = kb.sem("s_p4p")

        ve.wait(s_gp2)
        ve.wait(s3v)
        te.wait(s_gp2)
        te.wait(s3p)
        te.wait(s3v)
        r = te.matmul(self.pu0[:, 3:4], self.ones_f[:],
                      self.gmax2[0:1, 0:1], start=True, stop=True).inc(s4p)
        ve.wait(s4p)
        ve.tensor_copy(out=self.scl[:, 9:10], in_=self.pu0[:, 3:4]).inc(s4v)
        ve.wait(s4v)
        ve.reciprocal(out=self.scl[:, 8:9], in_=self.scl[:, 9:10]).inc(s4v)
        ve.wait(s4v)
        ve.tensor_scalar(out=self.scl[:, 10:11], in0=self.scl[:, 8:9],
                         scalar1=127.0, scalar2=None, op0=ALU.mult).inc(s4v)
        ve.wait(s4v)
        ve.tensor_tensor(out=self.scl[:, 8:9], in0=self.scl[:, 9:10],
                         in1=self.scl[:, 2:3], op=ALU.mult).inc(s4v)
        ve.wait(s4v)
        ve.tensor_scalar(out=self.scl[:, 11:12], in0=self.scl[:, 8:9],
                         scalar1=1.0 / 16129.0, scalar2=None, op0=ALU.mult)\
            .inc(s4v)
        B2S = self.tf(35840, 1024)
        te.wait(s4v)
        for nb in range(2):
            r = te.matmul(self.pS[0][:, nb * 512:(nb + 1) * 512],
                          self.ones_col[:],
                          self.b2_rb[0:1, nb * 512:(nb + 1) * 512],
                          start=True, stop=True).inc(s4p)
        ve.wait(s4p)
        for nb in range(2):
            ve.tensor_scalar(out=B2S[:, nb * 512:(nb + 1) * 512],
                             in0=self.pS[0][:, nb * 512:(nb + 1) * 512],
                             scalar1=self.scl[:, 11:12], scalar2=None,
                             op0=ALU.mult).inc(s4v)

        x1 = self.tf(0, 16384)
        hq = self.tb(16384, 16384)
        w2qT = self.tb(24576, 8192)
        P4t = contextlib.ExitStack()
        w2cb = [self.tf(28672 + i * 1024, 1024) for i in range(2)]
        hcb = [self.tf(30720 + i * 2048, 2048) for i in range(2)]
        w2qn = [self.tb(34816 + i * 512, 1024) for i in range(2)]
        s_w2ld = [kb.sem("s_w2ld0"), kb.sem("s_w2ld1")]
        s_hld = [kb.sem("s_hld0"), kb.sem("s_hld1")]

        def load_hc(hcx):
            buf = hcx % 2
            if hcx >= 2:
                gp.wait(s4s, self._w2free[hcx - 2])
                sy.wait(s4s, self._hfree[hcx - 2])
            for i in range(DC):
                gp.dma_start(
                    out=w2cb[buf][:, i * 128:(i + 1) * 128],
                    in_=self.din["w2"][i * 128:(i + 1) * 128,
                                       hcx * 128: hcx * 128 + 128])\
                    .inc(s_w2ld[buf], 16)
            sy.dma_start(out=hcb[buf][:],
                         in_=self.dscr["h_spill"][hcx * 128:(hcx + 1) * 128, :])\
                .inc(s_hld[buf], 16)

        self._w2free = {}
        self._hfree = {}
        snap_x1ev = {}
        snap_mmg = {}
        ve.wait(s3s)
        ve.wait(s3p)
        sc.wait(s3p)
        sc.wait(s3v)
        te.wait(s3s)
        gp.wait(s3v)
        gp.wait(s3s)
        sy.wait(s3v)
        for s_ in self.s_hsp_m:
            ve.wait(s_)
            sc.wait(s_)
            sy.wait(s_)
        load_hc(0)
        snap_s1h = {}
        snap_s2h = {}
        snap_s1w = {}
        snap_s2w = {}
        snap_tgw = {}
        snap_evw_sc = {}
        snap_evw_ve = {}
        for hg in range(4):
            for c8 in range(8):
                hcx = hg * 8 + c8
                buf = hcx % 2
                if hcx + 1 < 32:
                    load_hc(hcx + 1)
                ve.wait(s_hld[buf])
                ve.wait(s4v)
                ve.tensor_scalar(out=hcb[buf][:], in0=hcb[buf][:],
                                 scalar1=self.scl[:, 10:11], scalar2=MAGIC,
                                 op0=ALU.mult, op1=ALU.add).inc(s4v)
                snap_s1h[hcx] = s4v.n
                sc.wait(s4v, snap_s1h[hcx])
                if hg >= 1 and c8 == 0:
                    sc.wait(s4p, snap_mmg[hg - 1])
                sc.wait(s4s)
                sc.activation(out=hq[:, c8 * 2048:(c8 + 1) * 2048],
                              in_=hcb[buf][:], func=AF.Identity,
                              bias=self.scl[:, 15:16]).inc(s4s)
                snap_s2h[hcx] = s4s.n
                self._hfree[hcx] = s4s.n
                ve.wait(s_w2ld[buf])
                ve.wait(s4v)
                ve.tensor_scalar(out=w2cb[buf][:], in0=w2cb[buf][:],
                                 scalar1=self.scl[:, 5:6], scalar2=MAGIC,
                                 op0=ALU.mult, op1=ALU.add).inc(s4v)
                snap_s1w[hcx] = s4v.n
                sc.wait(s4v, snap_s1w[hcx])
                if hcx >= 2:
                    sc.wait(s4p, snap_tgw[hcx - 2])
                sc.wait(s4s)
                sc.activation(out=w2qn[buf][:], in_=w2cb[buf][:],
                              func=AF.Identity, bias=self.scl[:, 15:16]).inc(s4s)
                snap_s2w[hcx] = s4s.n
                self._w2free[hcx] = s4s.n
                te.wait(s4s, snap_s2w[hcx])
                if hcx >= 1:
                    te.wait(s4s, snap_evw_sc[hcx - 1])
                te.wait(s4p)
                for i in range(DC):
                    te.transpose(self.ptb[:, i * 128:(i + 1) * 128],
                                 w2qn[buf][:, i * 128:(i + 1) * 128],
                                 self.IdB[:]).inc(s4p)
                snap_tgw[hcx] = s4p.n
                sc.wait(s4p, snap_tgw[hcx])
                sc.wait(s4s)
                sc.activation(out=w2qT[:, c8 * 1024:(c8 + 1) * 1024],
                              in_=self.ptb[:], func=AF.Identity).inc(s4s)
                snap_evw_sc[hcx] = s4s.n
            te.wait(s4s, snap_evw_sc[hg * 8 + 7])
            for tt in range(16):
                for nb in range(2):
                    idx = tt * 2 + nb
                    ps = self.pS[idx % 2]
                    gidx = hg * 32 + idx
                    if gidx >= 2:
                        te.wait(s4v, snap_x1ev.get(gidx - 2, 0))
                    lastm = None
                    for c8 in range(8):
                        lastm = te.matmul(
                            ps[:, 0:512],
                            hq[:, c8 * 2048 + tt * 128: c8 * 2048 + tt * 128 + 128],
                            w2qT[:, c8 * 1024 + nb * 512: c8 * 1024 + nb * 512 + 512],
                            start=(c8 == 0), stop=(c8 == 7))
                    lastm.inc(s4p)
                    ve.wait(s4p)
                    dst = x1[:, tt * 1024 + nb * 512: tt * 1024 + nb * 512 + 512]
                    if hg == 0:
                        ve.tensor_copy(out=dst, in_=ps[:, 0:512]).inc(s4v)
                    else:
                        ve.scalar_tensor_tensor(out=dst, in0=ps[:, 0:512],
                                                scalar=1.0, in1=dst,
                                                op0=ALU.mult, op1=ALU.add)\
                            .inc(s4v)
                    snap_x1ev[gidx] = s4v.n
            snap_mmg[hg] = s4p.n
        P4t.close()

        P4f = contextlib.ExitStack()
        xf = [kb.sbuf(P4f, f"xf{i}", [128, 1024], F32) for i in range(2)]
        af = [kb.sbuf(P4f, f"af{i}", [128, 1024], BF16) for i in range(2)]
        s_xf = [kb.sem("s_xf0"), kb.sem("s_xf1")]
        s_af = [kb.sem("s_af0"), kb.sem("s_af1")]
        snap_fin = {}
        for tt in range(NT):
            buf = tt % 2
            if tt >= 2:
                sy.wait(s4v, snap_fin[tt - 2])
            sy.dma_start(out=xf[buf][:],
                         in_=self.din["x_loc"][tt * 128:(tt + 1) * 128, :])\
                .inc(s_xf[buf], 16)
            sy.dma_start(out=af[buf][:],
                         in_=self.dscr["attn0_spill"][tt * 128:(tt + 1) * 128, :])\
                .inc(s_af[buf], 16)
            dst = x1[:, tt * 1024:(tt + 1) * 1024]
            ve.wait(s4v)
            ve.tensor_scalar(out=dst, in0=dst, scalar1=self.scl[:, 11:12],
                             scalar2=None, op0=ALU.mult).inc(s4v)
            ve.wait(s4v)
            ve.tensor_tensor(out=dst, in0=dst, in1=B2S[:], op=ALU.add).inc(s4v)
            ve.wait(s_af[buf])
            ve.wait(s4v)
            ve.tensor_tensor(out=dst, in0=dst, in1=af[buf][:], op=ALU.add)\
                .inc(s4v)
            ve.wait(s_xf[buf])
            ve.wait(s4v)
            ve.tensor_tensor(out=dst, in0=dst, in1=xf[buf][:], op=ALU.add)\
                .inc(s4v)
            snap_fin[tt] = s4v.n
        P4f.close()
        self.x1 = x1
        self.s_p4 = (s4p, s4s, s4v)
        self.s_p4_dma = (s_xf[0], s_xf[1], s_af[0], s_af[1])

    def p5(self):
        kb = self.kb
        ve, sc, te, gp, sy = (kb.vector, kb.scalar, kb.tensor, kb.gpsimd,
                              kb.sync)
        s4p, s4s, s4v = self.s_p4
        s5v = kb.sem("s_p5v")
        s5s = kb.sem("s_p5s")
        s5p = kb.sem("s_p5p")
        s_x1sp = kb.sem("s_x1sp")
        s_x1h = [kb.sem("s_wld"), kb.sem("s_x3_0")]
        self.s_x1h = s_x1h
        s_xn1sp = kb.sem("s_xn1sp")

        x1 = self.x1
        xn1_tm = self.tb(16384, 16384)
        xn1_fm = self.tb(24576, 16384)
        normed = self.tb(32768, 16384)
        self.xn1_fm_off = 24576
        self.normed_off = 32768

        P5t = contextlib.ExitStack()
        nt5 = kb.sbuf(P5t, "nt5", [128, D], BF16)
        sq5 = kb.sbuf(P5t, "sq5", [128, D], F32)

        sc.wait(s4v)
        sc.wait(s4p)
        ve.wait(s4v)
        ve.wait(s4p)
        te.wait(s4v)
        sy.wait(s4v)

        def ln_stats(src_ap_f32orbf, cb, is_f32):
            sc.wait(s5v)
            sc.wait(s5s)
            sc.activation(out=sq5[:], in_=src_ap_f32orbf, func=AF.Square,
                          accum_out=self.stats[:, cb + 1:cb + 2]).inc(s5s)
            ve.wait(s5v)
            ve.wait(s5s)
            ve.reduce_sum(out=self.stats[:, cb + 0:cb + 1], in_=src_ap_f32orbf,
                          axis=AX.X).inc(s5v)
            ve.wait(s5v)
            ve.tensor_scalar(out=self.stats[:, cb + 2:cb + 3],
                             in0=self.stats[:, cb + 0:cb + 1],
                             scalar1=1.0 / D, scalar2=None, op0=ALU.mult)\
                .inc(s5v)
            ve.wait(s5v)
            ve.tensor_tensor(out=self.stats[:, cb + 3:cb + 4],
                             in0=self.stats[:, cb + 2:cb + 3],
                             in1=self.stats[:, cb + 2:cb + 3], op=ALU.mult)\
                .inc(s5v)
            ve.wait(s5v)
            ve.tensor_scalar(out=self.stats[:, cb + 4:cb + 5],
                             in0=self.stats[:, cb + 3:cb + 4],
                             scalar1=-1.0, scalar2=LN_EPS, op0=ALU.mult,
                             op1=ALU.add).inc(s5v)
            sc.wait(s5s)
            sc.wait(s5v)
            sc.activation(out=self.stats[:, cb + 5:cb + 6],
                          in_=self.stats[:, cb + 1:cb + 2], func=AF.Ln,
                          scale=1.0 / D, bias=self.stats[:, cb + 4:cb + 5])\
                .inc(s5s)
            sc.wait(s5s)
            sc.activation(out=self.stats[:, cb + 6:cb + 7],
                          in_=self.stats[:, cb + 5:cb + 6], func=AF.Exp,
                          scale=-0.5).inc(s5s)
            ve.wait(s5s)
            ve.scalar_tensor_tensor(out=self.stats[:, cb + 7:cb + 8],
                                    in0=self.stats[:, cb + 2:cb + 3],
                                    scalar=-1.0,
                                    in1=self.stats[:, cb + 6:cb + 7],
                                    op0=ALU.mult, op1=ALU.mult).inc(s5v)

        snap_tr5 = 0
        for t in range(NT):
            cb = (t % 2) * 8
            cb2 = 16 + (t % 2) * 8
            x1t = x1[:, t * 1024:(t + 1) * 1024]
            xn1t = xn1_tm[:, t * 1024:(t + 1) * 1024]
            ln_stats(x1t, cb, True)
            sc.wait(s5v)
            sc.activation(out=xn1t, in_=x1t, func=AF.Identity,
                          scale=self.stats[:, cb + 6:cb + 7],
                          bias=self.stats[:, cb + 7:cb + 8]).inc(s5s)
            sy.wait(s5s)
            sy.dma_start(out=self.dscr["x1_spill"][t * 128:(t + 1) * 128, :],
                         in_=x1t).inc(s_x1h[t // 8], 16)
            sy.dma_start(out=self.dscr["xn1_spill"][t * 128:(t + 1) * 128, :],
                         in_=xn1t).inc(s_x1sp, 16)
            if not hasattr(self, "snap_x1rd"):
                self.snap_x1rd = {}
            self.snap_x1rd[t] = (s5s.n, s5v.n)
            ln_stats(xn1t, cb2, False)
            sc.wait(s5v)
            sc.wait(s5s)
            if t >= 1:
                sc.wait(s5p, snap_nt5)
            sc.activation(out=nt5[:], in_=xn1t, func=AF.Identity,
                          scale=self.stats[:, cb2 + 6:cb2 + 7],
                          bias=self.stats[:, cb2 + 7:cb2 + 8]).inc(s5s)
            te.wait(s5s)
            if snap_tr5:
                te.wait(s5v, snap_tr5)
            for d in range(DC):
                te.transpose(self.ptb[:, d * 128:(d + 1) * 128],
                             xn1t[:, d * 128:(d + 1) * 128], self.IdB[:])\
                    .inc(s5p)
            ve.wait(s5p)
            for d in range(DC):
                ve.tensor_copy(
                    out=xn1_fm[:, d * TOK + t * 128: d * TOK + t * 128 + 128],
                    in_=self.ptb[:, d * 128:(d + 1) * 128]).inc(s5v)
            te.wait(s5v)
            for d in range(DC):
                te.transpose(self.ptb[:, d * 128:(d + 1) * 128],
                             nt5[:, d * 128:(d + 1) * 128], self.IdB[:])\
                    .inc(s5p)
            snap_nt5 = s5p.n
            ve.wait(s5p)
            for d in range(DC):
                ve.tensor_scalar(
                    out=normed[:, d * TOK + t * 128: d * TOK + t * 128 + 128],
                    in0=self.ptb[:, d * 128:(d + 1) * 128],
                    scalar1=self.lnw_cols[:, d:d + 1],
                    scalar2=self.lnb_cols[:, d:d + 1],
                    op0=ALU.mult, op1=ALU.add).inc(s5v)
            snap_tr5 = s5v.n

        sy.wait(s5v)
        sy.dma_start(out=self.dscr["xn1fm_spill"], in_=xn1_fm).inc(s_xn1sp, 16)

        wg_sb = kb.sbuf(self.longs, "wg_sb", [128, 64], F32, side="left")
        s_wg = kb.sem("s_wg")
        for d in range(DC):
            sy.dma_start(out=wg_sb[:, d * 8:(d + 1) * 8],
                         in_=self.din["Wg"][d * 128:(d + 1) * 128, :])\
                .inc(s_wg, 16)
        wgb_sb = kb.sbuf(self.longs, "wgb_sb", [128, 64], BF16, side="left")
        ve.wait(s_wg)
        ve.wait(s5v)
        ve.tensor_copy(out=wgb_sb[:], in_=wg_sb[:]).inc(s5v)
        glog = kb.sbuf(self.longs, "glog", [128, 128], F32, side="left")
        gext = kb.sbuf(self.longs, "gext", [128, 128], F32, side="left")
        zg = kb.sbuf(self.longs, "zg", [128, 48], F32, side="left")
        self.we_mat = kb.sbuf(self.longs, "we_mat", [128, 128], F32,
                              side="left")
        self.we_fm = kb.sbuf(self.longs, "we_fm", [8, 2048], BF16, side="left")
        te.wait(s5v)
        for t in range(NT):
            pvx = self.pav[t % 2]
            if t >= 2:
                te.wait(s5v, snap_gev.get(t - 2, 0))
            lastm = None
            for d in range(DC):
                lastm = te.matmul(pvx[:, 0:8],
                                  xn1_fm[:, d * TOK + t * 128:
                                         d * TOK + t * 128 + 128],
                                  wgb_sb[:, d * 8:(d + 1) * 8],
                                  start=(d == 0), stop=False)
            lastm = te.matmul(pvx[:, 0:8], self.ones_col[:],
                              self.bg_rb[:], start=False, stop=True)
            lastm.inc(s5p)
            ve.wait(s5p)
            ve.tensor_copy(out=glog[:, t * 8:(t + 1) * 8], in_=pvx[:, 0:8])\
                .inc(s5v)
            if t == 0:
                snap_gev = {}
            snap_gev[t] = s5v.n
        sc.wait(s5v)
        sc.activation(out=gext[:], in_=glog[:], func=AF.Exp).inc(s5s)
        ve.wait(s5s)
        g3 = gext.rearrange("p (t e) -> p t e", e=8)
        ve.reduce_sum(out=zg[:, 0:16], in_=g3, axis=AX.X).inc(s5v)
        ve.wait(s5v)
        ve.reduce_max(out=zg[:, 16:32], in_=g3, axis=AX.X).inc(s5v)
        ve.wait(s5v)
        ve.reciprocal(out=zg[:, 32:48], in_=zg[:, 0:16]).inc(s5v)
        gt2 = kb.sbuf(P5t, "gt2", [128, 16], F32)
        for t in range(NT):
            sl = slice(t * 8, (t + 1) * 8)
            ve.wait(s5v)
            ve.tensor_single_scalar(out=gt2[:, 0:8], in_=gext[:, sl],
                                    scalar=zg[:, 16 + t:17 + t],
                                    op=ALU.is_equal).inc(s5v)
            ve.wait(s5v)
            ve.scalar_tensor_tensor(out=gt2[:, 0:8], in0=gt2[:, 0:8],
                                    scalar=-1.0e30, in1=gext[:, sl],
                                    op0=ALU.mult, op1=ALU.add).inc(s5v)
            ve.wait(s5v)
            ve.reduce_max(out=gt2[:, 8:9], in_=gt2[:, 0:8], axis=AX.X)\
                .inc(s5v)
            ve.wait(s5v)
            ve.tensor_single_scalar(out=gt2[:, 0:8], in_=gext[:, sl],
                                    scalar=gt2[:, 8:9], op=ALU.is_ge)\
                .inc(s5v)
            ve.wait(s5v)
            ve.tensor_tensor(out=gt2[:, 0:8], in0=gt2[:, 0:8],
                             in1=gext[:, sl], op=ALU.mult).inc(s5v)
            ve.wait(s5v)
            ve.tensor_scalar(out=self.we_mat[:, sl], in0=gt2[:, 0:8],
                             scalar1=zg[:, 32 + t:33 + t], scalar2=None,
                             op0=ALU.mult).inc(s5v)
        web = kb.sbuf(P5t, "web", [128, 128], BF16)
        ve.wait(s5v)
        ve.tensor_copy(out=web[:], in_=self.we_mat[:]).inc(s5v)
        te.wait(s5v)
        for t in range(NT):
            pvx = self.pav[t % 2]
            if t >= 2:
                te.wait(s5v, snap_wev.get(t - 2, 0))
            te.transpose(pvx.bitcast(BF16)[0:8, 0:128],
                         web[:, t * 8:(t + 1) * 8], self.IdB[:]).inc(s5p)
            ve.wait(s5p)
            ve.tensor_copy(out=self.we_fm[:, t * 128:(t + 1) * 128],
                           in_=pvx.bitcast(BF16)[0:8, 0:128]).inc(s5v)
            if t == 0:
                snap_wev = {}
            snap_wev[t] = s5v.n
        self.snap_p5p_end = s5p.n
        P5t.close()
        self.s_p5 = (s5p, s5s, s5v, s_x1sp, s_xn1sp)

    def p6(self):
        kb = self.kb
        ve, sc, te, gp, sy = (kb.vector, kb.scalar, kb.tensor, kb.gpsimd,
                              kb.sync)
        s5p, s5s, s5v, s_x1sp, s_xn1sp = self.s_p5
        s6v = kb.sem("s_p6v")
        s6s = kb.sem("s_p6s")
        s6p = kb.sem("s_p6p")
        s_whld = kb.sem("s_whld")
        s_vsp = kb.sem("s_vsp")

        Whv = self.tb(0, 16384)
        Whg = self.tb(8192, 16384)
        v_g = self.tb(16384, 4096)
        qq_fm = self.tb(18432, 2048)
        lq_fm = self.tb(19456, 2048)
        qk_fm = self.tb(20480, 2048)
        link = self.tb(21504, 2048)
        linkvT = self.tf(22528, 2048)
        qk_tm = self.tb(40960, 256)
        qk4 = self.tb(41088, 256)

        self.L1 = dict(Whv=Whv, Whg=Whg, v_g=v_g, qq=qq_fm, lq=lq_fm,
                       qk=qk_fm, link=link, linkvT=linkvT)

        gp.wait(s5s, self.snap_x1rd[7][0])
        gp.wait(s5v, self.snap_x1rd[7][1])
        gp.wait(self.s_x1h[0])
        for s_ in self.s_p4_dma:
            gp.wait(s_)
        wqk_b = kb.sbuf(self.longs, "wqk_b", [128, 1024], BF16, side="left")
        s_wqk = kb.sem("s_wqk")
        for d in range(DC):
            gp.dma_start(out=wqk_b[:, d * 128:(d + 1) * 128],
                         in_=self.din["Wqk"][d * 128:(d + 1) * 128, :])\
                .inc(s_wqk, 16)
        for d in range(DC):
            gp.dma_start(out=Whv[:, d * 2048:(d + 1) * 2048],
                         in_=self.din["Wh"][d * 128:(d + 1) * 128, 0:2048])\
                .inc(s_whld, 16)
        gp.wait(s5s, self.snap_x1rd[15][0])
        gp.wait(s5v, self.snap_x1rd[15][1])
        gp.wait(self.s_x1h[1])
        for d in range(DC):
            gp.dma_start(out=Whg[:, d * 2048:(d + 1) * 2048],
                         in_=self.din["Wh"][d * 128:(d + 1) * 128, 2048:4096])\
                .inc(s_whld, 16)
        s_wo = kb.sem("s_wo")
        self.s_wo = s_wo
        Wout_b = self.tb(24576, 16384)
        self.Wout_b = Wout_b
        gp.wait(s5p, self.snap_p5p_end)
        gp.wait(s_xn1sp)
        for m in range(16):
            gp.dma_start(out=Wout_b[:, m * 1024:(m + 1) * 1024],
                         in_=self.din["Wout"][m * 128:(m + 1) * 128, :])\
                .inc(s_wo, 16)

        def nslice(d, tok0, w):
            return self.tb(self.normed_off, 16384)[:, d * TOK + tok0:
                                                   d * TOK + tok0 + w]

        te.wait(s5v)
        te.wait(s5s)
        te.wait(s_wqk)
        te.wait(s_x1sp)
        ve.wait(s_x1sp)
        sc.wait(s_x1sp)
        sy.wait(s_x1sp)
        if self.upto == "P6a":
            return
        snap_qkev = 0
        snap_vev = {}
        snap_lkv = {}
        ngroups = 1 if self.upto == "P6b" else NG
        for g in range(ngroups):
            t0 = g * G
            for half in range(2):
                pvx = self.pav[half]
                if g > 0 or half > 0:
                    te.wait(s6v, snap_qkev)
                lastm = None
                for d in range(DC):
                    lastm = te.matmul(pvx[:, 0:128],
                                      nslice(d, t0 + half * 128, 128),
                                      wqk_b[:, d * 128:(d + 1) * 128],
                                      start=(d == 0), stop=False)
                lastm = te.matmul(pvx[:, 0:128], self.ones_col[:],
                                  self.bqk_rb[:], start=False, stop=True)
                lastm.inc(s6p)
                sc.wait(s6p)
                if g > 0 or half > 0:
                    sc.wait(s6v, snap_qkev)
                sc.activation(out=qk_tm[:, half * 128:(half + 1) * 128],
                              in_=pvx[:, 0:128], func=AF.Silu).inc(s6s)
            for vx, dst, is_fm in ((0, qq_fm, True), (1, lq_fm, True),
                                   (2, qk_fm, True), (3, link, False)):
                for half in range(2):
                    ve.wait(s6s)
                    ve.wait(s6v)
                    ve.tensor_tensor(out=qk4[:, half * 128:(half + 1) * 128],
                                     in0=qk_tm[:, half * 128:(half + 1) * 128],
                                     in1=self.qkwb_t[:, vx * 128:(vx + 1) * 128],
                                     op=ALU.mult).inc(s6v)
                    ve.wait(s6v)
                    ve.tensor_tensor(out=qk4[:, half * 128:(half + 1) * 128],
                                     in0=qk4[:, half * 128:(half + 1) * 128],
                                     in1=self.qkwb_t[:, (4 + vx) * 128:
                                                     (5 + vx) * 128],
                                     op=ALU.add).inc(s6v)
                if not is_fm:
                    ve.wait(s6v)
                    ve.tensor_copy(out=link[:, g * 256:(g + 1) * 256],
                                   in_=qk4[:]).inc(s6v)
                else:
                    te.wait(s6v)
                    for half in range(2):
                        te.transpose(
                            self.ptb[:, half * 128:half * 128 + 128],
                            qk4[:, half * 128:(half + 1) * 128],
                            self.IdB[:]).inc(s6p)
                    ve.wait(s6p)
                    for half in range(2):
                        ve.tensor_copy(
                            out=dst[:, g * 256 + half * 128:
                                    g * 256 + half * 128 + 128],
                            in_=self.ptb[:, half * 128:half * 128 + 128])\
                            .inc(s6v)
            snap_qkev = s6v.n
            te.wait(s_whld)
            te.wait(s6v)
            for half in range(2):
                for nb in range(4):
                    vidx = half * 4 + nb
                    ps = self.pS[vidx % 2]
                    if vidx >= 2:
                        te.wait(s6s, snap_vev[vidx - 2])
                    lastm = None
                    for d in range(DC):
                        lastm = te.matmul(
                            ps[:, 0:512],
                            nslice(d, t0 + half * 128, 128),
                            Whv[:, d * 2048 + nb * 512: d * 2048 + nb * 512 + 512],
                            start=(d == 0), stop=False)
                    lastm = te.matmul(ps[:, 0:512], self.ones_col[:],
                                      self.bh_rb[0:1, nb * 512:(nb + 1) * 512],
                                      start=False, stop=True)
                    lastm.inc(s6p)
                    sc.wait(s6p)
                    sc.activation(out=v_g[:, half * 2048 + nb * 512:
                                          half * 2048 + nb * 512 + 512],
                                  in_=ps[:, 0:512], func=AF.Silu).inc(s6s)
                    snap_vev[vidx] = s6s.n
            sy.wait(s6s)
            for half in range(2):
                sy.dma_start(
                    out=self.dscr["v_spill"][t0 + half * 128:
                                             t0 + half * 128 + 128, :],
                    in_=v_g[:, half * 2048:(half + 1) * 2048]).inc(s_vsp, 16)
            snap_vspill = s_vsp.n
            te.wait(s6v)
            te.wait(s6s)
            for nb in range(4):
                lidx = g * 4 + nb
                ps = self.pS[nb % 2]
                if nb >= 2:
                    te.wait(s6v, snap_lkv[lidx - 2])
                lastm = None
                for half in range(2):
                    lastm = te.matmul(
                        ps[:, 0:512],
                        link[:, g * 256 + half * 128: g * 256 + half * 128 + 128],
                        v_g[:, half * 2048 + nb * 512:
                            half * 2048 + nb * 512 + 512],
                        start=(half == 0), stop=(half == 1))
                lastm.inc(s6p)
                ve.wait(s6p)
                dstv = linkvT[:, nb * 512:(nb + 1) * 512]
                if g == 0:
                    ve.tensor_scalar(out=dstv, in0=ps[:, 0:512],
                                     scalar1=1.0 / G, scalar2=None,
                                     op0=ALU.mult).inc(s6v)
                else:
                    ve.scalar_tensor_tensor(out=dstv, in0=ps[:, 0:512],
                                            scalar=1.0 / G, in1=dstv,
                                            op0=ALU.mult, op1=ALU.add)\
                        .inc(s6v)
                snap_lkv[lidx] = s6v.n
            sc.wait(s_vsp, snap_vspill)
            sc.wait(s6v)

        if self.upto in ("P6b",):
            return
        s_cc3 = kb.sem("s_cc3")
        s_gp3 = kb.sem("s_gp3")
        gp.wait(s6v)
        gp.dma_start(out=self.dscr["pair_in"], in_=linkvT).inc(s_gp3, 16)
        gp.wait(s_gp3)
        if self.no_cc:
            gp.dma_start(out=self.dscr["pair_out"][0:128, :],
                         in_=self.dscr["pair_in"]).inc(s_cc3, 16)
            gp.dma_start(out=self.dscr["pair_out"][128:256, :],
                         in_=self.dscr["pair_in"]).inc(s_cc3, 16)
        else:
            gp.collective_compute(
                "AllGather", ALU.bypass,
                replica_groups=[[2 * i, 2 * i + 1] for i in range(4)],
                ins=[self.dscr["pair_in"].opt()],
                outs=[self.dscr["pair_out"].opt()],
            ).inc(s_cc3, 1)
        gp.wait(s_cc3)
        gp.dma_start(out=linkvT, in_=self.dscr["pair_out"][0:128, :])\
            .inc(s_gp3, 16)
        te.wait(kb.sem("s_pmld"))
        te.wait(s6p)
        r = te.matmul(self.pu0[:, 12:13], self.ones_f[:],
                      self.pmask_sb[0:1, 0:1], start=True, stop=True).inc(s6p)
        ve.wait(s6p)
        ve.wait(s6v)
        ve.tensor_copy(out=self.scl[:, 12:13], in_=self.pu0[:, 12:13])\
            .inc(s6v)
        ve.wait(s_gp3)
        ve.wait(s6v)
        ve.tensor_scalar(out=linkvT, in0=linkvT,
                         scalar1=self.scl[:, 12:13], scalar2=None,
                         op0=ALU.mult).inc(s6v)
        self.s_p6 = (s6p, s6s, s6v, s_vsp)

    def p7(self):
        kb = self.kb
        ve, sc, te, gp, sy = (kb.vector, kb.scalar, kb.tensor, kb.gpsimd,
                              kb.sync)
        s6p, s6s, s6v, s_vsp = self.s_p6
        s7v = kb.sem("s_p7v")
        s7s = kb.sem("s_p7s")
        s7p = kb.sem("s_p7p")
        s_wo = kb.sem("s_wo")
        s_vrl = kb.sem("s_vrl")
        s_xnr = kb.sem("s_xnr")
        s_a1 = kb.sem("s_a1")

        L = self.L1
        v_g = L["v_g"]
        prefix = L["linkvT"]
        Wout_b = self.Wout_b
        s_wo = self.s_wo
        gate = self.tb(0, 4096)
        houtT = self.tb(2048, 4096)
        a1g = self.tb(4096, 2048)
        attnT = self.tb(5120, 512)
        prefixb = self.tb(5376, 2048)
        xn1g = self.tb(6400, 2048)
        sim_s = self.tf(40960, 512)

        sxsp = self.s_p5[4]
        for eng in (sc, ve, te, sy, gp):
            eng.wait(sxsp)
            eng.wait(s6p)
            eng.wait(s6s)
            eng.wait(s6v)

        def nslice(d, tok0, w):
            return self.tb(self.normed_off, 16384)[:, d * TOK + tok0:
                                                   d * TOK + tok0 + w]

        snap = {}
        for g in range(NG):
            t0 = g * G
            sy.wait(s6s)
            if g == 0:
                sy.wait(s_vsp)
                sy.wait(s6p)
                sy.wait(self.s_p5[4])
            if g >= 1:
                sy.wait(s7p, snap[("vfree", g - 1)])
                sy.wait(s7v, snap[("xnfree", g - 1)])
            for half in range(2):
                sy.dma_start(out=v_g[:, half * 2048:(half + 1) * 2048],
                             in_=self.dscr["v_spill"][t0 + half * 128:
                                                      t0 + half * 128 + 128, :])\
                    .inc(s_vrl, 16)
                sy.dma_start(out=xn1g[:, half * 1024:(half + 1) * 1024],
                             in_=self.dscr["xn1_spill"][t0 + half * 128:
                                                        t0 + half * 128 + 128, :])\
                    .inc(s_xnr, 16)
            ve.wait(s6v)
            ve.wait(s7v)
            if g >= 1:
                ve.wait(s7p, snap[("pfb_free", g - 1)])
            ve.tensor_copy(out=prefixb[:], in_=prefix[:]).inc(s7v)
            te.wait(s_wo)
            te.wait(s7v)
            te.wait(s6v)
            for m in range(16):
                pvx = self.pS[m % 2]
                if g * 16 + m >= 2:
                    te.wait(s7s, snap.get(("gev", g * 16 + m - 2), 0))
                lastm = None
                for d in range(DC):
                    lastm = te.matmul(
                        pvx[:, 0:256],
                        L["Whg"][:, d * 2048 + m * 128: d * 2048 + m * 128 + 128],
                        nslice(d, t0, 256),
                        start=(d == 0), stop=(d == DC - 1))
                lastm.inc(s7p)
                sc.wait(s7p)
                sc.activation(out=gate[:, m * 256:(m + 1) * 256],
                              in_=pvx[:, 0:256], func=AF.Silu,
                              bias=self.bh_cols[:, 16 + m:17 + m]).inc(s7s)
                snap[("gev", g * 16 + m)] = s7s.n
            te.wait(s7s)
            if g >= 1:
                te.wait(s7v, snap[("qo_done", g - 1)])
            for half in range(2):
                pvx = self.pav[half]
                r = te.matmul(pvx[:, 0:256],
                              L["qk"][:, g * 256 + half * 128:
                                      g * 256 + half * 128 + 128],
                              L["qq"][:, g * 256:(g + 1) * 256],
                              start=True, stop=True).inc(s7p)
            ve.wait(s7p)
            for half in range(2):
                ve.scalar_tensor_tensor(
                    out=sim_s[:, half * 256:(half + 1) * 256],
                    in0=self.pav[half][:, 0:256], scalar=1.0 / G,
                    in1=self.BiasT[:, half * 256:(half + 1) * 256],
                    op0=ALU.mult, op1=ALU.add).inc(s7v)
            sc.wait(s7v)
            sc.activation(out=sim_s[:], in_=sim_s[:], func=AF.Relu).inc(s7s)
            ve.wait(s7s)
            ve.tensor_tensor(out=attnT[:], in0=sim_s[:], in1=sim_s[:],
                             op=ALU.mult).inc(s7v)
            te.wait(s7v)
            ve.wait(s_vrl)
            te.wait(s_vrl)
            for m in range(16):
                pvx = self.pav[m % 2]
                if m >= 2:
                    te.wait(s7v, snap[("qev", m - 2)])
                lastm = te.matmul(pvx[:, 0:256],
                                  v_g[:, 0 * 2048 + m * 128: 0 * 2048 + m * 128 + 128],
                                  attnT[:, 0:256], start=True, stop=False)
                lastm = te.matmul(pvx[:, 0:256],
                                  v_g[:, 1 * 2048 + m * 128: 1 * 2048 + m * 128 + 128],
                                  attnT[:, 256:512], start=False, stop=False)
                lastm = te.matmul(pvx[:, 0:256],
                                  prefixb[:, m * 128:(m + 1) * 128],
                                  L["lq"][:, g * 256:(g + 1) * 256],
                                  start=False, stop=True)
                lastm.inc(s7p)
                ve.wait(s7p)
                ve.wait(s7s)
                ve.tensor_tensor(out=houtT[:, m * 256:(m + 1) * 256],
                                 in0=pvx[:, 0:256],
                                 in1=gate[:, m * 256:(m + 1) * 256],
                                 op=ALU.mult).inc(s7v)
                snap[("qev", m)] = s7v.n
            snap[("qo_done", g)] = s7v.n
            te.wait(s7v)
            for half in range(2):
                for nb in range(2):
                    widx = half * 2 + nb
                    ps = self.pS[widx % 2]
                    if widx >= 2:
                        te.wait(s7v, snap[("wev", widx - 2)])
                    lastm = None
                    for m in range(16):
                        lastm = te.matmul(
                            ps[:, 0:512],
                            houtT[:, m * 256 + half * 128: m * 256 + half * 128 + 128],
                            Wout_b[:, m * 1024 + nb * 512: m * 1024 + nb * 512 + 512],
                            start=(m == 0), stop=False)
                    lastm = te.matmul(ps[:, 0:512], self.ones_col[:],
                                      self.bout_rb[0:1, nb * 512:(nb + 1) * 512],
                                      start=False, stop=True)
                    lastm.inc(s7p)
                    ve.wait(s7p)
                    ve.wait(s_xnr)
                    ve.tensor_tensor(
                        out=a1g[:, half * 1024 + nb * 512:
                                half * 1024 + nb * 512 + 512],
                        in0=ps[:, 0:512],
                        in1=xn1g[:, half * 1024 + nb * 512:
                                 half * 1024 + nb * 512 + 512],
                        op=ALU.add).inc(s7v)
                    snap[("wev", widx)] = s7v.n
            snap[("xnfree", g)] = s7v.n
            sy.wait(s7v)
            for half in range(2):
                sy.dma_start(
                    out=self.dscr["attn1_spill"][t0 + half * 128:
                                                 t0 + half * 128 + 128, :],
                    in_=a1g[:, half * 1024:(half + 1) * 1024]).inc(s_a1, 16)
            if True:
                snap[("a1", g)] = s_a1.n
            ve.wait(s_a1, snap[("a1", g)]) if False else None
            te.wait(s7v)
            for nb in range(4):
                ps = self.pS[nb % 2]
                if nb >= 2:
                    te.wait(s7v, snap[("lkv2", nb - 2)])
                lastm = None
                for half in range(2):
                    lastm = te.matmul(
                        ps[:, 0:512],
                        L["link"][:, g * 256 + half * 128:
                                  g * 256 + half * 128 + 128],
                        v_g[:, half * 2048 + nb * 512:
                            half * 2048 + nb * 512 + 512],
                        start=(half == 0), stop=(half == 1))
                lastm.inc(s7p)
                ve.wait(s7p)
                ve.scalar_tensor_tensor(
                    out=prefix[:, nb * 512:(nb + 1) * 512],
                    in0=ps[:, 0:512], scalar=1.0 / G,
                    in1=prefix[:, nb * 512:(nb + 1) * 512],
                    op0=ALU.mult, op1=ALU.add).inc(s7v)
                snap[("lkv2", nb)] = s7v.n
            snap[("vfree", g)] = s7p.n
            snap[("pfb_free", g)] = s7p.n
            ve.wait(s_a1, snap[("a1", g)])
        self.s_p7 = (s7p, s7s, s7v, s_a1)

    def p8(self):
        kb = self.kb
        ve, sc, te, gp, sy = (kb.vector, kb.scalar, kb.tensor, kb.gpsimd,
                              kb.sync)
        s7p, s7s, s7v, s_a1 = self.s_p7
        s8v = kb.sem("s_p8v")
        s8s = kb.sem("s_p8s")
        s8p = kb.sem("s_p8p")
        s_xnl = kb.sem("s_xnl")
        s_wm = [kb.sem("s_wm0"), kb.sem("s_wm1")]

        mixed = self.tf(0, 16384)
        xn1f = self.tb(16384, 16384)
        W1b = [self.tb(24576 + i * 2048, 4096) for i in range(2)]
        W3b = [self.tb(28672 + i * 2048, 4096) for i in range(2)]
        W2b = [self.tb(32768 + i * 2048, 4096) for i in range(2)]
        h1b = self.tb(36864, 4096)
        h3b = self.tb(38912, 4096)
        self.mixed = mixed

        for eng in (gp, sy, ve, sc, te):
            eng.wait(s7p)
            eng.wait(s7v)
            eng.wait(s7s)
        gp.wait(s_a1)
        gp.dma_start(out=xn1f[:], in_=self.dscr["xn1fm_spill"]).inc(s_xnl, 16)

        def load_ehb(e, hb, buf):
            for d in range(DC):
                gp.dma_start(
                    out=W1b[buf][:, d * 512:(d + 1) * 512],
                    in_=self.din["W1m"][e, d * 128:(d + 1) * 128,
                                        hb * 512:(hb + 1) * 512])\
                    .inc(s_wm[buf], 16)
                gp.dma_start(
                    out=W3b[buf][:, d * 512:(d + 1) * 512],
                    in_=self.din["W3m"][e, d * 128:(d + 1) * 128,
                                        hb * 512:(hb + 1) * 512])\
                    .inc(s_wm[buf], 16)
            for hc in range(4):
                gp.dma_start(
                    out=W2b[buf][:, hc * 1024:(hc + 1) * 1024],
                    in_=self.din["W2m"][e, hb * 512 + hc * 128:
                                        hb * 512 + hc * 128 + 128, :])\
                    .inc(s_wm[buf], 16)

        load_ehb(0, 0, 0)
        te.wait(s_xnl)
        snap = {}
        first = True
        step = 0
        for e in range(NE):
            for hb in range(4):
                buf = step % 2
                if step + 1 < 32:
                    ne, nhb = (e, hb + 1) if hb < 3 else (e + 1, 0)
                    if step >= 1:
                        gp.wait(s8p, snap[("wfree", step - 1)])
                    load_ehb(ne, nhb, (step + 1) % 2)
                te.wait(s_wm[buf])
                for th in range(2):
                    for which in ("h1", "h3"):
                        hsrc = W1b[buf] if which == "h1" else W3b[buf]
                        for m in range(4):
                            for nb2 in range(2):
                                hidx = m * 2 + nb2
                                ps = self.pS[hidx % 2]
                                if hidx >= 2:
                                    if which == "h1":
                                        te.wait(s8s, snap.get(("h1f", hidx - 2), 0))
                                    else:
                                        te.wait(s8v, snap.get(("h3f", hidx - 2), 0))
                                elif which == "h1":
                                    te.wait(s8v,
                                            snap.get(("mev", 14 + hidx), 0))
                                else:
                                    te.wait(s8s,
                                            snap.get(("h1f", 6 + hidx), 0))
                                lastm = None
                                for d in range(DC):
                                    lastm = te.matmul(
                                        ps[:, 0:512],
                                        hsrc[:, d * 512 + m * 128:
                                             d * 512 + m * 128 + 128],
                                        xn1f[:, d * 2048 + th * 1024 + nb2 * 512:
                                             d * 2048 + th * 1024 + nb2 * 512 + 512],
                                        start=(d == 0), stop=(d == DC - 1))
                                lastm.inc(s8p)
                                bcol = e * 16 + hb * 4 + m
                                if which == "h1":
                                    sc.wait(s8p)
                                    sc.activation(
                                        out=h1b[:, m * 1024 + nb2 * 512:
                                                m * 1024 + nb2 * 512 + 512],
                                        in_=ps[:, 0:512], func=AF.Identity,
                                        bias=self.b1m_cols[:, bcol:bcol + 1])\
                                        .inc(s8s)
                                    snap[("h1f", hidx)] = s8s.n
                                else:
                                    ve.wait(s8p)
                                    ve.tensor_scalar(
                                        out=h3b[:, m * 1024 + nb2 * 512:
                                                m * 1024 + nb2 * 512 + 512],
                                        in0=ps[:, 0:512],
                                        scalar1=self.b3m_cols[:, bcol:bcol + 1],
                                        scalar2=None, op0=ALU.add).inc(s8v)
                                    snap[("h3f", hidx)] = s8v.n
                                    if nb2 == 1:
                                        ve.wait(s8v)
                                        ve.wait(s8s, snap[("h1f", hidx)]
                                                if ("h1f", hidx) in snap else 0)
                                        ve.tensor_tensor(
                                            out=h1b[:, m * 1024:(m + 1) * 1024],
                                            in0=h1b[:, m * 1024:(m + 1) * 1024],
                                            in1=h3b[:, m * 1024:(m + 1) * 1024],
                                            op=ALU.mult).inc(s8v)
                                        sc.wait(s8v)
                                        sc.activation(
                                            out=h1b[:, m * 1024:(m + 1) * 1024],
                                            in_=h1b[:, m * 1024:(m + 1) * 1024],
                                            func=AF.Silu).inc(s8s)
                                        snap[("silu", m)] = s8s.n
                    for tloc in range(8):
                        tt = th * 8 + tloc
                        for nb in range(2):
                            widx = tloc * 2 + nb
                            ps = self.pS[widx % 2]
                            if widx >= 2:
                                te.wait(s8v, snap.get(("mev", widx - 2), 0))
                            else:
                                te.wait(s8v, snap.get(("h3f", 6 + widx), 0))
                            lastm = None
                            for m in range(4):
                                if widx == 0:
                                    te.wait(s8s, snap[("silu", m)])
                                lastm = te.matmul(
                                    ps[:, 0:512],
                                    h1b[:, m * 1024 + tloc * 128:
                                        m * 1024 + tloc * 128 + 128],
                                    W2b[buf][:, m * 1024 + nb * 512:
                                             m * 1024 + nb * 512 + 512],
                                    start=(m == 0), stop=(m == 3))
                            lastm.inc(s8p)
                            ve.wait(s8p)
                            dst = mixed[:, tt * 1024 + nb * 512:
                                        tt * 1024 + nb * 512 + 512]
                            wecol = self.we_mat[:, tt * 8 + e:tt * 8 + e + 1]
                            if first:
                                ve.tensor_scalar(out=dst, in0=ps[:, 0:512],
                                                 scalar1=wecol, scalar2=None,
                                                 op0=ALU.mult).inc(s8v)
                            else:
                                ve.scalar_tensor_tensor(
                                    out=dst, in0=ps[:, 0:512], scalar=wecol,
                                    in1=dst, op0=ALU.mult, op1=ALU.add)\
                                    .inc(s8v)
                            snap[("mev", widx)] = s8v.n
                    first = False
                snap[("wfree", step)] = s8p.n
                step += 1
        self.s_p8 = (s8p, s8s, s8v)

    def p9(self):
        kb = self.kb
        ve, sc, te, gp, sy = (kb.vector, kb.scalar, kb.tensor, kb.gpsimd,
                              kb.sync)
        s8p, s8s, s8v = self.s_p8
        s9v = kb.sem("s_p9v")
        s9s = kb.sem("s_p9s")
        s9p = kb.sem("s_p9p")
        s_wf = kb.sem("s_wf")
        s_x1r = [kb.sem("s_x1r0"), kb.sem("s_x1r1")]
        s_a1r = [kb.sem("s_a1r0"), kb.sem("s_a1r1")]
        s_out = [kb.sem("s_out0"), kb.sem("s_out1")]

        mixedb = self.tb(16384, 16384)
        mixed_fm = self.tb(24576, 16384)
        Wfb = self.tb(32768, 8192)
        P9t = contextlib.ExitStack()
        x1r = [self.tf(36864 + i * 1024, 1024) for i in range(2)]
        otile = [self.tf(38912 + i * 1024, 1024) for i in range(2)]
        a1r = [self.tb(16384 + i * 512, 1024) for i in range(2)]

        for eng in (gp, sy, ve, sc, te):
            eng.wait(s8p)
            eng.wait(s8v)
            eng.wait(s8s)
        for d in range(DC):
            gp.dma_start(out=Wfb[:, d * 1024:(d + 1) * 1024],
                         in_=self.din["Wf"][d * 128:(d + 1) * 128, :])\
                .inc(s_wf, 16)
        ve.wait(s8v)
        ve.tensor_copy(out=mixedb[:], in_=self.mixed[:]).inc(s9v)
        snap = {}
        for t in range(NT):
            te.wait(s9v)
            if t >= 1:
                te.wait(s9v, snap.get(("ev", t - 1), 0))
            for d in range(DC):
                te.transpose(self.ptb[:, d * 128:(d + 1) * 128],
                             mixedb[:, t * 1024 + d * 128:
                                    t * 1024 + d * 128 + 128],
                             self.IdB[:]).inc(s9p)
            ve.wait(s9p)
            for d in range(DC):
                ve.tensor_copy(
                    out=mixed_fm[:, d * 2048 + t * 128: d * 2048 + t * 128 + 128],
                    in_=self.ptb[:, d * 128:(d + 1) * 128]).inc(s9v)
            snap[("ev", t)] = s9v.n
        te.wait(s_wf)
        te.wait(s9v)
        sy.wait(s8v)
        sy.wait(s9p)
        sy.wait(s9v)
        for t in range(NT):
            buf = t % 2
            if t >= 2:
                sy.wait(s9v, snap[("ot", t - 2)])
            sy.dma_start(out=x1r[buf][:],
                         in_=self.dscr["x1_spill"][t * 128:(t + 1) * 128, :])\
                .inc(s_x1r[buf], 16)
            sy.dma_start(out=a1r[buf][:],
                         in_=self.dscr["attn1_spill"][t * 128:(t + 1) * 128, :])\
                .inc(s_a1r[buf], 16)
            if t >= 2:
                ve.wait(s_out[t % 2], snap[("od", t - 2)])
            for nb in range(2):
                pidx = t * 2 + nb
                ps = self.pS[pidx % 2]
                if pidx >= 2:
                    te.wait(s9v, snap.get(("fev", pidx - 2), 0))
                lastm = None
                for d in range(DC):
                    lastm = te.matmul(
                        ps[:, 0:512],
                        mixed_fm[:, d * 2048 + t * 128: d * 2048 + t * 128 + 128],
                        Wfb[:, d * 1024 + nb * 512: d * 1024 + nb * 512 + 512],
                        start=(d == 0), stop=False)
                lastm = te.matmul(
                    ps[:, 0:512], self.we_fm[:, t * 128:(t + 1) * 128],
                    self.b2m_b[:, nb * 512:(nb + 1) * 512],
                    start=False, stop=False)
                lastm = te.matmul(
                    ps[:, 0:512], self.ones_col[:],
                    self.bf_rb[0:1, nb * 512:(nb + 1) * 512],
                    start=False, stop=True)
                lastm.inc(s9p)
                ve.wait(s9p)
                ve.wait(s_x1r[buf])
                ve.tensor_tensor(out=otile[buf][:, nb * 512:(nb + 1) * 512],
                                 in0=ps[:, 0:512],
                                 in1=x1r[buf][:, nb * 512:(nb + 1) * 512],
                                 op=ALU.add).inc(s9v)
                ve.wait(s_a1r[buf])
                ve.wait(s9v)
                ve.tensor_tensor(out=otile[buf][:, nb * 512:(nb + 1) * 512],
                                 in0=otile[buf][:, nb * 512:(nb + 1) * 512],
                                 in1=a1r[buf][:, nb * 512:(nb + 1) * 512],
                                 op=ALU.add).inc(s9v)
                snap[("fev", pidx)] = s9v.n
            sy.wait(s9v)
            sy.dma_start(out=self.out[t * 128:(t + 1) * 128, :],
                         in_=otile[buf][:]).inc(s_out[t % 2], 16)
            snap[("ot", t)] = s9v.n
            snap[("od", t)] = s_out[t % 2].n
        P9t.close()

    def finish(self):
        for nm, sem in list(self.kb._sems.items()):
            if sem.n > 0:
                self.kb.gpsimd.wait(sem)
        self.kb.replay()
        return self.nc, self.es

    def dump(self, aps, waits=()):
        kb = self.kb
        s_fin = kb.sem("s_fin")
        for s in waits:
            kb.gpsimd.wait(kb.sem(s) if isinstance(s, str) else s)
        flat = self.out.rearrange("a b -> (a b)")
        off = 0
        for ap in aps:
            p, L = ap.shape
            kb.gpsimd.dma_start(
                out=flat[off:off + p * L].rearrange("(p l) -> p l", p=p),
                in_=ap).inc(s_fin, 16)
            off += p * L
        kb.gpsimd.wait(s_fin)

    def finish2(self):
        self.kb.replay()
        return self.nc, self.es


def build(upto=None, no_cc=False):
    pr = Prog(upto, no_cc=no_cc)
    pr.p0()
    pr.p1()
    kb = pr.kb
    if upto in ("P1", "P1s"):
        if upto == "P1":
            aps = [pr.ar_b16[:, d * N: d * N + 2048] for d in range(DC)]
        else:
            aps = [pr.stats[:, 48:51]]
        pr.dump(aps, waits=("s_p1ve", "s_p1sc"))
        return pr.finish()
    pr.p2()
    if upto in (None, "P3", "P3q", "P3m", "P3z", "P3t", "P4", "P5", "P5b", "P6a", "P6b", "P6d",
                "P6e", "P7"):
        pr.p3()
        if upto == "P3q":
            pr.dump([pr.tb(32768, 16384)], waits=("s_p3v", "s_p3s", "s_p3p"))
            return pr.finish()
        if upto == "P3z":
            pr.dump([pr.tb(9216, 4096)], waits=("s_p3v", "s_p3s"))
            return pr.finish()
        if upto == "P3t":
            pr.dump([pr.tb(11264, 4096)], waits=("s_p3v", "s_p3s"))
            return pr.finish()
        if upto == "P3m":
            s_fin = kb.sem("s_fin")
            for s_ in pr.s_hsp_m:
                kb.gpsimd.wait(s_)
            kb.gpsimd.dma_start(
                out=pr.out.rearrange("a b -> (a b)")[0:512 * 2048]
                .rearrange("(p l) -> p l", p=512),
                in_=pr.dscr["h_spill"][0:512, :]).inc(s_fin, 16)
            kb.gpsimd.wait(s_fin)
            return pr.finish()
        if upto == "P3":
            s_fin = kb.sem("s_fin")
            for s_ in pr.s_hsp_m:
                kb.gpsimd.wait(s_)
            kb.gpsimd.dma_start(
                out=pr.out.rearrange("a b -> (a b)")
                .rearrange("(p l) -> p l", p=1024),
                in_=pr.dscr["h_spill"][0:1024, :]).inc(s_fin, 16)
            kb.gpsimd.wait(s_fin)
            return pr.finish()
        pr.p4()
        if upto == "P4":
            pr.dump([pr.tf(0, 16384)], waits=("s_p4v",))
            return pr.finish()
        pr.p5()
        if upto == "P5":
            pr.dump([pr.tb(32768, 16384)], waits=("s_p5v", "s_p5s"))
            return pr.finish()
        if upto == "P5b":
            aps = [pr.we_mat[:], pr.tb(16384, 16256)]
            pr.dump(aps, waits=("s_p5v", "s_p5s"))
            return pr.finish()
        pr.p6()
        if upto in ("P6a", "P6b"):
            aps = [pr.tb(32768, 16384)]
            pr.dump(aps, waits=("s_p6v", "s_p6s", "s_p6p", "s_whld", "s_wqk"))
            return pr.finish()
        if upto == "P6e":
            aps = [pr.qkwb_t[:], pr.tb(40960, 256), pr.tb(32768, 14336)]
            pr.dump(aps, waits=("s_p6v", "s_p6s", "s_p6p"))
            return pr.finish()
        if upto == "P6d":
            L = pr.L1
            aps = [L["linkvT"], L["qq"], L["lq"], L["qk"], L["link"]]
            pr.dump(aps, waits=("s_p6v", "s_p6s", "s_p6p"))
            s_fin2 = kb.sem("s_fin2")
            kb.gpsimd.wait(kb.sem("s_vsp"))
            kb.gpsimd.dma_start(
                out=pr.out.rearrange("a b -> (a b)")
                [2048 * 512:2048 * 512 + 512 * 2048]
                .rearrange("(p l) -> p l", p=512),
                in_=pr.dscr["v_spill"][0:512, :]).inc(s_fin2, 16)
            kb.gpsimd.wait(s_fin2)
            return pr.finish()
        pr.p7()
        if upto is None:
            pr.p8()
            pr.p9()
            return pr.finish()
        if upto == "P7":
            s_fin = kb.sem("s_fin")
            kb.gpsimd.wait(kb.sem("s_a1"))
            kb.gpsimd.dma_start(out=pr.out, in_=pr.dscr["attn1_spill"])\
                .inc(s_fin, 16)
            kb.gpsimd.wait(s_fin)
            return pr.finish()
        raise NotImplementedError(upto)
    if upto == "P2":
        s_fin = kb.sem("s_fin")
        kb.gpsimd.wait(kb.sem("s_ast0"))
        kb.gpsimd.wait(kb.sem("s_ast1"))
        kb.gpsimd.dma_start(out=pr.out, in_=pr.dscr["attn0_spill"])\
            .inc(s_fin, 16)
        kb.gpsimd.wait(s_fin)
        return pr.finish()
    raise NotImplementedError(upto)


def _stage_inputs(inputs):
    x = np.ascontiguousarray(np.asarray(inputs["L0_attn_x"], np.float32))
    x2 = x.reshape(B, 2, TOK, D)
    com = {
        "W0": inputs["L0_attn_W"],
        "w1": inputs["L0_ffn_w1"], "b1": inputs["L0_ffn_b1"],
        "w2": inputs["L0_ffn_w2"], "b2": inputs["L0_ffn_b2"],
        "ln_w": inputs["L1_attn_ln_w"], "ln_b": inputs["L1_attn_ln_b"],
        "Wh": inputs["L1_attn_Wh"], "bh": inputs["L1_attn_bh"],
        "Wqk": inputs["L1_attn_Wqk"], "bqk": inputs["L1_attn_bqk"],
        "qk_w": inputs["L1_attn_qk_w"], "qk_b": inputs["L1_attn_qk_b"],
        "rel_table": inputs["L1_attn_rel_table"],
        "Wout": inputs["L1_attn_Wout"], "bout": inputs["L1_attn_bout"],
        "W1m": inputs["L1_ffn_W1"], "b1m": inputs["L1_ffn_b1"],
        "W3m": inputs["L1_ffn_W3"], "b3m": inputs["L1_ffn_b3"],
        "W2m": inputs["L1_ffn_W2"], "b2m": inputs["L1_ffn_b2"],
        "Wg": inputs["L1_ffn_Wg"], "bg": inputs["L1_ffn_bg"],
        "Wf": inputs["L1_ffn_Wf"], "bf": inputs["L1_ffn_bf"],
    }
    com = {k: np.ascontiguousarray(np.asarray(v, np.float32))
           for k, v in com.items()}
    maps = []
    for c in range(NCORES):
        b, h = c // 2, c % 2
        pm = np.zeros((1, 16), np.float32)
        pm[0, 0] = 1.0 if h == 1 else 0.0
        m = dict(com)
        m["x_loc"] = np.ascontiguousarray(x2[b, h])
        m["x_rem"] = np.ascontiguousarray(x2[b, 1 - h])
        m["w1shard"] = np.ascontiguousarray(com["w1"][c * 512:(c + 1) * 512])
        m["w2shard"] = np.ascontiguousarray(com["w2"][c * 128:(c + 1) * 128])
        m["pmask"] = pm
        maps.append(m)
    return maps


def _run(upto=None, inputs=None, core_ids=None):
    nc, es = build(upto, no_cc=(core_ids is not None and len(core_ids) == 1))
    with es:
        maps = _stage_inputs(inputs)
        if core_ids is None:
            core_ids = list(range(NCORES))
        res = run_bass_kernel_spmd(nc, [maps[c] for c in core_ids],
                                   core_ids=core_ids)
    return res


def kernel(**inputs):
    res = _run(upto=None, inputs=inputs)
    outs = [res.results[c]["out"] for c in range(NCORES)]
    full = np.stack(outs).reshape(B, 2, TOK, D).reshape(B, N, D)
    return np.ascontiguousarray(full.astype(np.float32))
